# version 05
# baseline (speedup 1.0000x reference)
import numpy as np

import concourse.bass as bass
import concourse.mybir as mybir
from concourse.alu_op_type import AluOpType
from concourse.bass_utils import run_bass_kernel_spmd

F32 = mybir.dt.float32
F32R = mybir.dt.float32r
I32 = mybir.dt.int32
I8 = mybir.dt.int8
BF16 = mybir.dt.bfloat16
AF = mybir.ActivationFunctionType
AX = mybir.AxisListType
OP = AluOpType

D = 1024
C = 8
H = 16
DH = 64
T4 = 4096
T2 = 2048
FF0 = 2816
FF1 = 4096
NCORES = 8

_DEBUG = {}



class Buf:
    __slots__ = ("w", "r", "tag")

    def __init__(self, tag=""):
        self.w = None
        self.r = []
        self.tag = tag


class Prog:
    ENG = {"te": "tensor", "ve": "vector", "sc": "scalar",
           "gp": "gpsimd", "sy": "sync"}

    NQ = 8

    def __init__(self, nc, block, sems):
        self.nc = nc
        self.block = block
        self.sem = sems
        self.cnt = {k: 0 for k in sems}
        self.wm = {}
        self.eng_sem = {"te": "te", "ve": "ve", "sc": "sc", "gp": "gp"}
        self.qcnt = {"dsy": 0, "dgp": 0}

    def op(self, eng, emit, reads=(), writes=(), dma=None):
        need = {}

        def add(h):
            if h is not None:
                s, v = h
                if need.get(s, 0) < v:
                    need[s] = v

        for b in reads:
            add(b.w)
        for b in writes:
            add(b.w)
            for h in b.r:
                add(h)
        if dma is not None:
            i = self.qcnt[dma]
            self.qcnt[dma] = i + 1
            semname = f"{dma}{i % self.NQ}"
            inc = 16
            if i >= self.NQ:
                add((semname, 16 * (i // self.NQ)))
        else:
            semname, inc = self.eng_sem[eng], 1
        waits = [(s, v) for s, v in need.items()
                 if self.wm.get((eng, s), 0) < v]
        semh = self.sem[semname]

        def body(proxy):
            for s, v in waits:
                proxy.wait_ge(self.sem[s], v)
            inst = emit(proxy)
            if inst is not None:
                inst.then_inc(semh, inc)

        getattr(self.block, self.ENG[eng])(body)
        for s, v in waits:
            self.wm[(eng, s)] = v
        self.cnt[semname] += inc
        h = (semname, self.cnt[semname])
        for b in reads:
            b.r.append(h)
        for b in writes:
            b.w = h
            b.r = []
        return h

    def fence(self):
        vals = [(s, v) for s, v in self.cnt.items() if v > 0]
        for eng in self.ENG:
            waits = [(s, v) for s, v in vals if self.wm.get((eng, s), 0) < v]
            if not waits:
                continue

            def body(proxy, waits=waits):
                for s, v in waits:
                    proxy.wait_ge(self.sem[s], v)

            getattr(self.block, self.ENG[eng])(body)
            for s, v in waits:
                self.wm[(eng, s)] = v

    def final_barrier(self, eng="gp"):
        vals = [(s, v) for s, v in self.cnt.items()
                if v and (s.startswith("dgp") or s.startswith("dsy"))]

        def body(proxy):
            for s, v in vals:
                proxy.wait_ge(self.sem[s], v)

        getattr(self.block, self.ENG[eng])(body)



def _rope_tables_tm(pos, scale):
    inv = 1.0 / (10000.0 ** (np.arange(0, DH, 2, dtype=np.float64) / DH))
    ang = np.asarray(pos, np.float64)[:, None] * inv[None, :]
    c = np.cos(ang) * scale
    s = np.sin(ang) * scale
    C64 = np.concatenate([c, c], 1)
    S64 = np.concatenate([-s, s], 1)
    out = []
    for t in (C64, S64):
        hi = t.astype(np.float32)
        lo = (t - hi.astype(np.float64)).astype(np.float32)
        out += [hi, lo]
    return np.concatenate(out, 1).astype(np.float32)


def _perm_rope_cols(w_fm, b):
    perm = np.concatenate([np.arange(0, DH, 2), np.arange(1, DH, 2)])
    full = np.concatenate([h * DH + perm for h in range(H)])
    return np.ascontiguousarray(w_fm[:, full]), np.ascontiguousarray(b[full])


def _bias_pp(biases):
    nb = len(biases)
    out = np.zeros((128, nb, C), np.float32)
    for i, b in enumerate(biases):
        out[:, i, :] = b.reshape(C, 128).T
    return out


def prepare_inputs(inp):
    f32 = np.float32
    x = np.ascontiguousarray(np.asarray(inp["L0_attn_x"], f32))

    common = {}
    for nm, key in [("l0_wq", "L0_attn_Wq"), ("l0_wk", "L0_attn_Wk"),
                    ("l0_wv", "L0_attn_Wv"), ("l0_wo", "L0_attn_Wo")]:
        common[nm] = np.ascontiguousarray(np.asarray(inp[key], f32))
    l0b = [np.asarray(inp[k], f32) for k in
           ("L0_attn_bq", "L0_attn_bk", "L0_attn_bv", "L0_attn_bo")]
    common["l0_bpp"] = _bias_pp(l0b)
    common["l0_brow"] = np.concatenate(l0b).reshape(1, 4 * D).astype(f32)

    common["f0_wg"] = np.ascontiguousarray(np.asarray(inp["L0_ffn_w_gate"], f32))
    common["f0_wu"] = np.ascontiguousarray(np.asarray(inp["L0_ffn_w_up"], f32))
    common["f0_wd"] = np.ascontiguousarray(np.asarray(inp["L0_ffn_w_down"], f32))

    wq1, bq1 = _perm_rope_cols(np.asarray(inp["L1_attn_Wq"], f32).T,
                               np.asarray(inp["L1_attn_bq"], f32))
    wk1, bk1 = _perm_rope_cols(np.asarray(inp["L1_attn_Wk"], f32).T,
                               np.asarray(inp["L1_attn_bk"], f32))
    common["l1_wq"] = wq1
    common["l1_wk"] = wk1
    common["l1_wv"] = np.ascontiguousarray(np.asarray(inp["L1_attn_Wv"], f32).T)
    common["l1_wo"] = np.ascontiguousarray(np.asarray(inp["L1_attn_Wo"], f32).T)
    bv1 = np.asarray(inp["L1_attn_bv"], f32)
    bo1 = np.asarray(inp["L1_attn_bo"], f32)
    common["l1_bpp"] = _bias_pp([bq1, bk1, bv1, bo1])
    common["l1_brow"] = np.concatenate([bq1, bk1, bv1, bo1]).reshape(1, 4 * D).astype(f32)

    common["f1_wg"] = np.ascontiguousarray(np.asarray(inp["L1_ffn_w_gate"], f32).T)
    common["f1_wu"] = np.ascontiguousarray(np.asarray(inp["L1_ffn_w_up"], f32).T)
    common["f1_wd"] = np.ascontiguousarray(np.asarray(inp["L1_ffn_w_down"], f32).T)

    common["idn"] = np.eye(128, dtype=f32)
    common["ones128"] = np.ones((128, 128), f32)
    common["onescol"] = np.ones((1, 128), f32)
    cp = np.zeros((128, 8), f32)
    cp[:, 0] = 1e-5
    cp[:, 1] = 1.0 / 1024
    common["cpool"] = cp
    common["signmask"] = np.full((128, 1), -2147483648, np.int32)

    in_maps = []
    for c in range(NCORES):
        b, h = c // 2, c % 2
        m = dict(common)
        xb = x[b]
        if h == 1:
            xb = np.concatenate([xb[T2:], xb[:T2]], 0)
        m["x_fm"] = np.ascontiguousarray(xb.T)
        pos = np.arange(T4)
        if h == 1:
            pos = np.concatenate([pos[T2:], pos[:T2]])
        m["ropek"] = np.ascontiguousarray(_rope_tables_tm(pos, 1.0))
        m["ropeq"] = np.ascontiguousarray(_rope_tables_tm(pos[:T2], 0.125))
        in_maps.append(m)
    return in_maps



_uid = [0]


def _nm(base):
    _uid[0] += 1
    return f"{base}_{_uid[0]}"


class Ctx:
    pass


def build_program(stage=None):
    nc = bass.Bass("TRN2", target_bir_lowering=False)
    g = {}

    def din(name, shape, dt=F32):
        g[name] = nc.dram_tensor(name, list(shape), dt, kind="ExternalInput").ap()

    def dout(name, shape, dt=F32):
        g[name] = nc.dram_tensor(name, list(shape), dt, kind="ExternalOutput").ap()

    def dscr(name, shape, dt=F32):
        g[name] = nc.dram_tensor(name, list(shape), dt).ap()

    din("x_fm", [D, T4])
    for nm in ("l0_wq", "l0_wk", "l0_wv", "l0_wo",
               "l1_wq", "l1_wk", "l1_wv", "l1_wo"):
        din(nm, [D, D])
    din("l0_bpp", [128, 4, C]); din("l1_bpp", [128, 4, C])
    din("l0_brow", [1, 4 * D]); din("l1_brow", [1, 4 * D])
    din("f0_wg", [FF0, D]); din("f0_wu", [FF0, D]); din("f0_wd", [D, FF0])
    din("f1_wg", [D, FF1]); din("f1_wu", [D, FF1]); din("f1_wd", [FF1, D])
    din("idn", [128, 128]); din("ones128", [128, 128]); din("onescol", [1, 128])
    din("cpool", [128, 8]); din("signmask", [128, 1], I32)
    din("ropek", [T4, 256]); din("ropeq", [T2, 256])

    dout("y_tm", [T2, D])
    for nm, shape in _DEBUG.get("outs", {}).items():
        dout(nm, shape)

    dscr("x1_hbm", [128, C * T4])
    dscr("mr_hbm", [2, T4])
    dscr("x2_hbm", [128, C * T4])
    dscr("x3_hbm", [128, C * T2])
    dscr("k_hbm", [H, DH, T4])
    dscr("q_hbm", [H, DH, T2])
    dscr("v_hbm", [C, T4, 128])
    dscr("wq_fm_hbm", [128, C * 2 * FF0])
    dscr("wd_fm_hbm", [128, (FF0 // 128) * D])
    dscr("wint_hbm", [D, 2 * FF1], BF16)
    dscr("wdint_hbm", [FF1, D], BF16)

    sems = {}
    ctxs = []
    names = ["te", "ve", "sc", "gp"]
    names += [f"dsy{i}" for i in range(Prog.NQ)]
    names += [f"dgp{i}" for i in range(Prog.NQ)]
    for s in names:
        cm = nc.semaphore("sem_" + s)
        sems[s] = cm.__enter__()
        ctxs.append(cm)

    blk_cm = nc.Block()
    block = blk_cm.__enter__()
    P = Prog(nc, block, sems)

    import contextlib
    with contextlib.ExitStack() as stack:
        _build_phases(nc, P, g, stack, stage)
        P.final_barrier()
    blk_cm.__exit__(None, None, None)
    for cm in reversed(ctxs):
        cm.__exit__(None, None, None)
    return nc


def _sb(nc, stack, name, shape, dt=F32):
    return stack.enter_context(nc.sbuf_tensor(name, list(shape), dt))


def _ps(nc, stack, name, shape, dt=F32):
    return stack.enter_context(nc.psum_tensor(name, list(shape), dt))


def _build_phases(nc, P, g, stack, stage):
    cx = Ctx()
    cx.nc, cx.P, cx.g, cx.stack = nc, P, g, stack
    B = cx.B = {}

    def buf(name):
        if name not in B:
            B[name] = Buf(name)
        return B[name]
    cx.buf = buf

    cx.idn = _sb(nc, stack, "c_idn", [128, 128])
    cx.ones = _sb(nc, stack, "c_ones", [128, 128])
    cx.ones_r = _sb(nc, stack, "c_ones_r", [128, 128], F32R)
    cx.onescol = _sb(nc, stack, "c_onescol", [1, 128])
    cx.cpool = _sb(nc, stack, "c_cpool", [128, 8])
    cx.signmask = _sb(nc, stack, "c_sgn", [128, 1], I32)
    cx.bpp0 = _sb(nc, stack, "c_bpp0", [128, 4, C])
    cx.bpp1 = _sb(nc, stack, "c_bpp1", [128, 4, C])
    for t, src in [(cx.idn, "idn"), (cx.ones, "ones128"), (cx.onescol, "onescol"),
                   (cx.cpool, "cpool"), (cx.signmask, "signmask"),
                   (cx.bpp0, "l0_bpp"), (cx.bpp1, "l1_bpp")]:
        P.op("sy", lambda s, t=t, src=src: s.dma_start(out=t[:], in_=g[src][:]),
             reads=[buf("hbm_" + src)], writes=[buf(t.name)], dma="dsy")
    P.op("ve", lambda v: v.tensor_copy(cx.ones_r[:], cx.ones[:]),
         reads=[buf(cx.ones.name)], writes=[buf(cx.ones_r.name)])

    cx.eps_ap = cx.cpool[:, 0:1]

    def split(src_ap, hi_t, lo_t, rbuf, hi_ap=None, lo_ap=None):
        hi_ap = hi_ap if hi_ap is not None else hi_t[:]
        lo_ap = lo_ap if lo_ap is not None else lo_t[:]
        hb, lb = buf(hi_t.name), buf(lo_t.name)
        P.op("ve", lambda v: v.tensor_copy(hi_ap, src_ap),
             reads=[rbuf], writes=[hb])
        P.op("ve", lambda v: v.tensor_tensor(lo_ap, src_ap, hi_ap.bitcast(F32),
                                             op=OP.subtract),
             reads=[rbuf, hb], writes=[lb])
    cx.split = split

    def dbg(name, src_ap, reads):
        if name in _DEBUG.get("outs", {}):
            P.op("gp", lambda gp: gp.dma_start(out=g[name][:], in_=src_ap),
                 reads=reads, writes=[buf("dbg_" + name)], dma="dgp")
    cx.dbg = dbg

    def dbg_sl(name, dst_sl, src_ap, reads):
        if name in _DEBUG.get("outs", {}):
            P.op("gp", lambda gp: gp.dma_start(out=g[name][dst_sl], in_=src_ap),
                 reads=reads, writes=[buf("dbg_" + name)], dma="dgp")
    cx.dbg_sl = dbg_sl

    _phase1(cx)
    P.fence()
    if stage == "p1":
        return
    _phase2(cx)
    P.fence()
    if stage == "p2":
        return
    _phase3(cx)
    P.fence()
    if stage == "p3":
        return
    _phase4(cx)



def _fm_chunk(t, c, N):
    return t[:, c * N:(c + 1) * N]


def _ln_fm(cx, xtile, x_hi, x_lo, sq, sq_hi, sq_lo, ps_stats, m_t, r_t, NT,
           cache=None, it=0):
    nc, P, buf, split = cx.nc, cx.P, cx.buf, cx.split
    bx = buf(xtile.name)
    split(xtile[:], x_hi, x_lo, bx)
    P.op("sc", lambda s: s.activation(sq[:], xtile[:], AF.Square),
         reads=[bx], writes=[buf(sq.name)])
    split(sq[:], sq_hi, sq_lo, buf(sq.name))
    psb = buf(ps_stats.name)

    def mm_sum(hi, lo, col):
        def e(te):
            i = None
            for cch in range(C):
                i = te.matmul(ps_stats[:, col:col + NT], cx.ones_r[:],
                              _fm_chunk(hi, cch, NT), start=(cch == 0), stop=False)
                i = te.matmul(ps_stats[:, col:col + NT], cx.ones_r[:],
                              _fm_chunk(lo, cch, NT), start=False,
                              stop=(cch == C - 1))
            return i
        return e
    P.op("te", mm_sum(x_hi, x_lo, 0),
         reads=[buf(x_hi.name), buf(x_lo.name), buf(cx.ones_r.name)], writes=[psb])
    P.op("te", mm_sum(sq_hi, sq_lo, NT),
         reads=[buf(sq_hi.name), buf(sq_lo.name), buf(cx.ones_r.name)], writes=[psb])
    bm, br = buf(m_t.name), buf(r_t.name)
    P.op("ve", lambda v: v.tensor_scalar(m_t[:], ps_stats[:, 0:NT], 1.0 / D, None,
                                         op0=OP.mult), reads=[psb], writes=[bm])
    P.op("ve", lambda v: v.tensor_tensor(r_t[:], m_t[:], m_t[:], op=OP.mult),
         reads=[bm], writes=[br])
    P.op("ve", lambda v: v.scalar_tensor_tensor(r_t[:], ps_stats[:, NT:2 * NT],
                                                1.0 / D, r_t[:],
                                                op0=OP.mult, op1=OP.subtract),
         reads=[psb, br], writes=[br])
    P.op("sc", lambda s: s.activation(r_t[:], r_t[:], AF.Sqrt, bias=cx.eps_ap),
         reads=[br, buf(cx.cpool.name)], writes=[br])
    P.op("ve", lambda v: v.reciprocal(r_t[:], r_t[:]), reads=[br], writes=[br])
    if cache is not None:
        m_cache, r_cache = cache
        P.op("ve", lambda v: v.tensor_copy(m_cache[:, it * NT:(it + 1) * NT],
                                           m_t[0:1, :]),
             reads=[bm], writes=[buf(m_cache.name)])
        P.op("ve", lambda v: v.tensor_copy(r_cache[:, it * NT:(it + 1) * NT],
                                           r_t[0:1, :]),
             reads=[br], writes=[buf(r_cache.name)])


def _apply_ln_fm(cx, xtile, m_t, r_t, xn, NT, m_b=None, r_b=None, m_ap=None,
                 r_ap=None):
    P, buf = cx.P, cx.buf
    m_ap = m_ap if m_ap is not None else m_t[:]
    r_ap = r_ap if r_ap is not None else r_t[:]
    m_b = m_b if m_b is not None else buf(m_t.name)
    r_b = r_b if r_b is not None else buf(r_t.name)
    x3 = xtile[:].rearrange("p (c t) -> p c t", c=C)
    o3 = xn[:].rearrange("p (c t) -> p c t", c=C)
    mb = m_ap.unsqueeze(1).broadcast_to((128, C, NT))
    rb = r_ap.unsqueeze(1).broadcast_to((128, C, NT))
    P.op("ve", lambda v: v.tensor_tensor(o3, x3, mb, op=OP.subtract),
         reads=[buf(xtile.name), m_b], writes=[buf(xn.name)])
    P.op("ve", lambda v: v.tensor_tensor(o3, o3, rb, op=OP.mult),
         reads=[buf(xn.name), r_b], writes=[buf(xn.name)])


def _load_w(cx, ph, wname, tag):
    nc, P, g, buf = cx.nc, cx.P, cx.g, cx.buf
    w = _sb(nc, ph, tag, [128, C * D])
    src = g[wname][:].rearrange("(c p) n -> p c n", p=128)
    dst = w[:].rearrange("p (c n) -> p c n", c=C)
    P.op("sy", lambda s: s.dma_start(out=dst, in_=src),
         reads=[buf("hbm_" + wname)], writes=[buf(w.name)], dma="dsy")
    return w



def _phase1(cx):
    nc, P, g, buf, split = cx.nc, cx.P, cx.g, cx.buf, cx.split
    NT = 512
    NTILE = T4 // NT
    import contextlib
    with contextlib.ExitStack() as ph:
        wA = _load_w(cx, ph, "l0_wk", "p1_wA")
        wB = _load_w(cx, ph, "l0_wv", "p1_wB")

        xt = [_sb(nc, ph, f"p1_x{i}", [128, C * NT]) for i in range(2)]
        S0 = _sb(nc, ph, "p1_S0", [128, C * NT])
        S1 = _sb(nc, ph, "p1_S1", [128, C * NT])
        S2 = _sb(nc, ph, "p1_S2", [128, C * NT])
        SHt = _sb(nc, ph, "p1_SH", [128, C * NT], F32R)
        SLt = _sb(nc, ph, "p1_SL", [128, C * NT], F32R)
        SH = SHt[:]
        SL = SLt[:]
        m_t = _sb(nc, ph, "p1_m", [128, NT])
        r_t = _sb(nc, ph, "p1_r", [128, NT])
        m_row = _sb(nc, ph, "p1_mrow", [1, NT])
        r_row = _sb(nc, ph, "p1_rrow", [1, NT])
        bkv_bc = _sb(nc, ph, "p1_bkv", [128, 2 * D])
        kvd = _sb(nc, ph, "p1_kvd", [128, C * 128])

        ps_stats = _ps(nc, ph, "p1_ps_st", [128, 2 * NT])
        ps_proj = [_ps(nc, ph, f"p1_ps_p{i}", [128, NT]) for i in range(2)]
        ps_kv = [_ps(nc, ph, f"p1_ps_kv{i}", [128, 512]) for i in range(4)]

        ph_brow = contextlib.ExitStack()
        brow0 = _sb(nc, ph_brow, "p1_brow0", [1, D])
        for i, off in enumerate((D, 2 * D)):
            P.op("sy", lambda s, off=off: s.dma_start(
                out=brow0[:], in_=g["l0_brow"][0:1, off:off + D]),
                 reads=[buf("hbm_l0_brow")], writes=[buf(brow0.name)], dma="dsy")
            for half in range(2):
                P.op("te", lambda te, half=half: te.matmul(
                    ps_stats[:, 0:512], cx.onescol[0:1, :],
                    brow0[0:1, half * 512:(half + 1) * 512],
                    start=True, stop=True),
                     reads=[buf(cx.onescol.name), buf(brow0.name)],
                     writes=[buf(ps_stats.name)])
                P.op("ve", lambda v, i=i, half=half: v.tensor_copy(
                    bkv_bc[:, i * D + half * 512: i * D + (half + 1) * 512],
                    ps_stats[:, 0:512]),
                     reads=[buf(ps_stats.name)], writes=[buf(bkv_bc.name)])
        ph_brow.close()

        def hl_split(src_ap, sbuf):
            P.op("ve", lambda v: v.tensor_copy(SH, src_ap),
                 reads=[sbuf], writes=[buf(SHt.name)])
            P.op("ve", lambda v: v.tensor_tensor(SL, src_ap, SH.bitcast(F32),
                                                 op=OP.subtract),
                 reads=[sbuf, buf(SHt.name)], writes=[buf(SLt.name)])

        def fmchunk_ap(ap, cch):
            return ap[:, cch * NT:(cch + 1) * NT]

        def ln_stats(xtile, it, cache=True):
            bx = buf(xtile.name)
            psb = buf(ps_stats.name)

            def mm_sum(col):
                def e(te):
                    i = None
                    for cch in range(C):
                        i = te.matmul(ps_stats[:, col:col + NT], cx.ones_r[:],
                                      fmchunk_ap(SH, cch), start=(cch == 0),
                                      stop=False)
                        i = te.matmul(ps_stats[:, col:col + NT], cx.ones_r[:],
                                      fmchunk_ap(SL, cch), start=False,
                                      stop=(cch == C - 1))
                    return i
                return e
            hl_split(xtile[:], bx)
            P.op("te", mm_sum(0),
                 reads=[buf(SHt.name), buf(SLt.name), buf(cx.ones_r.name)],
                 writes=[psb])
            P.op("sc", lambda s: s.activation(S2[:], xtile[:], AF.Square),
                 reads=[bx], writes=[buf(S2.name)])
            hl_split(S2[:], buf(S2.name))
            P.op("te", mm_sum(NT),
                 reads=[buf(SHt.name), buf(SLt.name), buf(cx.ones_r.name)],
                 writes=[psb])
            bm, br = buf(m_t.name), buf(r_t.name)
            P.op("ve", lambda v: v.tensor_scalar(m_t[:], ps_stats[:, 0:NT],
                                                 1.0 / D, None, op0=OP.mult),
                 reads=[psb], writes=[bm])
            P.op("ve", lambda v: v.tensor_tensor(r_t[:], m_t[:], m_t[:],
                                                 op=OP.mult),
                 reads=[bm], writes=[br])
            P.op("ve", lambda v: v.scalar_tensor_tensor(
                r_t[:], ps_stats[:, NT:2 * NT], 1.0 / D, r_t[:],
                op0=OP.mult, op1=OP.subtract),
                 reads=[psb, br], writes=[br])
            P.op("sc", lambda s: s.activation(r_t[:], r_t[:], AF.Sqrt,
                                              bias=cx.eps_ap),
                 reads=[br, buf(cx.cpool.name)], writes=[br])
            P.op("ve", lambda v: v.reciprocal(r_t[:], r_t[:]),
                 reads=[br], writes=[br])
            if cache:
                P.op("gp", lambda gp, it=it: gp.dma_start(
                    out=g["mr_hbm"][0:1, it * NT:(it + 1) * NT], in_=m_t[0:1, :]),
                     reads=[bm], writes=[buf("hbm_mr")], dma="dgp")
                P.op("gp", lambda gp, it=it: gp.dma_start(
                    out=g["mr_hbm"][1:2, it * NT:(it + 1) * NT], in_=r_t[0:1, :]),
                     reads=[br], writes=[buf("hbm_mr")], dma="dgp")

        def load_x(it, src_name="x_fm"):
            xtile = xt[it % 2]
            bx = buf(xtile.name)
            if src_name == "x_fm":
                src = g["x_fm"][:].rearrange("(c p) t -> p c t", p=128)
            else:
                src = g[src_name][:].rearrange("p (c t) -> p c t", c=C)
            P.op("sy", lambda s, it=it, xtile=xtile, src=src: s.dma_start(
                out=xtile[:].rearrange("p (c t) -> p c t", c=C),
                in_=src[:, :, it * NT:(it + 1) * NT]),
                 reads=[buf("hbm_" + src_name)], writes=[bx], dma="dsy")
            return xtile, bx

        for it in range(NTILE):
            xtile, bx = load_x(it)
            ln_stats(xtile, it)
            xn = S2
            _apply_ln_fm(cx, xtile, m_t, r_t, xn, NT)

            ktile, vtile = S0, S1
            for which, w, otile in [(0, wA, ktile), (1, wB, vtile)]:
                for tp in range(4):
                    for dh2 in range(2):
                        ps = ps_proj[(tp * 2 + dh2) % 2]
                        psb = buf(ps.name)

                        def emit(te, tp=tp, dh2=dh2, w=w, ps=ps):
                            i = None
                            for ci in range(C):
                                st = _fm_chunk(xn, ci, NT)[:, tp * 128:(tp + 1) * 128]
                                mv = _fm_chunk(w, ci, D)[:, dh2 * 512:(dh2 + 1) * 512]
                                i = te.matmul(ps[:], st, mv, start=(ci == 0),
                                              stop=(ci == C - 1))
                            return i
                        P.op("te", emit,
                             reads=[buf(xn.name), buf(w.name)], writes=[psb])
                        dst = otile[:, tp * D + dh2 * 512: tp * D + (dh2 + 1) * 512]
                        bias_ap = bkv_bc[:, which * D + dh2 * 512:
                                         which * D + (dh2 + 1) * 512]
                        P.op("ve", lambda v, ps=ps, dst=dst, bias_ap=bias_ap:
                             v.tensor_tensor(dst, ps[:], bias_ap, op=OP.add),
                             reads=[psb, buf(bkv_bc.name)], writes=[buf(otile.name)])
                        if which == 0:
                            P.op("ve", lambda v, dst=dst: v.tensor_scalar(
                                dst, dst, 0.0, None, op0=OP.max),
                                 reads=[buf(otile.name)], writes=[buf(otile.name)])

            for tp in range(4):
                for hp in range(C):
                    ps = ps_kv[hp // 2]
                    col = (hp % 2) * 256
                    vblk = (hp // 2) * 256

                    def emit(te, tp=tp, hp=hp, ps=ps, col=col, vblk=vblk, it=it):
                        first = (it == 0 and tp == 0 and hp % 2 == 0)
                        last = (it == NTILE - 1 and tp == 3)
                        st = ktile[:, tp * D + hp * 128: tp * D + (hp + 1) * 128]
                        mv = vtile[:, tp * D + vblk: tp * D + vblk + 256]
                        return te.matmul(ps[:, col:col + 256], st, mv,
                                         start=first, stop=last,
                                         skip_group_check=True)
                    P.op("te", emit,
                         reads=[buf(ktile.name), buf(vtile.name)],
                         writes=[buf(ps.name)])

        P.op("ve", lambda v: v.memset(kvd[:], 0.0), writes=[buf(kvd.name)])
        sc_kv = 1.0 / (T4 * DH)
        for hp in range(C):
            ps = ps_kv[hp // 2]
            col = (hp % 2) * 256
            vin = (hp % 2) * 128
            P.op("ve", lambda v, hp=hp, ps=ps, col=col, vin=vin: v.tensor_scalar(
                kvd[0:64, hp * 128:hp * 128 + 64],
                ps[0:64, col + vin:col + vin + 64], sc_kv, None, op0=OP.mult),
                 reads=[buf(ps.name)], writes=[buf(kvd.name)])
            P.op("ve", lambda v, hp=hp, ps=ps, col=col, vin=vin: v.tensor_scalar(
                kvd[64:128, hp * 128 + 64:(hp + 1) * 128],
                ps[64:128, col + vin + 64:col + vin + 128], sc_kv, None,
                op0=OP.mult),
                 reads=[buf(ps.name)], writes=[buf(kvd.name)])
        cx.dbg("dbg_kvd", kvd[:], [buf(kvd.name)])

        for wname, w in [("l0_wq", wA), ("l0_wo", wB)]:
            src = g[wname][:].rearrange("(c p) n -> p c n", p=128)
            dst = w[:].rearrange("p (c n) -> p c n", c=C)
            P.op("sy", lambda s, src=src, dst=dst: s.dma_start(out=dst, in_=src),
                 reads=[buf("hbm_" + wname)], writes=[buf(w.name)], dma="dsy")

        for it in range(NTILE):
            xtile, bx = load_x(it)
            for which, rw in ((0, m_row), (1, r_row)):
                P.op("sy", lambda s, which=which, rw=rw, it=it: s.dma_start(
                    out=rw[:], in_=g["mr_hbm"][which:which + 1,
                                               it * NT:(it + 1) * NT]),
                     reads=[buf("hbm_mr")], writes=[buf(rw.name)], dma="dsy")
                P.op("te", lambda te, which=which, rw=rw: te.matmul(
                    ps_stats[:, which * NT:(which + 1) * NT],
                    cx.onescol[0:1, :], rw[:],
                    start=True, stop=True),
                     reads=[buf(rw.name), buf(cx.onescol.name)],
                     writes=[buf(ps_stats.name)])
            xn = S2
            _apply_ln_fm(cx, xtile, None, None, xn, NT,
                         m_b=buf(ps_stats.name), r_b=buf(ps_stats.name),
                         m_ap=ps_stats[:, 0:NT], r_ap=ps_stats[:, NT:2 * NT])

            q_fm, a_fm, x1t = S0, S1, xtile
            for co in range(C):
                ps = ps_proj[co % 2]
                psb = buf(ps.name)

                def emit(te, co=co, ps=ps):
                    i = None
                    for ci in range(C):
                        st = _fm_chunk(wA, ci, D)[:, co * 128:(co + 1) * 128]
                        mv = _fm_chunk(xn, ci, NT)
                        i = te.matmul(ps[:], st, mv, start=(ci == 0),
                                      stop=(ci == C - 1))
                    return i
                P.op("te", emit, reads=[buf(xn.name), buf(wA.name)], writes=[psb])
                P.op("sc", lambda s, co=co, ps=ps: s.activation(
                    _fm_chunk(q_fm, co, NT), ps[:], AF.Relu,
                    bias=cx.bpp0[:, 0, co:co + 1]),
                     reads=[psb, buf(cx.bpp0.name)], writes=[buf(q_fm.name)])

            for hp in range(C):
                ps = ps_proj[hp % 2]
                psb = buf(ps.name)
                P.op("te", lambda te, hp=hp, ps=ps: te.matmul(
                    ps[:], kvd[:, hp * 128:(hp + 1) * 128],
                    _fm_chunk(q_fm, hp, NT), start=True, stop=True),
                     reads=[buf(q_fm.name), buf(kvd.name)], writes=[psb])
                P.op("ve", lambda v, hp=hp, ps=ps: v.tensor_copy(
                    _fm_chunk(a_fm, hp, NT), ps[:]),
                     reads=[psb], writes=[buf(a_fm.name)])

            for co in range(C):
                ps = ps_proj[co % 2]
                psb = buf(ps.name)

                def emit(te, co=co, ps=ps):
                    i = None
                    for ci in range(C):
                        st = _fm_chunk(wB, ci, D)[:, co * 128:(co + 1) * 128]
                        mv = _fm_chunk(a_fm, ci, NT)
                        i = te.matmul(ps[:], st, mv, start=(ci == 0),
                                      stop=(ci == C - 1))
                    return i
                P.op("te", emit, reads=[buf(a_fm.name), buf(wB.name)],
                     writes=[psb])
                P.op("ve", lambda v, co=co, ps=ps, xtile=xtile:
                     v.scalar_tensor_tensor(
                         _fm_chunk(x1t, co, NT), ps[:], cx.bpp0[:, 3, co:co + 1],
                         _fm_chunk(xtile, co, NT), op0=OP.add, op1=OP.add),
                     reads=[psb, bx, buf(cx.bpp0.name)], writes=[buf(x1t.name)])
            P.op("gp", lambda gp, it=it: gp.dma_start(
                out=g["x1_hbm"][:].rearrange("p (c t) -> p c t", c=C)
                [:, :, it * NT:(it + 1) * NT],
                in_=x1t[:].rearrange("p (c t) -> p c t", c=C)),
                 reads=[buf(x1t.name)], writes=[buf("hbm_x1")], dma="dgp")
            cx.dbg_sl("dbg_x1", np.s_[:, it * NT * C:(it + 1) * NT * C],
                      x1t[:], [buf(x1t.name)])


def _mm3(te, ps_ap, st_hi, st_lo, mv_hi, mv_lo, first, last, skip=False):
    i = te.matmul(ps_ap, st_hi, mv_hi, start=first, stop=False,
                  skip_group_check=skip)
    i = te.matmul(ps_ap, st_lo, mv_hi, start=False, stop=False,
                  skip_group_check=skip)
    i = te.matmul(ps_ap, st_hi, mv_lo, start=False, stop=last,
                  skip_group_check=skip)
    return i


def _fp4_chain(cx, xb_ap, q_ap, W, s_a, s_g, s_p2, s_int, bm, rs, bufs):
    P = cx.P
    bx, ba, bg, bi, bbm, brs, bq = (bufs[k] for k in
                                    ("bx", "ba", "bg", "bi", "bbm", "brs", "bq"))
    a3 = s_a.rearrange("p (b k) -> p b k", k=16)
    P.op("sc", lambda s: s.activation(s_a, xb_ap, AF.Abs), reads=[bx], writes=[ba])
    P.op("ve", lambda v: v.tensor_reduce(bm, a3, axis=AX.X, op=OP.max),
         reads=[ba], writes=[bbm])
    P.op("ve", lambda v: v.tensor_scalar(bm, bm, 1.0 / 6.0, 1e-12,
                                         op0=OP.mult, op1=OP.max),
         reads=[bbm], writes=[bbm])
    P.op("ve", lambda v: v.reciprocal(rs, bm), reads=[bbm], writes=[brs])
    rsb = rs.unsqueeze(2).broadcast_to((128, W // 16, 16))
    bmb = bm.unsqueeze(2).broadcast_to((128, W // 16, 16))
    P.op("ve", lambda v: v.tensor_tensor(a3, a3, rsb, op=OP.mult),
         reads=[ba, brs], writes=[ba])
    P.op("ve", lambda v: v.tensor_scalar(s_g, s_a, 0.25, None, op0=OP.is_gt),
         reads=[ba], writes=[bg])
    for thr in (0.75, 1.25, 1.75):
        P.op("ve", lambda v, thr=thr: v.scalar_tensor_tensor(
            s_g, s_a, thr, s_g, op0=OP.is_gt, op1=OP.add),
             reads=[ba, bg], writes=[bg])
    bp2 = bufs["bp2"]
    P.op("ve", lambda v: v.tensor_scalar(s_p2, s_a, 5.0, 2.0,
                                         op0=OP.is_gt, op1=OP.mult),
         reads=[ba], writes=[bp2])
    for thr in (2.5, 3.5):
        P.op("ve", lambda v, thr=thr: v.scalar_tensor_tensor(
            s_p2, s_a, thr, s_p2, op0=OP.is_gt, op1=OP.add),
             reads=[ba, bp2], writes=[bp2])
    P.op("ve", lambda v: v.scalar_tensor_tensor(
        s_g, s_g, 0.5, s_p2, op0=OP.mult, op1=OP.add),
         reads=[bg, bp2], writes=[bg])
    P.op("sc", lambda s: s.activation(s_a, xb_ap, AF.Sign),
         reads=[bx], writes=[ba])
    P.op("ve", lambda v: v.tensor_tensor(s_g, s_g, s_a, op=OP.mult),
         reads=[bg, ba], writes=[bg])
    q3 = q_ap.rearrange("p (b k) -> p b k", k=16)
    P.op("ve", lambda v: v.tensor_tensor(
        q3, s_g.rearrange("p (b k) -> p b k", k=16), bmb, op=OP.mult),
         reads=[bg, bbm], writes=[bq])


def _phase2(cx):
    nc, P, g, buf = cx.nc, cx.P, cx.g, cx.buf
    import contextlib
    NSLAB = FF0 // 128
    with contextlib.ExitStack() as ph:
        wslab = [_sb(nc, ph, f"p2_wsl{i}", [128, FF0]) for i in range(2)]
        s_a = _sb(nc, ph, "p2_sa", [128, FF0])
        s_g = _sb(nc, ph, "p2_sg", [128, FF0])
        s_int = _sb(nc, ph, "p2_si", [128, FF0], I32)
        bm_t = _sb(nc, ph, "p2_bm", [128, FF0 // 16])
        rs_t = _sb(nc, ph, "p2_rs", [128, FF0 // 16])
        ps_tp = _ps(nc, ph, "p2_ps_tp", [128, 512])
        ps_tp2 = _ps(nc, ph, "p2_ps_tp2", [128, 512])

        s_p2t = _sb(nc, ph, "p2_sp2", [128, FF0])

        def chain_bufs(bx, bq):
            return {"bx": bx, "ba": buf(s_a.name), "bg": buf(s_g.name),
                    "bp2": buf(s_p2t.name), "bi": buf(s_int.name),
                    "bbm": buf(bm_t.name), "brs": buf(rs_t.name), "bq": bq}

        for which, wname in ((0, "f0_wg"), (1, "f0_wu")):
            for n in range(NSLAB):
                w = wslab[n % 2]
                bw = buf(w.name)
                P.op("sy", lambda s, wname=wname, n=n, w=w: s.dma_start(
                    out=w[:, 0:D], in_=g[wname][n * 128:(n + 1) * 128, :]),
                     reads=[buf("hbm_" + wname)], writes=[bw], dma="dsy")
                _fp4_chain(cx, w[:, 0:D], w[:, D:2 * D], D, s_a[:, 0:D],
                           s_g[:, 0:D], s_p2t[:, 0:D], s_int[:, 0:D],
                           bm_t[:, 0:64], rs_t[:, 0:64], chain_bufs(bw, bw))
                for ci in range(C):
                    dst_ps = ps_tp if ci % 2 == 0 else ps_tp2
                    P.op("te", lambda te, w=w, ci=ci, dst_ps=dst_ps: te.transpose(
                        dst_ps[:, 0:128], w[:, D + ci * 128:D + (ci + 1) * 128],
                        cx.idn[:]),
                         reads=[bw, buf(cx.idn.name)], writes=[buf(dst_ps.name)])
                    P.op("ve", lambda v, ci=ci, dst_ps=dst_ps: v.tensor_copy(
                        s_a[:, ci * 128:(ci + 1) * 128], dst_ps[:, 0:128]),
                         reads=[buf(dst_ps.name)], writes=[buf(s_a.name)])
                P.op("gp", lambda gp, which=which, n=n: gp.dma_start(
                    out=g["wq_fm_hbm"][:].rearrange("p (c n) -> p c n", c=C)
                    [:, :, which * FF0 + n * 128: which * FF0 + (n + 1) * 128],
                    in_=s_a[:, 0:D].rearrange("p (c n) -> p c n", c=C)),
                     reads=[buf(s_a.name)], writes=[buf("hbm_wqfm")], dma="dgp")

        for n in range(C):
            w = wslab[n % 2]
            bw = buf(w.name)
            P.op("sy", lambda s, n=n, w=w: s.dma_start(
                out=w[:], in_=g["f0_wd"][n * 128:(n + 1) * 128, :]),
                 reads=[buf("hbm_f0_wd")], writes=[bw], dma="dsy")
            _fp4_chain(cx, w[:], s_g[:], FF0, s_a[:], s_g[:], s_p2t[:],
                       s_int[:], bm_t[:], rs_t[:], chain_bufs(bw, buf(s_g.name)))
            for k2 in range(NSLAB):
                dst_ps = ps_tp if k2 % 2 == 0 else ps_tp2
                P.op("te", lambda te, k2=k2, dst_ps=dst_ps: te.transpose(
                    dst_ps[:, 0:128], s_g[:, k2 * 128:(k2 + 1) * 128], cx.idn[:]),
                     reads=[buf(s_g.name), buf(cx.idn.name)],
                     writes=[buf(dst_ps.name)])
                P.op("ve", lambda v, k2=k2, dst_ps=dst_ps: v.tensor_copy(
                    s_a[:, k2 * 128:(k2 + 1) * 128], dst_ps[:, 0:128]),
                     reads=[buf(dst_ps.name)], writes=[buf(s_a.name)])
            P.op("gp", lambda gp, n=n: gp.dma_start(
                out=g["wd_fm_hbm"][:].rearrange("p (c n) -> p c n", c=NSLAB)
                [:, :, n * 128:(n + 1) * 128],
                in_=s_a[:, 0:FF0].rearrange("p (c n) -> p c n", c=NSLAB)),
                 reads=[buf(s_a.name)], writes=[buf("hbm_wdfm")], dma="dgp")

    P.fence()
    NSUP = 512
    NTP = NSUP // 128
    with contextlib.ExitStack() as ph:
        x1sup = _sb(nc, ph, "p2_x1s", [128, C * NSUP])
        xqT_hi = _sb(nc, ph, "p2_xqh", [128, NTP * C * 128], F32R)
        xqT_lo = _sb(nc, ph, "p2_xql", [128, NTP * C * 128], F32R)
        hqfm = _sb(nc, ph, "p2_hqfm", [128, NSLAB * NSUP])
        wsl_hi = [_sb(nc, ph, f"p2_wslh{i}", [128, C * 256], F32R) for i in range(2)]
        wsl_lo = [_sb(nc, ph, f"p2_wsll{i}", [128, C * 256], F32R) for i in range(2)]
        wsl_raw = [_sb(nc, ph, f"p2_wslr{i}", [128, C * 256]) for i in range(2)]
        wdcol = [_sb(nc, ph, f"p2_wdc{i}", [128, NSLAB * 128]) for i in range(2)]
        hpiece = _sb(nc, ph, "p2_hp", [128, NTP * 256])
        s_a = _sb(nc, ph, "p2m_sa", [128, NTP * 256])
        s_g = _sb(nc, ph, "p2m_sg", [128, NTP * 256])
        s_int = _sb(nc, ph, "p2m_si", [128, NTP * 256], I32)
        bm_t = _sb(nc, ph, "p2m_bm", [128, NTP * 16])
        rs_t = _sb(nc, ph, "p2m_rs", [128, NTP * 16])
        xn2 = _sb(nc, ph, "p2_xn2", [128, D])
        xq2 = _sb(nc, ph, "p2_xq2", [128, D])
        junk = _sb(nc, ph, "p2_junk", [128, D])
        mv_t = _sb(nc, ph, "p2_mv", [128, 4])
        ps_x1 = _ps(nc, ph, "p2_ps_x1", [128, D])
        ps_tp = _ps(nc, ph, "p2_ps_t", [128, 512])
        ps_gu = [_ps(nc, ph, f"p2_ps_gu{i}", [128, 256]) for i in range(2)]
        ps_dn = [_ps(nc, ph, f"p2_ps_dn{i}", [128, NTP * 128]) for i in range(2)]

        s_p2t = _sb(nc, ph, "p2m_sp2", [128, NTP * 256])

        def chain_bufs(bx, bq):
            return {"bx": bx, "ba": buf(s_a.name), "bg": buf(s_g.name),
                    "bp2": buf(s_p2t.name), "bi": buf(s_int.name),
                    "bbm": buf(bm_t.name), "brs": buf(rs_t.name), "bq": bq}

        for sup in range(T4 // NSUP):
            P.op("sy", lambda s, sup=sup: s.dma_start(
                out=x1sup[:].rearrange("p (c t) -> p c t", c=C),
                in_=g["x1_hbm"][:].rearrange("p (c t) -> p c t", c=C)
                [:, :, sup * NSUP:(sup + 1) * NSUP]),
                 reads=[buf("hbm_x1")], writes=[buf(x1sup.name)], dma="dsy")
            if sup == 0:
                cx.dbg("dbg_x1rd", x1sup[:], [buf(x1sup.name)])
            for tp in range(NTP):
                for ci in range(C):
                    P.op("te", lambda te, tp=tp, ci=ci: te.transpose(
                        ps_x1[:, ci * 128:(ci + 1) * 128],
                        x1sup[:, ci * NSUP + tp * 128: ci * NSUP + (tp + 1) * 128],
                        cx.idn[:]),
                         reads=[buf(x1sup.name), buf(cx.idn.name)],
                         writes=[buf(ps_x1.name)])
                P.op("ve", lambda v: v.tensor_reduce(mv_t[:, 0:1], ps_x1[:],
                                                     axis=AX.X, op=OP.add),
                     reads=[buf(ps_x1.name)], writes=[buf(mv_t.name)])
                P.op("sc", lambda s: s.activation(junk[:], ps_x1[:], AF.Square,
                                                  accum_out=mv_t[:, 1:2]),
                     reads=[buf(ps_x1.name)], writes=[buf(junk.name), buf(mv_t.name)])
                P.op("ve", lambda v: v.tensor_scalar(mv_t[:, 0:1], mv_t[:, 0:1],
                                                     1.0 / D, None, op0=OP.mult),
                     reads=[buf(mv_t.name)], writes=[buf(mv_t.name)])
                P.op("ve", lambda v: v.tensor_tensor(mv_t[:, 2:3], mv_t[:, 0:1],
                                                     mv_t[:, 0:1], op=OP.mult),
                     reads=[buf(mv_t.name)], writes=[buf(mv_t.name)])
                P.op("ve", lambda v: v.scalar_tensor_tensor(
                    mv_t[:, 2:3], mv_t[:, 1:2], 1.0 / D, mv_t[:, 2:3],
                    op0=OP.mult, op1=OP.subtract),
                     reads=[buf(mv_t.name)], writes=[buf(mv_t.name)])
                P.op("sc", lambda s: s.activation(mv_t[:, 2:3], mv_t[:, 2:3],
                                                  AF.Sqrt, bias=cx.eps_ap),
                     reads=[buf(mv_t.name), buf(cx.cpool.name)],
                     writes=[buf(mv_t.name)])
                P.op("ve", lambda v: v.reciprocal(mv_t[:, 2:3], mv_t[:, 2:3]),
                     reads=[buf(mv_t.name)], writes=[buf(mv_t.name)])
                P.op("ve", lambda v: v.tensor_scalar(
                    xn2[:], ps_x1[:], mv_t[:, 0:1], mv_t[:, 2:3],
                    op0=OP.subtract, op1=OP.mult),
                     reads=[buf(ps_x1.name), buf(mv_t.name)], writes=[buf(xn2.name)])
                _fp4_chain(cx, xn2[:], xq2[:], D, s_a[:, 0:D], s_g[:, 0:D],
                           s_p2t[:, 0:D], s_int[:, 0:D], bm_t[:, 0:64],
                           rs_t[:, 0:64],
                           chain_bufs(buf(xn2.name), buf(xq2.name)))
                if sup == 0 and tp == 0:
                    cx.dbg("dbg_xq", xq2[:], [buf(xq2.name)])
                    cx.dbg("dbg_xn2", xn2[:], [buf(xn2.name)])
                for ci in range(C):
                    P.op("te", lambda te, ci=ci: te.transpose(
                        ps_tp[:, 0:128], xq2[:, ci * 128:(ci + 1) * 128],
                        cx.idn[:]),
                         reads=[buf(xq2.name), buf(cx.idn.name)],
                         writes=[buf(ps_tp.name)])
                    off = (tp * C + ci) * 128
                    P.op("ve", lambda v, off=off: v.tensor_copy(
                        xqT_hi[:, off:off + 128], ps_tp[:, 0:128]),
                         reads=[buf(ps_tp.name)], writes=[buf(xqT_hi.name)])
                    P.op("ve", lambda v, off=off: v.tensor_tensor(
                        xqT_lo[:, off:off + 128], ps_tp[:, 0:128],
                        xqT_hi[:, off:off + 128].bitcast(F32), op=OP.subtract),
                         reads=[buf(ps_tp.name), buf(xqT_hi.name)],
                         writes=[buf(xqT_lo.name)])

            for sl in range(FF0 // 256):
                wr = wsl_raw[sl % 2]
                wh, wl = wsl_hi[sl % 2], wsl_lo[sl % 2]
                for which in range(2):
                    P.op("sy", lambda s, sl=sl, which=which, wr=wr: s.dma_start(
                        out=wr[:].rearrange("p (c n) -> p c n", c=C),
                        in_=g["wq_fm_hbm"][:].rearrange("p (c n) -> p c n", c=C)
                        [:, :, which * FF0 + sl * 256: which * FF0 + (sl + 1) * 256]),
                         reads=[buf("hbm_wqfm")], writes=[buf(wr.name)], dma="dsy")
                    if sup == 0 and sl == 0 and which == 0:
                        cx.dbg("dbg_wq", wr[:], [buf(wr.name)])
                    P.op("ve", lambda v, wr=wr, wh=wh: v.tensor_copy(wh[:], wr[:]),
                         reads=[buf(wr.name)], writes=[buf(wh.name)])
                    P.op("ve", lambda v, wr=wr, wh=wh, wl=wl: v.tensor_tensor(
                        wl[:], wr[:], wh[:].bitcast(F32), op=OP.subtract),
                         reads=[buf(wr.name), buf(wh.name)], writes=[buf(wl.name)])
                    for tp in range(NTP):
                        ps = ps_gu[tp % 2]

                        def emit(te, tp=tp, wh=wh, wl=wl, ps=ps):
                            i = None
                            for ci in range(C):
                                off = (tp * C + ci) * 128
                                sh = xqT_hi[:, off:off + 128]
                                sl_ = xqT_lo[:, off:off + 128]
                                mh = wh[:, ci * 256:(ci + 1) * 256]
                                ml = wl[:, ci * 256:(ci + 1) * 256]
                                i = _mm3(te, ps[:], sh, sl_, mh, ml,
                                         ci == 0, ci == C - 1)
                            return i
                        P.op("te", emit,
                             reads=[buf(xqT_hi.name), buf(xqT_lo.name),
                                    buf(wh.name), buf(wl.name)],
                             writes=[buf(ps.name)])
                        dst = hpiece[:, tp * 256:(tp + 1) * 256]
                        if which == 0:
                            P.op("sc", lambda s, ps=ps, dst=dst: s.activation(
                                dst, ps[:], AF.Silu),
                                 reads=[buf(ps.name)], writes=[buf(hpiece.name)])
                        else:
                            P.op("ve", lambda v, ps=ps, dst=dst: v.tensor_tensor(
                                dst, ps[:], dst, op=OP.mult),
                                 reads=[buf(ps.name), buf(hpiece.name)],
                                 writes=[buf(hpiece.name)])
                _fp4_chain(cx, hpiece[:], hpiece[:], NTP * 256, s_a[:], s_g[:],
                           s_p2t[:], s_int[:], bm_t[:], rs_t[:],
                           chain_bufs(buf(hpiece.name), buf(hpiece.name)))
                for tp in range(NTP):
                    for half in range(2):
                        k2c = sl * 2 + half
                        P.op("te", lambda te, tp=tp, half=half: te.transpose(
                            ps_tp[:, 0:128],
                            hpiece[:, tp * 256 + half * 128: tp * 256 + (half + 1) * 128],
                            cx.idn[:]),
                             reads=[buf(hpiece.name), buf(cx.idn.name)],
                             writes=[buf(ps_tp.name)])
                        P.op("ve", lambda v, k2c=k2c, tp=tp: v.tensor_copy(
                            hqfm[:, k2c * NSUP + tp * 128: k2c * NSUP + (tp + 1) * 128],
                            ps_tp[:, 0:128]),
                             reads=[buf(ps_tp.name)], writes=[buf(hqfm.name)])

            if sup == 0:
                cx.dbg("dbg_hq", hqfm[:], [buf(hqfm.name)])
            for co in range(C):
                wd = wdcol[co % 2]
                P.op("sy", lambda s, co=co, wd=wd: s.dma_start(
                    out=wd[:].rearrange("p (c n) -> p c n", c=NSLAB),
                    in_=g["wd_fm_hbm"][:].rearrange("p (c n) -> p c n", c=NSLAB)
                    [:, :, co * 128:(co + 1) * 128]),
                     reads=[buf("hbm_wdfm")], writes=[buf(wd.name)], dma="dsy")
                if sup == 0 and co == 0:
                    cx.dbg("dbg_wd", wd[:], [buf(wd.name)])
                ps = ps_dn[co % 2]

                def emit(te, co=co, wd=wd, ps=ps):
                    i = None
                    for k2 in range(NSLAB):
                        st = wd[:, k2 * 128:(k2 + 1) * 128]
                        mv = hqfm[:, k2 * NSUP:(k2 + 1) * NSUP]
                        i = te.matmul(ps[:], st, mv, start=(k2 == 0),
                                      stop=(k2 == NSLAB - 1))
                    return i
                P.op("te", emit, reads=[buf(wd.name), buf(hqfm.name)],
                     writes=[buf(ps.name)])
                P.op("ve", lambda v, co=co, ps=ps: v.tensor_tensor(
                    s_a[:, 0:NSUP], ps[:],
                    x1sup[:, co * NSUP:(co + 1) * NSUP], op=OP.add),
                     reads=[buf(ps.name), buf(x1sup.name)], writes=[buf(s_a.name)])
                P.op("gp", lambda gp, co=co, sup=sup: gp.dma_start(
                    out=g["x2_hbm"][:, co * T4 + sup * NSUP:
                                    co * T4 + (sup + 1) * NSUP],
                    in_=s_a[:, 0:NSUP]),
                     reads=[buf(s_a.name)], writes=[buf("hbm_x2")], dma="dgp")
                cx.dbg_sl("dbg_x2", np.s_[:, co * T4 + sup * NSUP:
                                          co * T4 + (sup + 1) * NSUP],
                          s_a[:, 0:NSUP], [buf(s_a.name)])


def _phase3(cx):
    nc, P, g, buf = cx.nc, cx.P, cx.g, cx.buf
    import contextlib
    NT = 512

    def proj_pass(kind):
        with contextlib.ExitStack() as ph:
            wname = {"k": "l1_wk", "v": "l1_wv", "q": "l1_wq"}[kind]
            w = _load_w(cx, ph, wname, f"p3_{kind}_w")
            ntile = (T2 if kind == "q" else T4) // NT
            xt = [_sb(nc, ph, f"p3{kind}_x{i}", [128, C * NT]) for i in range(2)]
            S0 = _sb(nc, ph, f"p3{kind}_S0", [128, C * NT])
            S1 = _sb(nc, ph, f"p3{kind}_S1", [128, C * NT])
            S2 = _sb(nc, ph, f"p3{kind}_S2", [128, C * NT])
            SHt = _sb(nc, ph, f"p3{kind}_SH", [128, C * NT], F32R)
            SLt = _sb(nc, ph, f"p3{kind}_SL", [128, C * NT], F32R)
            m_t = _sb(nc, ph, f"p3{kind}_m", [128, NT])
            r_t = _sb(nc, ph, f"p3{kind}_r", [128, NT])
            m_row = _sb(nc, ph, f"p3{kind}_mr", [1, NT])
            r_row = _sb(nc, ph, f"p3{kind}_rr", [1, NT])
            rope_raw = _sb(nc, ph, f"p3{kind}_rr4", [128, 4 * DH])
            rope_cs = _sb(nc, ph, f"p3{kind}_rcs", [128, 2 * DH])
            bv_bc = _sb(nc, ph, f"p3{kind}_bvb", [128, D])
            ps_stats = _ps(nc, ph, f"p3{kind}_pst", [128, 2 * NT])
            ps_a = [_ps(nc, ph, f"p3{kind}_psa{i}", [128, NT]) for i in range(2)]
            ps_tm = _ps(nc, ph, f"p3{kind}_ptm", [128, 2048])

            if kind == "v":
                brow1 = _sb(nc, ph, f"p3v_brow", [1, D])
                P.op("sy", lambda s: s.dma_start(
                    out=brow1[:], in_=g["l1_brow"][0:1, 2 * D:3 * D]),
                     reads=[buf("hbm_l1_brow")], writes=[buf(brow1.name)],
                     dma="dsy")
                for half in range(2):
                    P.op("te", lambda te, half=half: te.matmul(
                        ps_stats[:, 0:512], cx.onescol[0:1, :],
                        brow1[0:1, half * 512:(half + 1) * 512],
                        start=True, stop=True),
                         reads=[buf(cx.onescol.name), buf(brow1.name)],
                         writes=[buf(ps_stats.name)])
                    P.op("ve", lambda v, half=half: v.tensor_copy(
                        bv_bc[:, half * 512:(half + 1) * 512], ps_stats[:, 0:512]),
                         reads=[buf(ps_stats.name)], writes=[buf(bv_bc.name)])

            def hl_split(src_ap, sbuf):
                P.op("ve", lambda v: v.tensor_copy(SHt[:], src_ap),
                     reads=[sbuf], writes=[buf(SHt.name)])
                P.op("ve", lambda v: v.tensor_tensor(
                    SLt[:], src_ap, SHt[:].bitcast(F32), op=OP.subtract),
                     reads=[sbuf, buf(SHt.name)], writes=[buf(SLt.name)])

            for it in range(ntile):
                xtile = xt[it % 2]
                bx = buf(xtile.name)
                P.op("sy", lambda s, it=it, xtile=xtile: s.dma_start(
                    out=xtile[:].rearrange("p (c t) -> p c t", c=C),
                    in_=g["x2_hbm"][:].rearrange("p (c t) -> p c t", c=C)
                    [:, :, it * NT:(it + 1) * NT]),
                     reads=[buf("hbm_x2")], writes=[bx], dma="dsy")
                if kind == "k":
                    hl_split(xtile[:], bx)
                    psb = buf(ps_stats.name)

                    def mm_sum(col):
                        def e(te):
                            i = None
                            for cch in range(C):
                                i = te.matmul(ps_stats[:, col:col + NT],
                                              cx.ones_r[:],
                                              SHt[:, cch * NT:(cch + 1) * NT],
                                              start=(cch == 0), stop=False)
                                i = te.matmul(ps_stats[:, col:col + NT],
                                              cx.ones_r[:],
                                              SLt[:, cch * NT:(cch + 1) * NT],
                                              start=False, stop=(cch == C - 1))
                            return i
                        return e
                    P.op("te", mm_sum(0),
                         reads=[buf(SHt.name), buf(SLt.name),
                                buf(cx.ones_r.name)], writes=[psb])
                    P.op("sc", lambda s: s.activation(S2[:], xtile[:], AF.Square),
                         reads=[bx], writes=[buf(S2.name)])
                    hl_split(S2[:], buf(S2.name))
                    P.op("te", mm_sum(NT),
                         reads=[buf(SHt.name), buf(SLt.name),
                                buf(cx.ones_r.name)], writes=[psb])
                    bm, br = buf(m_t.name), buf(r_t.name)
                    P.op("ve", lambda v: v.tensor_scalar(
                        m_t[:], ps_stats[:, 0:NT], 1.0 / D, None, op0=OP.mult),
                         reads=[psb], writes=[bm])
                    P.op("ve", lambda v: v.tensor_tensor(
                        r_t[:], m_t[:], m_t[:], op=OP.mult),
                         reads=[bm], writes=[br])
                    P.op("ve", lambda v: v.scalar_tensor_tensor(
                        r_t[:], ps_stats[:, NT:2 * NT], 1.0 / D, r_t[:],
                        op0=OP.mult, op1=OP.subtract),
                         reads=[psb, br], writes=[br])
                    P.op("sc", lambda s: s.activation(
                        r_t[:], r_t[:], AF.Sqrt, bias=cx.eps_ap),
                         reads=[br, buf(cx.cpool.name)], writes=[br])
                    P.op("ve", lambda v: v.reciprocal(r_t[:], r_t[:]),
                         reads=[br], writes=[br])
                    P.op("gp", lambda gp, it=it: gp.dma_start(
                        out=g["mr_hbm"][0:1, it * NT:(it + 1) * NT],
                        in_=m_t[0:1, :]),
                         reads=[bm], writes=[buf("hbm_mr")], dma="dgp")
                    P.op("gp", lambda gp, it=it: gp.dma_start(
                        out=g["mr_hbm"][1:2, it * NT:(it + 1) * NT],
                        in_=r_t[0:1, :]),
                         reads=[br], writes=[buf("hbm_mr")], dma="dgp")
                    _apply_ln_fm(cx, xtile, m_t, r_t, S2, NT)
                else:
                    for which, rw in ((0, m_row), (1, r_row)):
                        P.op("sy", lambda s, which=which, rw=rw, it=it:
                             s.dma_start(out=rw[:], in_=g["mr_hbm"]
                                         [which:which + 1, it * NT:(it + 1) * NT]),
                             reads=[buf("hbm_mr")], writes=[buf(rw.name)],
                             dma="dsy")
                        P.op("te", lambda te, which=which, rw=rw: te.matmul(
                            ps_stats[:, which * NT:(which + 1) * NT],
                            cx.onescol[0:1, :], rw[:], start=True, stop=True),
                             reads=[buf(rw.name), buf(cx.onescol.name)],
                             writes=[buf(ps_stats.name)])
                    _apply_ln_fm(cx, xtile, None, None, S2, NT,
                                 m_b=buf(ps_stats.name), r_b=buf(ps_stats.name),
                                 m_ap=ps_stats[:, 0:NT],
                                 r_ap=ps_stats[:, NT:2 * NT])
                xn = S2

                if kind == "v":
                    for tp in range(4):
                        for dh2 in range(2):
                            ps = ps_a[(tp * 2 + dh2) % 2]
                            psb = buf(ps.name)

                            def emit(te, tp=tp, dh2=dh2, ps=ps):
                                i = None
                                for ci in range(C):
                                    st = xn[:, ci * NT + tp * 128:
                                            ci * NT + (tp + 1) * 128]
                                    mv = w[:, ci * D + dh2 * 512:
                                           ci * D + (dh2 + 1) * 512]
                                    i = te.matmul(ps[:], st, mv,
                                                  start=(ci == 0),
                                                  stop=(ci == C - 1))
                                return i
                            P.op("te", emit,
                                 reads=[buf(xn.name), buf(w.name)], writes=[psb])
                            dst = S0[:, tp * D + dh2 * 512:
                                     tp * D + (dh2 + 1) * 512]
                            P.op("ve", lambda v, ps=ps, dst=dst, dh2=dh2:
                                 v.tensor_tensor(
                                     dst, ps[:],
                                     bv_bc[:, dh2 * 512:(dh2 + 1) * 512],
                                     op=OP.add),
                                 reads=[psb, buf(bv_bc.name)],
                                 writes=[buf(S0.name)])
                    for tp in range(4):
                        for hp in range(C):
                            P.op("gp", lambda gp, tp=tp, hp=hp, it=it:
                                 gp.dma_start(
                                     out=g["v_hbm"][hp:hp + 1,
                                                    it * NT + tp * 128:
                                                    it * NT + (tp + 1) * 128, :]
                                     .squeeze(0),
                                     in_=S0[:, tp * D + hp * 128:
                                            tp * D + (hp + 1) * 128]),
                                 reads=[buf(S0.name)], writes=[buf("hbm_v")],
                                 dma="dgp")
                else:
                    bcol = 0 if kind == "q" else 1
                    for co in range(C):
                        ps = ps_a[co % 2]
                        psb = buf(ps.name)

                        def emit(te, co=co, ps=ps):
                            i = None
                            for ci in range(C):
                                st = w[:, ci * D + co * 128:
                                       ci * D + (co + 1) * 128]
                                mv = xn[:, ci * NT:(ci + 1) * NT]
                                i = te.matmul(ps[:], st, mv, start=(ci == 0),
                                              stop=(ci == C - 1))
                            return i
                        P.op("te", emit,
                             reads=[buf(xn.name), buf(w.name)], writes=[psb])
                        P.op("sc", lambda s, co=co, ps=ps: s.activation(
                            S0[:, co * NT:(co + 1) * NT], ps[:], AF.Identity,
                            bias=cx.bpp1[:, bcol, co:co + 1]),
                             reads=[psb, buf(cx.bpp1.name)], writes=[buf(S0.name)])
                    tbl = "ropeq" if kind == "q" else "ropek"
                    for tp in range(4):
                        P.op("sy", lambda s, tp=tp, it=it: s.dma_start(
                            out=rope_raw[:],
                            in_=g[tbl][it * NT + tp * 128:
                                       it * NT + (tp + 1) * 128, :]),
                             reads=[buf("hbm_" + tbl)],
                             writes=[buf(rope_raw.name)], dma="dsy")
                        P.op("ve", lambda v: v.tensor_tensor(
                            rope_cs[:, 0:DH], rope_raw[:, 0:DH],
                            rope_raw[:, DH:2 * DH], op=OP.add),
                             reads=[buf(rope_raw.name)], writes=[buf(rope_cs.name)])
                        P.op("ve", lambda v: v.tensor_tensor(
                            rope_cs[:, DH:2 * DH], rope_raw[:, 2 * DH:3 * DH],
                            rope_raw[:, 3 * DH:4 * DH], op=OP.add),
                             reads=[buf(rope_raw.name)], writes=[buf(rope_cs.name)])
                        for ci in range(C):
                            P.op("te", lambda te, ci=ci, tp=tp: te.transpose(
                                ps_tm[:, ci * 128:(ci + 1) * 128],
                                S0[:, ci * NT + tp * 128:ci * NT + (tp + 1) * 128],
                                cx.idn[:]),
                                 reads=[buf(S0.name), buf(cx.idn.name)],
                                 writes=[buf(ps_tm.name)])
                        p3 = ps_tm[:, 0:1024].rearrange("p (h d) -> p h d", d=DH)
                        o3 = S1[:, tp * 1024:(tp + 1) * 1024].rearrange(
                            "p (h d) -> p h d", d=DH)
                        cs = rope_cs[:, 0:DH]
                        sn = rope_cs[:, DH:2 * DH]
                        P.op("ve", lambda v, p3=p3, o3=o3, cs=cs: v.tensor_tensor(
                            o3, p3, cs.unsqueeze(1).broadcast_to((128, 2 * C, DH)),
                            op=OP.mult),
                             reads=[buf(ps_tm.name), buf(rope_cs.name)],
                             writes=[buf(S1.name)])
                        for gxx in range(2):
                            po = ps_tm[:, 0:1024].rearrange("p (h g d) -> p h g d", g=2, d=32)
                            oo = S1[:, tp * 1024:(tp + 1) * 1024].rearrange(
                                "p (h g d) -> p h g d", g=2, d=32)
                            src_g = po[:, :, 1 - gxx, :]
                            dst_g = oo[:, :, gxx, :]
                            s_g = sn[:, gxx * 32:(gxx + 1) * 32]
                            P.op("ve", lambda v, src_g=src_g, s_g=s_g:
                                 v.tensor_tensor(
                                     S2[:, 0:2 * C * 32].rearrange(
                                         "p (h d) -> p h d", d=32),
                                     src_g,
                                     s_g.unsqueeze(1).broadcast_to((128, 2 * C, 32)),
                                     op=OP.mult),
                                 reads=[buf(ps_tm.name), buf(rope_cs.name)],
                                 writes=[buf(S2.name)])
                            P.op("ve", lambda v, dst_g=dst_g: v.tensor_tensor(
                                dst_g, dst_g,
                                S2[:, 0:2 * C * 32].rearrange(
                                    "p (h d) -> p h d", d=32),
                                op=OP.add),
                                 reads=[buf(S2.name), buf(S1.name)],
                                 writes=[buf(S1.name)])
                        for ci in range(C):
                            P.op("te", lambda te, ci=ci, tp=tp: te.transpose(
                                ps_tm[:, 1024 + ci * 128:1024 + (ci + 1) * 128],
                                S1[:, tp * 1024 + ci * 128:
                                   tp * 1024 + (ci + 1) * 128],
                                cx.idn[:]),
                                 reads=[buf(S1.name), buf(cx.idn.name)],
                                 writes=[buf(ps_tm.name)])
                            P.op("ve", lambda v, ci=ci, tp=tp: v.tensor_copy(
                                S0[:, ci * NT + tp * 128:ci * NT + (tp + 1) * 128],
                                ps_tm[:, 1024 + ci * 128:1024 + (ci + 1) * 128]),
                                 reads=[buf(ps_tm.name)], writes=[buf(S0.name)])
                    dst_t = "q_hbm" if kind == "q" else "k_hbm"
                    for ci in range(C):
                        for hh in range(2):
                            P.op("gp", lambda gp, ci=ci, hh=hh, it=it:
                                 gp.dma_start(
                                     out=g[dst_t][2 * ci + hh:2 * ci + hh + 1, :,
                                                  it * NT:(it + 1) * NT]
                                     .squeeze(0),
                                     in_=S0[hh * 64:(hh + 1) * 64,
                                            ci * NT:(ci + 1) * NT]),
                                 reads=[buf(S0.name)],
                                 writes=[buf("hbm_" + dst_t)], dma="dgp")

        P.fence()

    proj_pass("k")
    proj_pass("v")
    proj_pass("q")

    NSUPQ = 512
    with contextlib.ExitStack() as ph:
        kraw = _sb(nc, ph, "p3a_kraw", [64, T4])
        k_hi = _sb(nc, ph, "p3a_kh", [64, T4], F32R)
        k_lo = _sb(nc, ph, "p3a_kl", [64, T4], F32R)
        vstr = _sb(nc, ph, "p3a_v", [128, 32 * 128])
        vext = _sb(nc, ph, "p3a_ve", [128, 32 * 130])
        qs = _sb(nc, ph, "p3a_q", [64, 2 * NSUPQ])
        q_hi = _sb(nc, ph, "p3a_qh", [64, 2 * NSUPQ], F32R)
        q_lo = _sb(nc, ph, "p3a_ql", [64, 2 * NSUPQ], F32R)
        ptile = [_sb(nc, ph, f"p3a_p{i}", [128, NSUPQ]) for i in range(2)]
        dnr = _sb(nc, ph, "p3a_dnr", [1, NSUPQ])
        dbc = _sb(nc, ph, "p3a_dbc", [64, NSUPQ])
        o_heads = _sb(nc, ph, "p3a_oh", [64, H * NSUPQ])
        wo = [_sb(nc, ph, f"p3a_wo{i}", [64, H * 128]) for i in range(2)]
        x2own = _sb(nc, ph, "p3a_x2o", [128, C * NSUPQ])
        x3t = _sb(nc, ph, "p3a_x3", [128, C * NSUPQ])
        ps_S = [_ps(nc, ph, f"p3a_psS{i}", [128, NSUPQ]) for i in range(2)]
        ps_OA = _ps(nc, ph, "p3a_psOA", [65, NSUPQ])
        ps_OB = _ps(nc, ph, "p3a_psOB", [65, NSUPQ])
        ps_bc = _ps(nc, ph, "p3a_psbc", [64, NSUPQ])
        ps_o = [_ps(nc, ph, f"p3a_pso{i}", [128, NSUPQ]) for i in range(2)]

        P.op("ve", lambda v: v.memset(vext[:], 1.0), writes=[buf(vext.name)])

        for sup in range(T2 // NSUPQ):
            for hp in range(C):
                P.op("sy", lambda s, hp=hp: s.dma_start(
                    out=vstr[:].rearrange("p (k n) -> p k n", k=32),
                    in_=g["v_hbm"][hp:hp + 1, :, :].squeeze(0)
                    .rearrange("(k p) n -> p k n", p=128)),
                     reads=[buf("hbm_v")], writes=[buf(vstr.name)], dma="dsy")
                for half in range(2):
                    P.op("ve", lambda v, half=half: v.tensor_copy(
                        vext[:].rearrange("p (k n) -> p k n", k=32)
                        [:, :, half * 65:half * 65 + 64],
                        vstr[:].rearrange("p (k n) -> p k n", k=32)
                        [:, :, half * 64:(half + 1) * 64]),
                         reads=[buf(vstr.name)], writes=[buf(vext.name)])
                P.op("sy", lambda s, hp=hp, sup=sup: s.dma_start(
                    out=qs[:].rearrange("p (h n) -> p h n", h=2),
                    in_=g["q_hbm"][2 * hp:2 * hp + 2, :,
                                   sup * NSUPQ:(sup + 1) * NSUPQ]
                    .rearrange("h p n -> p h n")),
                     reads=[buf("hbm_q_hbm")], writes=[buf(qs.name)], dma="dsy")
                P.op("ve", lambda v: v.tensor_copy(q_hi[:], qs[:]),
                     reads=[buf(qs.name)], writes=[buf(q_hi.name)])
                P.op("ve", lambda v: v.tensor_tensor(
                    q_lo[:], qs[:], q_hi[:].bitcast(F32), op=OP.subtract),
                     reads=[buf(qs.name), buf(q_hi.name)], writes=[buf(q_lo.name)])

                for hh, ps_O in ((0, ps_OA), (1, ps_OB)):
                    kh, kl = k_hi, k_lo
                    P.op("sy", lambda s, hp=hp, hh=hh: s.dma_start(
                        out=kraw[:], in_=g["k_hbm"][2 * hp + hh:2 * hp + hh + 1,
                                                    :, :].squeeze(0)),
                         reads=[buf("hbm_k_hbm")], writes=[buf(kraw.name)],
                         dma="dsy")
                    P.op("ve", lambda v: v.tensor_copy(k_hi[:], kraw[:]),
                         reads=[buf(kraw.name)], writes=[buf(k_hi.name)])
                    P.op("ve", lambda v: v.tensor_tensor(
                        k_lo[:], kraw[:], k_hi[:].bitcast(F32), op=OP.subtract),
                         reads=[buf(kraw.name), buf(k_hi.name)],
                         writes=[buf(k_lo.name)])
                    for kt in range(32):
                        ps = ps_S[kt % 2]
                        psb = buf(ps.name)

                        def emit(te, kh=kh, kl=kl, kt=kt, hh=hh, ps=ps):
                            return _mm3(
                                te, ps[:],
                                kh[:, kt * 128:(kt + 1) * 128],
                                kl[:, kt * 128:(kt + 1) * 128],
                                q_hi[:, hh * NSUPQ:(hh + 1) * NSUPQ],
                                q_lo[:, hh * NSUPQ:(hh + 1) * NSUPQ],
                                True, True)
                        P.op("te", emit,
                             reads=[buf(kh.name), buf(kl.name),
                                    buf(q_hi.name), buf(q_lo.name)],
                             writes=[psb])
                        pt = ptile[kt % 2]
                        P.op("sc", lambda s, ps=ps, pt=pt: s.activation(
                            pt[:], ps[:], AF.Exp),
                             reads=[psb], writes=[buf(pt.name)])
                        P.op("te", lambda te, kt=kt, hh=hh, pt=pt, ps_O=ps_O:
                             te.matmul(
                                 ps_O[:],
                                 vext[:, kt * 130 + hh * 65:
                                      kt * 130 + (hh + 1) * 65],
                                 pt[:], start=(kt == 0), stop=(kt == 31),
                                 skip_group_check=True),
                             reads=[buf(pt.name), buf(vext.name)],
                             writes=[buf(ps_O.name)])
                    P.op("ve", lambda v, ps_O=ps_O: v.reciprocal(
                        dnr[:], ps_O[64:65, :]),
                         reads=[buf(ps_O.name)], writes=[buf(dnr.name)])
                    P.op("te", lambda te: te.matmul(
                        ps_bc[:], cx.onescol[0:1, 0:64], dnr[:],
                        start=True, stop=True),
                         reads=[buf(dnr.name), buf(cx.onescol.name)],
                         writes=[buf(ps_bc.name)])
                    P.op("ve", lambda v: v.tensor_copy(dbc[:], ps_bc[:]),
                         reads=[buf(ps_bc.name)], writes=[buf(dbc.name)])
                    hidx = 2 * hp + hh
                    P.op("ve", lambda v, hidx=hidx, ps_O=ps_O: v.tensor_tensor(
                        o_heads[:, hidx * NSUPQ:(hidx + 1) * NSUPQ],
                        ps_O[0:64, :], dbc[:], op=OP.mult),
                         reads=[buf(ps_O.name), buf(dbc.name)],
                         writes=[buf(o_heads.name)])

            P.op("sy", lambda s, sup=sup: s.dma_start(
                out=x2own[:].rearrange("p (c t) -> p c t", c=C),
                in_=g["x2_hbm"][:].rearrange("p (c t) -> p c t", c=C)
                [:, :, sup * NSUPQ:(sup + 1) * NSUPQ]),
                 reads=[buf("hbm_x2")], writes=[buf(x2own.name)], dma="dsy")
            for co in range(C):
                ps = ps_o[co % 2]
                psb = buf(ps.name)
                wot = wo[co % 2]
                P.op("sy", lambda s, co=co, wot=wot: s.dma_start(
                    out=wot[:].rearrange("p (h n) -> p h n", h=H),
                    in_=g["l1_wo"][:].rearrange("(h p) n -> p h n", p=64)
                    [:, :, co * 128:(co + 1) * 128]),
                     reads=[buf("hbm_l1_wo")], writes=[buf(wot.name)], dma="dsy")

                def emit(te, co=co, ps=ps, wot=wot):
                    i = None
                    for hh in range(H):
                        i = te.matmul(ps[:],
                                      wot[:, hh * 128:(hh + 1) * 128],
                                      o_heads[:, hh * NSUPQ:(hh + 1) * NSUPQ],
                                      start=(hh == 0), stop=(hh == H - 1))
                    return i
                P.op("te", emit,
                     reads=[buf(wot.name), buf(o_heads.name)], writes=[psb])
                P.op("ve", lambda v, co=co, ps=ps: v.scalar_tensor_tensor(
                    x3t[:, co * NSUPQ:(co + 1) * NSUPQ], ps[:],
                    cx.bpp1[:, 3, co:co + 1],
                    x2own[:, co * NSUPQ:(co + 1) * NSUPQ],
                    op0=OP.add, op1=OP.add),
                     reads=[psb, buf(x2own.name), buf(cx.bpp1.name)],
                     writes=[buf(x3t.name)])
            P.op("gp", lambda gp, sup=sup: gp.dma_start(
                out=g["x3_hbm"][:].rearrange("p (c t) -> p c t", c=C)
                [:, :, sup * NSUPQ:(sup + 1) * NSUPQ],
                in_=x3t[:].rearrange("p (c t) -> p c t", c=C)),
                 reads=[buf(x3t.name)], writes=[buf("hbm_x3")], dma="dgp")
            cx.dbg_sl("dbg_x3", np.s_[:, sup * NSUPQ * C:(sup + 1) * NSUPQ * C],
                      x3t[:], [buf(x3t.name)])


def _phase4(cx):
    nc, P, g, buf = cx.nc, cx.P, cx.g, cx.buf
    import contextlib
    NK2 = FF1 // 128

    with contextlib.ExitStack() as ph:
        wsl = [_sb(nc, ph, f"p4_wsl{i}", [128, FF1]) for i in range(2)]
        acc = _sb(nc, ph, "p4_acc", [128, 8])
        scl = _sb(nc, ph, "p4_scl", [128, 4])
        rscl = _sb(nc, ph, "p4_rscl", [128, 4])
        junk = _sb(nc, ph, "p4_junk", [128, FF1])
        nint = _sb(nc, ph, "p4_ni", [128, FF1], I32)
        nf = _sb(nc, ph, "p4_nf", [128, FF1])
        dd = _sb(nc, ph, "p4_dd", [128, FF1])
        cc = _sb(nc, ph, "p4_cc", [128, FF1])
        wi8 = _sb(nc, ph, "p4_wi8", [128, FF1], BF16)
        ps_s = _ps(nc, ph, "p4_ps_s", [128, 8])

        P.op("ve", lambda v: v.memset(rscl[:], 0.0), writes=[buf(rscl.name)])
        P.op("ve", lambda v: v.memset(scl[:], 0.0), writes=[buf(scl.name)])
        specs = [("f1_wg", C, FF1, 0), ("f1_wu", C, FF1, 1), ("f1_wd", NK2, D, 2)]
        for wname, nslab, ww, col in specs:
            for n in range(nslab):
                w = wsl[n % 2]
                bw = buf(w.name)
                P.op("sy", lambda s, wname=wname, n=n, w=w, ww=ww: s.dma_start(
                    out=w[:, 0:ww], in_=g[wname][n * 128:(n + 1) * 128, :]),
                     reads=[buf("hbm_" + wname)], writes=[bw], dma="dsy")
                P.op("sc", lambda s, w=w, ww=ww, col=col, n=n: s.activation(
                    junk[:, 0:ww], w[:, 0:ww], AF.Abs,
                    accum_out=acc[:, 4 + col:5 + col]),
                     reads=[bw], writes=[buf(junk.name), buf(acc.name)])
                if n == 0:
                    P.op("ve", lambda v, col=col: v.tensor_copy(
                        acc[:, col:col + 1], acc[:, 4 + col:5 + col]),
                         reads=[buf(acc.name)], writes=[buf(acc.name)])
                else:
                    P.op("ve", lambda v, col=col: v.tensor_tensor(
                        acc[:, col:col + 1], acc[:, col:col + 1],
                        acc[:, 4 + col:5 + col], op=OP.add),
                         reads=[buf(acc.name)], writes=[buf(acc.name)])
        for wname, nslab, ww, col in specs:
            P.op("te", lambda te, col=col: te.matmul(
                ps_s[:, col:col + 1], cx.ones[:], acc[:, col:col + 1],
                start=True, stop=True),
                 reads=[buf(acc.name), buf(cx.ones.name)], writes=[buf(ps_s.name)])
            nel = float(nslab * 128 * ww)
            P.op("ve", lambda v, col=col, nel=nel: v.tensor_scalar(
                rscl[:, col:col + 1], ps_s[:, col:col + 1], 1.0 / nel, 1e-5,
                op0=OP.mult, op1=OP.max),
                 reads=[buf(ps_s.name)], writes=[buf(rscl.name)])
            P.op("ve", lambda v, col=col: v.reciprocal(
                scl[:, col:col + 1], rscl[:, col:col + 1]),
                 reads=[buf(rscl.name)], writes=[buf(scl.name)])
        for wname, nslab, ww, col in specs:
            for n in range(nslab):
                w = wsl[n % 2]
                bw = buf(w.name)
                P.op("sy", lambda s, wname=wname, n=n, w=w, ww=ww: s.dma_start(
                    out=w[:, 0:ww], in_=g[wname][n * 128:(n + 1) * 128, :]),
                     reads=[buf("hbm_" + wname)], writes=[bw], dma="dsy")
                P.op("ve", lambda v, w=w, ww=ww, col=col: v.tensor_scalar(
                    junk[:, 0:ww], w[:, 0:ww], scl[:, col:col + 1], None,
                    op0=OP.mult),
                     reads=[bw, buf(scl.name)], writes=[buf(junk.name)])
                P.op("sc", lambda s, ww=ww: s.activation(
                    nint[:, 0:ww], junk[:, 0:ww], AF.Copy),
                     reads=[buf(junk.name)], writes=[buf(nint.name)])
                P.op("sc", lambda s, ww=ww: s.activation(
                    nf[:, 0:ww], nint[:, 0:ww], AF.Copy),
                     reads=[buf(nint.name)], writes=[buf(nf.name)])
                P.op("ve", lambda v, ww=ww: v.tensor_tensor(
                    dd[:, 0:ww], junk[:, 0:ww], nf[:, 0:ww], op=OP.subtract),
                     reads=[buf(junk.name), buf(nf.name)], writes=[buf(dd.name)])
                P.op("ve", lambda v, ww=ww: v.tensor_scalar(
                    cc[:, 0:ww], dd[:, 0:ww], 0.5, None, op0=OP.is_gt),
                     reads=[buf(dd.name)], writes=[buf(cc.name)])
                P.op("ve", lambda v, ww=ww: v.scalar_tensor_tensor(
                    cc[:, 0:ww], dd[:, 0:ww], -0.5, cc[:, 0:ww],
                    op0=OP.is_lt, op1=OP.subtract),
                     reads=[buf(dd.name), buf(cc.name)], writes=[buf(cc.name)])
                P.op("ve", lambda v, ww=ww: v.tensor_tensor(
                    junk[:, 0:ww], nf[:, 0:ww], cc[:, 0:ww], op=OP.subtract),
                     reads=[buf(nf.name), buf(cc.name)], writes=[buf(junk.name)])
                P.op("ve", lambda v, ww=ww: v.tensor_scalar(
                    wi8[:, 0:ww], junk[:, 0:ww], 1.0, -1.0,
                    op0=OP.min, op1=OP.max),
                     reads=[buf(junk.name)], writes=[buf(wi8.name)])
                if col < 2:
                    P.op("gp", lambda gp, col=col, n=n: gp.dma_start(
                        out=g["wint_hbm"][n * 128:(n + 1) * 128,
                                          col * FF1:(col + 1) * FF1],
                        in_=wi8[:, 0:FF1]),
                         reads=[buf(wi8.name)], writes=[buf("hbm_wint")],
                         dma="dgp")
                else:
                    P.op("gp", lambda gp, n=n: gp.dma_start(
                        out=g["wdint_hbm"][n * 128:(n + 1) * 128, :],
                        in_=wi8[:, 0:D]),
                         reads=[buf(wi8.name)], writes=[buf("hbm_wdint")],
                         dma="dgp")
        P.op("gp", lambda gp: gp.dma_start(
            out=g["mr_hbm"][0:1, 0:4], in_=rscl[0:1, 0:4]),
             reads=[buf(rscl.name)], writes=[buf("hbm_mr")], dma="dgp")
    P.fence()

    NSUP = 256
    NTP = 2
    with contextlib.ExitStack() as ph:
        x3sup = _sb(nc, ph, "p4_x3s", [128, C * NSUP])
        xqaT_hi = _sb(nc, ph, "p4_xth", [128, NTP * C * 128], F32R)
        xqaT_lo = _sb(nc, ph, "p4_xtl", [128, NTP * C * 128], F32R)
        hsup = _sb(nc, ph, "p4_hs", [128, NTP * FF1])
        wsl_i8 = [_sb(nc, ph, "p4_wsli0", [128, C * 512], BF16)] * 2
        wsl_r = [_sb(nc, ph, "p4_wslr0", [128, C * 512], F32R)] * 2
        wdbf = [_sb(nc, ph, f"p4_wdbf{i}", [128, NK2 * 128], BF16) for i in range(2)]
        wdstage = _sb(nc, ph, "p4_wdst", [128, NK2 * 128], F32R)
        hqT_hi = _sb(nc, ph, "p4_hqh", [128, NK2 * NSUP], F32R)
        hqT_lo = _sb(nc, ph, "p4_hql", [128, NK2 * NSUP], F32R)
        rscl = _sb(nc, ph, "p4m_rscl", [128, 4])
        amax = _sb(nc, ph, "p4_am", [128, 4])
        u_s = _sb(nc, ph, "p4_u", [128, 512])
        nf_s = _sb(nc, ph, "p4_nfs", [128, 512])
        ni_s = _sb(nc, ph, "p4_nis", [128, 512], I32)
        cc_s = _sb(nc, ph, "p4_ccs", [128, 512])
        xn4 = _sb(nc, ph, "p4_xn4", [128, D])
        xqa = _sb(nc, ph, "p4_xqa", [128, D])
        junk = xn4
        ytile = _sb(nc, ph, "p4_y", [128, NTP * D])
        ps_x3 = _ps(nc, ph, "p4_ps_x3", [128, D])
        ps_tp = _ps(nc, ph, "p4_ps_t", [128, 512])
        ps_gu = [_ps(nc, ph, f"p4_ps_gu{i}", [128, 512]) for i in range(2)]
        ps_dn = [_ps(nc, ph, f"p4_ps_dn{i}", [128, NSUP]) for i in range(2)]

        P.op("sy", lambda s: s.dma_start(
            out=rscl[0:1, :], in_=g["mr_hbm"][0:1, 0:4]),
             reads=[buf("hbm_mr")], writes=[buf(rscl.name)], dma="dsy")
        P.op("te", lambda te: te.matmul(ps_x3[:, 0:4], cx.onescol[0:1, :],
                                        rscl[0:1, :], start=True, stop=True),
             reads=[buf(rscl.name), buf(cx.onescol.name)], writes=[buf(ps_x3.name)])
        P.op("ve", lambda v: v.tensor_copy(rscl[:], ps_x3[:, 0:4]),
             reads=[buf(ps_x3.name)], writes=[buf(rscl.name)])

        def act_quant(src_ap, dst_ap, Wn, bsrc, bdst, scol):
            P.op("ve", lambda v: v.tensor_scalar(
                u_s[:, 0:Wn], src_ap, amax[:, scol:scol + 1], None, op0=OP.mult),
                 reads=[bsrc, buf(amax.name)], writes=[buf(u_s.name)])
            P.op("sc", lambda s: s.activation(ni_s[:, 0:Wn], u_s[:, 0:Wn],
                                              AF.Copy),
                 reads=[buf(u_s.name)], writes=[buf(ni_s.name)])
            P.op("sc", lambda s: s.activation(nf_s[:, 0:Wn], ni_s[:, 0:Wn],
                                              AF.Copy),
                 reads=[buf(ni_s.name)], writes=[buf(nf_s.name)])
            P.op("ve", lambda v: v.tensor_tensor(
                u_s[:, 0:Wn], u_s[:, 0:Wn], nf_s[:, 0:Wn], op=OP.subtract),
                 reads=[buf(u_s.name), buf(nf_s.name)], writes=[buf(u_s.name)])
            P.op("ve", lambda v: v.tensor_scalar(
                cc_s[:, 0:Wn], u_s[:, 0:Wn], 0.5, None, op0=OP.is_gt),
                 reads=[buf(u_s.name)], writes=[buf(cc_s.name)])
            P.op("ve", lambda v: v.scalar_tensor_tensor(
                cc_s[:, 0:Wn], u_s[:, 0:Wn], -0.5, cc_s[:, 0:Wn],
                op0=OP.is_lt, op1=OP.subtract),
                 reads=[buf(u_s.name), buf(cc_s.name)], writes=[buf(cc_s.name)])
            P.op("ve", lambda v: v.tensor_tensor(
                u_s[:, 0:Wn], nf_s[:, 0:Wn], cc_s[:, 0:Wn], op=OP.subtract),
                 reads=[buf(nf_s.name), buf(cc_s.name)], writes=[buf(u_s.name)])
            P.op("ve", lambda v: v.tensor_scalar(
                dst_ap, u_s[:, 0:Wn], amax[:, scol + 1:scol + 2], None,
                op0=OP.mult),
                 reads=[buf(u_s.name), buf(amax.name)], writes=[bdst])

        for sup in range(T2 // NSUP):
            P.op("sy", lambda s, sup=sup: s.dma_start(
                out=x3sup[:].rearrange("p (c t) -> p c t", c=C),
                in_=g["x3_hbm"][:].rearrange("p (c t) -> p c t", c=C)
                [:, :, sup * NSUP:(sup + 1) * NSUP]),
                 reads=[buf("hbm_x3")], writes=[buf(x3sup.name)], dma="dsy")
            for tp in range(NTP):
                for ci in range(C):
                    P.op("te", lambda te, tp=tp, ci=ci: te.transpose(
                        ps_x3[:, ci * 128:(ci + 1) * 128],
                        x3sup[:, ci * NSUP + tp * 128:ci * NSUP + (tp + 1) * 128],
                        cx.idn[:]),
                         reads=[buf(x3sup.name), buf(cx.idn.name)],
                         writes=[buf(ps_x3.name)])
                P.op("ve", lambda v: v.tensor_reduce(amax[:, 0:1], ps_x3[:],
                                                     axis=AX.X, op=OP.add),
                     reads=[buf(ps_x3.name)], writes=[buf(amax.name)])
                P.op("sc", lambda s: s.activation(junk[:], ps_x3[:], AF.Square,
                                                  accum_out=amax[:, 1:2]),
                     reads=[buf(ps_x3.name)],
                     writes=[buf(junk.name), buf(amax.name)])
                P.op("ve", lambda v: v.tensor_scalar(
                    amax[:, 0:1], amax[:, 0:1], 1.0 / D, None, op0=OP.mult),
                     reads=[buf(amax.name)], writes=[buf(amax.name)])
                P.op("ve", lambda v: v.tensor_tensor(
                    amax[:, 2:3], amax[:, 0:1], amax[:, 0:1], op=OP.mult),
                     reads=[buf(amax.name)], writes=[buf(amax.name)])
                P.op("ve", lambda v: v.scalar_tensor_tensor(
                    amax[:, 2:3], amax[:, 1:2], 1.0 / D, amax[:, 2:3],
                    op0=OP.mult, op1=OP.subtract),
                     reads=[buf(amax.name)], writes=[buf(amax.name)])
                P.op("sc", lambda s: s.activation(amax[:, 2:3], amax[:, 2:3],
                                                  AF.Sqrt, bias=cx.eps_ap),
                     reads=[buf(amax.name), buf(cx.cpool.name)],
                     writes=[buf(amax.name)])
                P.op("ve", lambda v: v.reciprocal(amax[:, 2:3], amax[:, 2:3]),
                     reads=[buf(amax.name)], writes=[buf(amax.name)])
                P.op("ve", lambda v: v.tensor_scalar(
                    xn4[:], ps_x3[:], amax[:, 0:1], amax[:, 2:3],
                    op0=OP.subtract, op1=OP.mult),
                     reads=[buf(ps_x3.name), buf(amax.name)], writes=[buf(xn4.name)])
                P.op("ve", lambda v: v.tensor_reduce(
                    amax[:, 0:1], xn4[:], axis=AX.X, op=OP.max,
                    apply_absolute_value=True),
                     reads=[buf(xn4.name)], writes=[buf(amax.name)])
                P.op("ve", lambda v: v.tensor_scalar(
                    amax[:, 1:2], amax[:, 0:1], 1e-5, 1.0 / 127.0,
                    op0=OP.max, op1=OP.mult),
                     reads=[buf(amax.name)], writes=[buf(amax.name)])
                P.op("ve", lambda v: v.reciprocal(amax[:, 0:1], amax[:, 1:2]),
                     reads=[buf(amax.name)], writes=[buf(amax.name)])
                for half in range(2):
                    act_quant(xn4[:, half * 512:(half + 1) * 512],
                              xqa[:, half * 512:(half + 1) * 512], 512,
                              buf(xn4.name), buf(xqa.name), 0)
                for ci in range(C):
                    P.op("te", lambda te, ci=ci: te.transpose(
                        ps_tp[:, 0:128], xqa[:, ci * 128:(ci + 1) * 128],
                        cx.idn[:]),
                         reads=[buf(xqa.name), buf(cx.idn.name)],
                         writes=[buf(ps_tp.name)])
                    off = (tp * C + ci) * 128
                    P.op("ve", lambda v, off=off: v.tensor_copy(
                        xqaT_hi[:, off:off + 128], ps_tp[:, 0:128]),
                         reads=[buf(ps_tp.name)], writes=[buf(xqaT_hi.name)])
                    P.op("ve", lambda v, off=off: v.tensor_tensor(
                        xqaT_lo[:, off:off + 128], ps_tp[:, 0:128],
                        xqaT_hi[:, off:off + 128].bitcast(F32), op=OP.subtract),
                         reads=[buf(ps_tp.name), buf(xqaT_hi.name)],
                         writes=[buf(xqaT_lo.name)])

            for sl in range(FF1 // 512):
                wi = wsl_i8[sl % 2]
                wr = wsl_r[sl % 2]
                for which in range(2):
                    P.op("sy", lambda s, sl=sl, which=which, wi=wi: s.dma_start(
                        out=wi[:].rearrange("p (c n) -> p c n", c=C),
                        in_=g["wint_hbm"][:, which * FF1 + sl * 512:
                                          which * FF1 + (sl + 1) * 512]
                        .rearrange("(c p) n -> p c n", p=128)),
                         reads=[buf("hbm_wint")], writes=[buf(wi.name)], dma="dsy")
                    P.op("ve", lambda v, wi=wi, wr=wr: v.tensor_copy(wr[:], wi[:]),
                         reads=[buf(wi.name)], writes=[buf(wr.name)])
                    for tp in range(NTP):
                        ps = ps_gu[tp % 2]

                        def emit(te, tp=tp, wr=wr, ps=ps):
                            i = None
                            for ci in range(C):
                                off = (tp * C + ci) * 128
                                i = te.matmul(ps[:], xqaT_hi[:, off:off + 128],
                                              wr[:, ci * 512:(ci + 1) * 512],
                                              start=(ci == 0), stop=False)
                                i = te.matmul(ps[:], xqaT_lo[:, off:off + 128],
                                              wr[:, ci * 512:(ci + 1) * 512],
                                              start=False, stop=(ci == C - 1))
                            return i
                        P.op("te", emit,
                             reads=[buf(xqaT_hi.name), buf(xqaT_lo.name),
                                    buf(wr.name)],
                             writes=[buf(ps.name)])
                        dst = hsup[:, tp * FF1 + sl * 512: tp * FF1 + (sl + 1) * 512]
                        if which == 0:
                            P.op("sc", lambda s, ps=ps, dst=dst: s.activation(
                                dst, ps[:], AF.Silu, scale=rscl[:, 0:1]),
                                 reads=[buf(ps.name), buf(rscl.name)],
                                 writes=[buf(hsup.name)])
                        else:
                            P.op("ve", lambda v, ps=ps, dst=dst:
                                 v.scalar_tensor_tensor(
                                     dst, ps[:], rscl[:, 1:2], dst,
                                     op0=OP.mult, op1=OP.mult),
                                 reads=[buf(ps.name), buf(hsup.name),
                                        buf(rscl.name)],
                                 writes=[buf(hsup.name)])

            for tp in range(NTP):
                hrow = hsup[:, tp * FF1:(tp + 1) * FF1]
                P.op("ve", lambda v, hrow=hrow: v.tensor_reduce(
                    amax[:, 0:1], hrow, axis=AX.X, op=OP.max,
                    apply_absolute_value=True),
                     reads=[buf(hsup.name)], writes=[buf(amax.name)])
                P.op("ve", lambda v: v.tensor_scalar(
                    amax[:, 1:2], amax[:, 0:1], 1e-5, 1.0 / 127.0,
                    op0=OP.max, op1=OP.mult),
                     reads=[buf(amax.name)], writes=[buf(amax.name)])
                P.op("ve", lambda v: v.reciprocal(amax[:, 0:1], amax[:, 1:2]),
                     reads=[buf(amax.name)], writes=[buf(amax.name)])
                for q8 in range(FF1 // 512):
                    act_quant(hrow[:, q8 * 512:(q8 + 1) * 512],
                              hrow[:, q8 * 512:(q8 + 1) * 512], 512,
                              buf(hsup.name), buf(hsup.name), 0)
                for k2 in range(NK2):
                    P.op("te", lambda te, k2=k2, hrow=hrow: te.transpose(
                        ps_tp[:, 0:128], hrow[:, k2 * 128:(k2 + 1) * 128],
                        cx.idn[:]),
                         reads=[buf(hsup.name), buf(cx.idn.name)],
                         writes=[buf(ps_tp.name)])
                    off = k2 * NSUP + tp * 128
                    P.op("ve", lambda v, off=off: v.tensor_copy(
                        hqT_hi[:, off:off + 128], ps_tp[:, 0:128]),
                         reads=[buf(ps_tp.name)], writes=[buf(hqT_hi.name)])
                    P.op("ve", lambda v, off=off: v.tensor_tensor(
                        hqT_lo[:, off:off + 128], ps_tp[:, 0:128],
                        hqT_hi[:, off:off + 128].bitcast(F32), op=OP.subtract),
                         reads=[buf(ps_tp.name), buf(hqT_hi.name)],
                         writes=[buf(hqT_lo.name)])
            for co in range(C):
                wdb = wdbf[co % 2]
                P.op("sy", lambda s, co=co, wdb=wdb: s.dma_start(
                    out=wdb[:].rearrange("p (k n) -> p k n", k=NK2),
                    in_=g["wdint_hbm"][:, co * 128:(co + 1) * 128]
                    .rearrange("(k p) n -> p k n", p=128)),
                     reads=[buf("hbm_wdint")], writes=[buf(wdb.name)], dma="dsy")
                P.op("ve", lambda v, wdb=wdb: v.tensor_copy(wdstage[:], wdb[:]),
                     reads=[buf(wdb.name)], writes=[buf(wdstage.name)])
                ps = ps_dn[co % 2]

                def emit(te, co=co, ps=ps):
                    i = None
                    for k2 in range(NK2):
                        st = wdstage[:, k2 * 128:(k2 + 1) * 128]
                        i = te.matmul(ps[:], st, hqT_hi[:, k2 * NSUP:(k2 + 1) * NSUP],
                                      start=(k2 == 0), stop=False)
                        i = te.matmul(ps[:], st, hqT_lo[:, k2 * NSUP:(k2 + 1) * NSUP],
                                      start=False, stop=(k2 == NK2 - 1))
                    return i
                P.op("te", emit,
                     reads=[buf(wdstage.name), buf(hqT_hi.name),
                            buf(hqT_lo.name)],
                     writes=[buf(ps.name)])
                P.op("ve", lambda v, co=co, ps=ps: v.scalar_tensor_tensor(
                    xn4[:, 0:NSUP], ps[:], rscl[:, 2:3],
                    x3sup[:, co * NSUP:(co + 1) * NSUP],
                    op0=OP.mult, op1=OP.add),
                     reads=[buf(ps.name), buf(x3sup.name), buf(rscl.name)],
                     writes=[buf(xn4.name)])
                for tp in range(NTP):
                    P.op("te", lambda te, tp=tp: te.transpose(
                        ps_tp[:, 128:256], xn4[:, tp * 128:(tp + 1) * 128],
                        cx.idn[:]),
                         reads=[buf(xn4.name), buf(cx.idn.name)],
                         writes=[buf(ps_tp.name)])
                    P.op("ve", lambda v, tp=tp, co=co: v.tensor_copy(
                        ytile[:, tp * D + co * 128:tp * D + (co + 1) * 128],
                        ps_tp[:, 128:256]),
                         reads=[buf(ps_tp.name)], writes=[buf(ytile.name)])
            P.op("gp", lambda gp, sup=sup: gp.dma_start(
                out=g["y_tm"][sup * NSUP:(sup + 1) * NSUP, :]
                .rearrange("(tp p) n -> p tp n", p=128),
                in_=ytile[:].rearrange("p (tp n) -> p tp n", tp=NTP)),
                 reads=[buf(ytile.name)], writes=[buf("hbm_y")], dma="dgp")



def kernel(**inputs):
    in_maps = prepare_inputs(inputs)
    stage = _DEBUG.get("stage")
    nc = build_program(stage)
    res = run_bass_kernel_spmd(nc, in_maps, core_ids=list(range(NCORES)))
    if stage is not None:
        return res
    out = np.zeros((4, T4, D), np.float32)
    for c in range(NCORES):
        b, h = c // 2, c % 2
        out[b, h * T2:(h + 1) * T2, :] = res.results[c]["y_tm"]
    return out


# version 06
# speedup vs baseline: 1.0037x; 1.0037x over previous
import numpy as np

import concourse.bass as bass
import concourse.mybir as mybir
from concourse.alu_op_type import AluOpType
from concourse.bass_utils import run_bass_kernel_spmd

F32 = mybir.dt.float32
F32R = mybir.dt.float32r
I32 = mybir.dt.int32
I8 = mybir.dt.int8
BF16 = mybir.dt.bfloat16
AF = mybir.ActivationFunctionType
AX = mybir.AxisListType
OP = AluOpType

D = 1024
C = 8
H = 16
DH = 64
T4 = 4096
T2 = 2048
FF0 = 2816
FF1 = 4096
NCORES = 8

_DEBUG = {}



class Buf:
    __slots__ = ("w", "r", "tag")

    def __init__(self, tag=""):
        self.w = None
        self.r = []
        self.tag = tag


class Prog:
    ENG = {"te": "tensor", "ve": "vector", "sc": "scalar",
           "gp": "gpsimd", "sy": "sync"}

    NQ = 8

    def __init__(self, nc, block, sems):
        self.nc = nc
        self.block = block
        self.sem = sems
        self.cnt = {k: 0 for k in sems}
        self.wm = {}
        self.eng_sem = {"te": "te", "ve": "ve", "sc": "sc", "gp": "gp"}
        self.qcnt = {"dsy": 0, "dgp": 0}

    def op(self, eng, emit, reads=(), writes=(), dma=None):
        need = {}

        def add(h):
            if h is not None:
                s, v = h
                if need.get(s, 0) < v:
                    need[s] = v

        for b in reads:
            add(b.w)
        for b in writes:
            add(b.w)
            for h in b.r:
                add(h)
        if dma is not None:
            i = self.qcnt[dma]
            self.qcnt[dma] = i + 1
            semname = f"{dma}{i % self.NQ}"
            inc = 16
            if i >= self.NQ:
                add((semname, 16 * (i // self.NQ)))
        else:
            semname, inc = self.eng_sem[eng], 1
        waits = [(s, v) for s, v in need.items()
                 if self.wm.get((eng, s), 0) < v]
        semh = self.sem[semname]

        def body(proxy):
            for s, v in waits:
                proxy.wait_ge(self.sem[s], v)
            inst = emit(proxy)
            if inst is not None:
                inst.then_inc(semh, inc)

        getattr(self.block, self.ENG[eng])(body)
        for s, v in waits:
            self.wm[(eng, s)] = v
        self.cnt[semname] += inc
        h = (semname, self.cnt[semname])
        for b in reads:
            b.r.append(h)
        for b in writes:
            b.w = h
            b.r = []
        return h

    def fence(self):
        vals = [(s, v) for s, v in self.cnt.items() if v > 0]
        for eng in self.ENG:
            waits = [(s, v) for s, v in vals if self.wm.get((eng, s), 0) < v]
            if not waits:
                continue

            def body(proxy, waits=waits):
                for s, v in waits:
                    proxy.wait_ge(self.sem[s], v)

            getattr(self.block, self.ENG[eng])(body)
            for s, v in waits:
                self.wm[(eng, s)] = v

    def final_barrier(self, eng="gp"):
        vals = [(s, v) for s, v in self.cnt.items()
                if v and (s.startswith("dgp") or s.startswith("dsy"))]

        def body(proxy):
            for s, v in vals:
                proxy.wait_ge(self.sem[s], v)

        getattr(self.block, self.ENG[eng])(body)



def _rope_tables_tm(pos, scale):
    inv = 1.0 / (10000.0 ** (np.arange(0, DH, 2, dtype=np.float64) / DH))
    ang = np.asarray(pos, np.float64)[:, None] * inv[None, :]
    c = np.cos(ang) * scale
    s = np.sin(ang) * scale
    C64 = np.concatenate([c, c], 1)
    S64 = np.concatenate([-s, s], 1)
    out = []
    for t in (C64, S64):
        hi = t.astype(np.float32)
        lo = (t - hi.astype(np.float64)).astype(np.float32)
        out += [hi, lo]
    return np.concatenate(out, 1).astype(np.float32)


def _perm_rope_cols(w_fm, b):
    perm = np.concatenate([np.arange(0, DH, 2), np.arange(1, DH, 2)])
    full = np.concatenate([h * DH + perm for h in range(H)])
    return np.ascontiguousarray(w_fm[:, full]), np.ascontiguousarray(b[full])


def _bias_pp(biases):
    nb = len(biases)
    out = np.zeros((128, nb, C), np.float32)
    for i, b in enumerate(biases):
        out[:, i, :] = b.reshape(C, 128).T
    return out


def prepare_inputs(inp):
    f32 = np.float32
    x = np.ascontiguousarray(np.asarray(inp["L0_attn_x"], f32))

    common = {}
    for nm, key in [("l0_wq", "L0_attn_Wq"), ("l0_wk", "L0_attn_Wk"),
                    ("l0_wv", "L0_attn_Wv"), ("l0_wo", "L0_attn_Wo")]:
        common[nm] = np.ascontiguousarray(np.asarray(inp[key], f32))
    l0b = [np.asarray(inp[k], f32) for k in
           ("L0_attn_bq", "L0_attn_bk", "L0_attn_bv", "L0_attn_bo")]
    common["l0_bpp"] = _bias_pp(l0b)
    common["l0_brow"] = np.concatenate(l0b).reshape(1, 4 * D).astype(f32)

    common["f0_wg"] = np.ascontiguousarray(np.asarray(inp["L0_ffn_w_gate"], f32))
    common["f0_wu"] = np.ascontiguousarray(np.asarray(inp["L0_ffn_w_up"], f32))
    common["f0_wd"] = np.ascontiguousarray(np.asarray(inp["L0_ffn_w_down"], f32))

    wq1, bq1 = _perm_rope_cols(np.asarray(inp["L1_attn_Wq"], f32).T,
                               np.asarray(inp["L1_attn_bq"], f32))
    wk1, bk1 = _perm_rope_cols(np.asarray(inp["L1_attn_Wk"], f32).T,
                               np.asarray(inp["L1_attn_bk"], f32))
    common["l1_wq"] = wq1
    common["l1_wk"] = wk1
    common["l1_wv"] = np.ascontiguousarray(np.asarray(inp["L1_attn_Wv"], f32).T)
    common["l1_wo"] = np.ascontiguousarray(np.asarray(inp["L1_attn_Wo"], f32).T)
    bv1 = np.asarray(inp["L1_attn_bv"], f32)
    bo1 = np.asarray(inp["L1_attn_bo"], f32)
    common["l1_bpp"] = _bias_pp([bq1, bk1, bv1, bo1])
    common["l1_brow"] = np.concatenate([bq1, bk1, bv1, bo1]).reshape(1, 4 * D).astype(f32)

    common["f1_wg"] = np.ascontiguousarray(np.asarray(inp["L1_ffn_w_gate"], f32).T)
    common["f1_wu"] = np.ascontiguousarray(np.asarray(inp["L1_ffn_w_up"], f32).T)
    common["f1_wd"] = np.ascontiguousarray(np.asarray(inp["L1_ffn_w_down"], f32).T)

    common["idn"] = np.eye(128, dtype=f32)
    common["ones128"] = np.ones((128, 128), f32)
    common["onescol"] = np.ones((1, 128), f32)
    cp = np.zeros((128, 8), f32)
    cp[:, 0] = 1e-5
    cp[:, 1] = 1.0 / 1024
    common["cpool"] = cp
    common["signmask"] = np.full((128, 1), -2147483648, np.int32)

    in_maps = []
    for c in range(NCORES):
        b, h = c // 2, c % 2
        m = dict(common)
        xb = x[b]
        if h == 1:
            xb = np.concatenate([xb[T2:], xb[:T2]], 0)
        m["x_fm"] = np.ascontiguousarray(xb.T)
        pos = np.arange(T4)
        if h == 1:
            pos = np.concatenate([pos[T2:], pos[:T2]])
        m["ropek"] = np.ascontiguousarray(_rope_tables_tm(pos, 1.0))
        m["ropeq"] = np.ascontiguousarray(_rope_tables_tm(pos[:T2], 0.125))
        in_maps.append(m)
    return in_maps



_uid = [0]


def _nm(base):
    _uid[0] += 1
    return f"{base}_{_uid[0]}"


class Ctx:
    pass


def build_program(stage=None):
    nc = bass.Bass("TRN2", target_bir_lowering=False)
    g = {}

    def din(name, shape, dt=F32):
        g[name] = nc.dram_tensor(name, list(shape), dt, kind="ExternalInput").ap()

    def dout(name, shape, dt=F32):
        g[name] = nc.dram_tensor(name, list(shape), dt, kind="ExternalOutput").ap()

    def dscr(name, shape, dt=F32):
        g[name] = nc.dram_tensor(name, list(shape), dt).ap()

    din("x_fm", [D, T4])
    for nm in ("l0_wq", "l0_wk", "l0_wv", "l0_wo",
               "l1_wq", "l1_wk", "l1_wv", "l1_wo"):
        din(nm, [D, D])
    din("l0_bpp", [128, 4, C]); din("l1_bpp", [128, 4, C])
    din("l0_brow", [1, 4 * D]); din("l1_brow", [1, 4 * D])
    din("f0_wg", [FF0, D]); din("f0_wu", [FF0, D]); din("f0_wd", [D, FF0])
    din("f1_wg", [D, FF1]); din("f1_wu", [D, FF1]); din("f1_wd", [FF1, D])
    din("idn", [128, 128]); din("ones128", [128, 128]); din("onescol", [1, 128])
    din("cpool", [128, 8]); din("signmask", [128, 1], I32)
    din("ropek", [T4, 256]); din("ropeq", [T2, 256])

    dout("y_tm", [T2, D])
    for nm, shape in _DEBUG.get("outs", {}).items():
        dout(nm, shape)

    dscr("x1_hbm", [128, C * T4])
    dscr("mr_hbm", [2, T4])
    dscr("x2_hbm", [128, C * T4])
    dscr("x3_hbm", [128, C * T2])
    dscr("k_hbm", [H, DH, T4])
    dscr("q_hbm", [H, DH, T2])
    dscr("v_hbm", [C, T4, 128])
    dscr("wq_fm_hbm", [128, C * 2 * FF0])
    dscr("wd_fm_hbm", [128, (FF0 // 128) * D])
    dscr("wint_hbm", [D, 2 * FF1], BF16)
    dscr("wdint_hbm", [FF1, D], BF16)

    sems = {}
    ctxs = []
    names = ["te", "ve", "sc", "gp"]
    names += [f"dsy{i}" for i in range(Prog.NQ)]
    names += [f"dgp{i}" for i in range(Prog.NQ)]
    for s in names:
        cm = nc.semaphore("sem_" + s)
        sems[s] = cm.__enter__()
        ctxs.append(cm)

    blk_cm = nc.Block()
    block = blk_cm.__enter__()
    P = Prog(nc, block, sems)

    import contextlib
    with contextlib.ExitStack() as stack:
        _build_phases(nc, P, g, stack, stage)
        P.final_barrier()
    blk_cm.__exit__(None, None, None)
    for cm in reversed(ctxs):
        cm.__exit__(None, None, None)
    return nc


def _sb(nc, stack, name, shape, dt=F32):
    return stack.enter_context(nc.sbuf_tensor(name, list(shape), dt))


def _ps(nc, stack, name, shape, dt=F32):
    return stack.enter_context(nc.psum_tensor(name, list(shape), dt))


def _build_phases(nc, P, g, stack, stage):
    cx = Ctx()
    cx.nc, cx.P, cx.g, cx.stack = nc, P, g, stack
    B = cx.B = {}

    def buf(name):
        if name not in B:
            B[name] = Buf(name)
        return B[name]
    cx.buf = buf

    cx.idn = _sb(nc, stack, "c_idn", [128, 128])
    cx.ones = _sb(nc, stack, "c_ones", [128, 128])
    cx.ones_r = _sb(nc, stack, "c_ones_r", [128, 128], F32R)
    cx.onescol = _sb(nc, stack, "c_onescol", [1, 128])
    cx.cpool = _sb(nc, stack, "c_cpool", [128, 8])
    cx.signmask = _sb(nc, stack, "c_sgn", [128, 1], I32)
    cx.bpp0 = _sb(nc, stack, "c_bpp0", [128, 4, C])
    cx.bpp1 = _sb(nc, stack, "c_bpp1", [128, 4, C])
    for t, src in [(cx.idn, "idn"), (cx.ones, "ones128"), (cx.onescol, "onescol"),
                   (cx.cpool, "cpool"), (cx.signmask, "signmask"),
                   (cx.bpp0, "l0_bpp"), (cx.bpp1, "l1_bpp")]:
        P.op("sy", lambda s, t=t, src=src: s.dma_start(out=t[:], in_=g[src][:]),
             reads=[buf("hbm_" + src)], writes=[buf(t.name)], dma="dsy")
    P.op("ve", lambda v: v.tensor_copy(cx.ones_r[:], cx.ones[:]),
         reads=[buf(cx.ones.name)], writes=[buf(cx.ones_r.name)])

    cx.recip6 = _sb(nc, stack, "c_recip6", [128, 1])
    P.op("ve", lambda v: v.memset(cx.recip6[:], 6.0), writes=[buf("c_recip6")])
    P.op("ve", lambda v: v.reciprocal(cx.recip6[:], cx.recip6[:]),
         reads=[buf("c_recip6")], writes=[buf("c_recip6")])

    cx.eps_ap = cx.cpool[:, 0:1]

    def split(src_ap, hi_t, lo_t, rbuf, hi_ap=None, lo_ap=None):
        hi_ap = hi_ap if hi_ap is not None else hi_t[:]
        lo_ap = lo_ap if lo_ap is not None else lo_t[:]
        hb, lb = buf(hi_t.name), buf(lo_t.name)
        P.op("ve", lambda v: v.tensor_copy(hi_ap, src_ap),
             reads=[rbuf], writes=[hb])
        P.op("ve", lambda v: v.tensor_tensor(lo_ap, src_ap, hi_ap.bitcast(F32),
                                             op=OP.subtract),
             reads=[rbuf, hb], writes=[lb])
    cx.split = split

    def dbg(name, src_ap, reads):
        if name in _DEBUG.get("outs", {}):
            P.op("gp", lambda gp: gp.dma_start(out=g[name][:], in_=src_ap),
                 reads=reads, writes=[buf("dbg_" + name)], dma="dgp")
    cx.dbg = dbg

    def dbg_sl(name, dst_sl, src_ap, reads):
        if name in _DEBUG.get("outs", {}):
            P.op("gp", lambda gp: gp.dma_start(out=g[name][dst_sl], in_=src_ap),
                 reads=reads, writes=[buf("dbg_" + name)], dma="dgp")
    cx.dbg_sl = dbg_sl

    _phase1(cx)
    P.fence()
    if stage == "p1":
        return
    _phase2(cx)
    P.fence()
    if stage == "p2":
        return
    _phase3(cx)
    P.fence()
    if stage == "p3":
        return
    _phase4(cx)



def _fm_chunk(t, c, N):
    return t[:, c * N:(c + 1) * N]


def _ln_fm(cx, xtile, x_hi, x_lo, sq, sq_hi, sq_lo, ps_stats, m_t, r_t, NT,
           cache=None, it=0):
    nc, P, buf, split = cx.nc, cx.P, cx.buf, cx.split
    bx = buf(xtile.name)
    split(xtile[:], x_hi, x_lo, bx)
    P.op("sc", lambda s: s.activation(sq[:], xtile[:], AF.Square),
         reads=[bx], writes=[buf(sq.name)])
    split(sq[:], sq_hi, sq_lo, buf(sq.name))
    psb = buf(ps_stats.name)

    def mm_sum(hi, lo, col):
        def e(te):
            i = None
            for cch in range(C):
                i = te.matmul(ps_stats[:, col:col + NT], cx.ones_r[:],
                              _fm_chunk(hi, cch, NT), start=(cch == 0), stop=False)
                i = te.matmul(ps_stats[:, col:col + NT], cx.ones_r[:],
                              _fm_chunk(lo, cch, NT), start=False,
                              stop=(cch == C - 1))
            return i
        return e
    P.op("te", mm_sum(x_hi, x_lo, 0),
         reads=[buf(x_hi.name), buf(x_lo.name), buf(cx.ones_r.name)], writes=[psb])
    P.op("te", mm_sum(sq_hi, sq_lo, NT),
         reads=[buf(sq_hi.name), buf(sq_lo.name), buf(cx.ones_r.name)], writes=[psb])
    bm, br = buf(m_t.name), buf(r_t.name)
    P.op("ve", lambda v: v.tensor_scalar(m_t[:], ps_stats[:, 0:NT], 1.0 / D, None,
                                         op0=OP.mult), reads=[psb], writes=[bm])
    P.op("ve", lambda v: v.tensor_tensor(r_t[:], m_t[:], m_t[:], op=OP.mult),
         reads=[bm], writes=[br])
    P.op("ve", lambda v: v.scalar_tensor_tensor(r_t[:], ps_stats[:, NT:2 * NT],
                                                1.0 / D, r_t[:],
                                                op0=OP.mult, op1=OP.subtract),
         reads=[psb, br], writes=[br])
    P.op("sc", lambda s: s.activation(r_t[:], r_t[:], AF.Sqrt, bias=cx.eps_ap),
         reads=[br, buf(cx.cpool.name)], writes=[br])
    P.op("ve", lambda v: v.reciprocal(r_t[:], r_t[:]), reads=[br], writes=[br])
    if cache is not None:
        m_cache, r_cache = cache
        P.op("ve", lambda v: v.tensor_copy(m_cache[:, it * NT:(it + 1) * NT],
                                           m_t[0:1, :]),
             reads=[bm], writes=[buf(m_cache.name)])
        P.op("ve", lambda v: v.tensor_copy(r_cache[:, it * NT:(it + 1) * NT],
                                           r_t[0:1, :]),
             reads=[br], writes=[buf(r_cache.name)])


def _apply_ln_fm(cx, xtile, m_t, r_t, xn, NT, m_b=None, r_b=None, m_ap=None,
                 r_ap=None):
    P, buf = cx.P, cx.buf
    m_ap = m_ap if m_ap is not None else m_t[:]
    r_ap = r_ap if r_ap is not None else r_t[:]
    m_b = m_b if m_b is not None else buf(m_t.name)
    r_b = r_b if r_b is not None else buf(r_t.name)
    x3 = xtile[:].rearrange("p (c t) -> p c t", c=C)
    o3 = xn[:].rearrange("p (c t) -> p c t", c=C)
    mb = m_ap.unsqueeze(1).broadcast_to((128, C, NT))
    rb = r_ap.unsqueeze(1).broadcast_to((128, C, NT))
    P.op("ve", lambda v: v.tensor_tensor(o3, x3, mb, op=OP.subtract),
         reads=[buf(xtile.name), m_b], writes=[buf(xn.name)])
    P.op("ve", lambda v: v.tensor_tensor(o3, o3, rb, op=OP.mult),
         reads=[buf(xn.name), r_b], writes=[buf(xn.name)])


def _load_w(cx, ph, wname, tag):
    nc, P, g, buf = cx.nc, cx.P, cx.g, cx.buf
    w = _sb(nc, ph, tag, [128, C * D])
    src = g[wname][:].rearrange("(c p) n -> p c n", p=128)
    dst = w[:].rearrange("p (c n) -> p c n", c=C)
    P.op("sy", lambda s: s.dma_start(out=dst, in_=src),
         reads=[buf("hbm_" + wname)], writes=[buf(w.name)], dma="dsy")
    return w



def _phase1(cx):
    nc, P, g, buf, split = cx.nc, cx.P, cx.g, cx.buf, cx.split
    NT = 512
    NTILE = T4 // NT
    import contextlib
    with contextlib.ExitStack() as ph:
        wA = _load_w(cx, ph, "l0_wk", "p1_wA")
        wB = _load_w(cx, ph, "l0_wv", "p1_wB")

        xt = [_sb(nc, ph, f"p1_x{i}", [128, C * NT]) for i in range(2)]
        S0 = _sb(nc, ph, "p1_S0", [128, C * NT])
        S1 = _sb(nc, ph, "p1_S1", [128, C * NT])
        S2 = _sb(nc, ph, "p1_S2", [128, C * NT])
        SHt = _sb(nc, ph, "p1_SH", [128, C * NT], F32R)
        SLt = _sb(nc, ph, "p1_SL", [128, C * NT], F32R)
        SH = SHt[:]
        SL = SLt[:]
        m_t = _sb(nc, ph, "p1_m", [128, NT])
        r_t = _sb(nc, ph, "p1_r", [128, NT])
        m_row = _sb(nc, ph, "p1_mrow", [1, NT])
        r_row = _sb(nc, ph, "p1_rrow", [1, NT])
        bkv_bc = _sb(nc, ph, "p1_bkv", [128, 2 * D])
        kvd = _sb(nc, ph, "p1_kvd", [128, C * 128])

        ps_stats = _ps(nc, ph, "p1_ps_st", [128, 2 * NT])
        ps_proj = [_ps(nc, ph, f"p1_ps_p{i}", [128, NT]) for i in range(2)]
        ps_kv = [_ps(nc, ph, f"p1_ps_kv{i}", [128, 512]) for i in range(4)]

        ph_brow = contextlib.ExitStack()
        brow0 = _sb(nc, ph_brow, "p1_brow0", [1, D])
        for i, off in enumerate((D, 2 * D)):
            P.op("sy", lambda s, off=off: s.dma_start(
                out=brow0[:], in_=g["l0_brow"][0:1, off:off + D]),
                 reads=[buf("hbm_l0_brow")], writes=[buf(brow0.name)], dma="dsy")
            for half in range(2):
                P.op("te", lambda te, half=half: te.matmul(
                    ps_stats[:, 0:512], cx.onescol[0:1, :],
                    brow0[0:1, half * 512:(half + 1) * 512],
                    start=True, stop=True),
                     reads=[buf(cx.onescol.name), buf(brow0.name)],
                     writes=[buf(ps_stats.name)])
                P.op("ve", lambda v, i=i, half=half: v.tensor_copy(
                    bkv_bc[:, i * D + half * 512: i * D + (half + 1) * 512],
                    ps_stats[:, 0:512]),
                     reads=[buf(ps_stats.name)], writes=[buf(bkv_bc.name)])
        ph_brow.close()

        def hl_split(src_ap, sbuf):
            P.op("ve", lambda v: v.tensor_copy(SH, src_ap),
                 reads=[sbuf], writes=[buf(SHt.name)])
            P.op("ve", lambda v: v.tensor_tensor(SL, src_ap, SH.bitcast(F32),
                                                 op=OP.subtract),
                 reads=[sbuf, buf(SHt.name)], writes=[buf(SLt.name)])

        def fmchunk_ap(ap, cch):
            return ap[:, cch * NT:(cch + 1) * NT]

        def ln_stats(xtile, it, cache=True):
            bx = buf(xtile.name)
            psb = buf(ps_stats.name)

            def mm_sum(col):
                def e(te):
                    i = None
                    for cch in range(C):
                        i = te.matmul(ps_stats[:, col:col + NT], cx.ones_r[:],
                                      fmchunk_ap(SH, cch), start=(cch == 0),
                                      stop=False)
                        i = te.matmul(ps_stats[:, col:col + NT], cx.ones_r[:],
                                      fmchunk_ap(SL, cch), start=False,
                                      stop=(cch == C - 1))
                    return i
                return e
            hl_split(xtile[:], bx)
            P.op("te", mm_sum(0),
                 reads=[buf(SHt.name), buf(SLt.name), buf(cx.ones_r.name)],
                 writes=[psb])
            P.op("sc", lambda s: s.activation(S2[:], xtile[:], AF.Square),
                 reads=[bx], writes=[buf(S2.name)])
            hl_split(S2[:], buf(S2.name))
            P.op("te", mm_sum(NT),
                 reads=[buf(SHt.name), buf(SLt.name), buf(cx.ones_r.name)],
                 writes=[psb])
            bm, br = buf(m_t.name), buf(r_t.name)
            P.op("ve", lambda v: v.tensor_scalar(m_t[:], ps_stats[:, 0:NT],
                                                 1.0 / D, None, op0=OP.mult),
                 reads=[psb], writes=[bm])
            P.op("ve", lambda v: v.tensor_tensor(r_t[:], m_t[:], m_t[:],
                                                 op=OP.mult),
                 reads=[bm], writes=[br])
            P.op("ve", lambda v: v.scalar_tensor_tensor(
                r_t[:], ps_stats[:, NT:2 * NT], 1.0 / D, r_t[:],
                op0=OP.mult, op1=OP.subtract),
                 reads=[psb, br], writes=[br])
            P.op("sc", lambda s: s.activation(r_t[:], r_t[:], AF.Sqrt,
                                              bias=cx.eps_ap),
                 reads=[br, buf(cx.cpool.name)], writes=[br])
            P.op("ve", lambda v: v.reciprocal(r_t[:], r_t[:]),
                 reads=[br], writes=[br])
            if cache:
                P.op("gp", lambda gp, it=it: gp.dma_start(
                    out=g["mr_hbm"][0:1, it * NT:(it + 1) * NT], in_=m_t[0:1, :]),
                     reads=[bm], writes=[buf("hbm_mr")], dma="dgp")
                P.op("gp", lambda gp, it=it: gp.dma_start(
                    out=g["mr_hbm"][1:2, it * NT:(it + 1) * NT], in_=r_t[0:1, :]),
                     reads=[br], writes=[buf("hbm_mr")], dma="dgp")

        def load_x(it, src_name="x_fm"):
            xtile = xt[it % 2]
            bx = buf(xtile.name)
            if src_name == "x_fm":
                src = g["x_fm"][:].rearrange("(c p) t -> p c t", p=128)
            else:
                src = g[src_name][:].rearrange("p (c t) -> p c t", c=C)
            P.op("sy", lambda s, it=it, xtile=xtile, src=src: s.dma_start(
                out=xtile[:].rearrange("p (c t) -> p c t", c=C),
                in_=src[:, :, it * NT:(it + 1) * NT]),
                 reads=[buf("hbm_" + src_name)], writes=[bx], dma="dsy")
            return xtile, bx

        for it in range(NTILE):
            xtile, bx = load_x(it)
            ln_stats(xtile, it)
            xn = S2
            _apply_ln_fm(cx, xtile, m_t, r_t, xn, NT)

            ktile, vtile = S0, S1
            for which, w, otile in [(0, wA, ktile), (1, wB, vtile)]:
                for tp in range(4):
                    for dh2 in range(2):
                        ps = ps_proj[(tp * 2 + dh2) % 2]
                        psb = buf(ps.name)

                        def emit(te, tp=tp, dh2=dh2, w=w, ps=ps):
                            i = None
                            for ci in range(C):
                                st = _fm_chunk(xn, ci, NT)[:, tp * 128:(tp + 1) * 128]
                                mv = _fm_chunk(w, ci, D)[:, dh2 * 512:(dh2 + 1) * 512]
                                i = te.matmul(ps[:], st, mv, start=(ci == 0),
                                              stop=(ci == C - 1))
                            return i
                        P.op("te", emit,
                             reads=[buf(xn.name), buf(w.name)], writes=[psb])
                        dst = otile[:, tp * D + dh2 * 512: tp * D + (dh2 + 1) * 512]
                        bias_ap = bkv_bc[:, which * D + dh2 * 512:
                                         which * D + (dh2 + 1) * 512]
                        P.op("ve", lambda v, ps=ps, dst=dst, bias_ap=bias_ap:
                             v.tensor_tensor(dst, ps[:], bias_ap, op=OP.add),
                             reads=[psb, buf(bkv_bc.name)], writes=[buf(otile.name)])
                        if which == 0:
                            P.op("ve", lambda v, dst=dst: v.tensor_scalar(
                                dst, dst, 0.0, None, op0=OP.max),
                                 reads=[buf(otile.name)], writes=[buf(otile.name)])

            for tp in range(4):
                for hp in range(C):
                    ps = ps_kv[hp // 2]
                    col = (hp % 2) * 256
                    vblk = (hp // 2) * 256

                    def emit(te, tp=tp, hp=hp, ps=ps, col=col, vblk=vblk, it=it):
                        first = (it == 0 and tp == 0 and hp % 2 == 0)
                        last = (it == NTILE - 1 and tp == 3)
                        st = ktile[:, tp * D + hp * 128: tp * D + (hp + 1) * 128]
                        mv = vtile[:, tp * D + vblk: tp * D + vblk + 256]
                        return te.matmul(ps[:, col:col + 256], st, mv,
                                         start=first, stop=last,
                                         skip_group_check=True)
                    P.op("te", emit,
                         reads=[buf(ktile.name), buf(vtile.name)],
                         writes=[buf(ps.name)])

        P.op("ve", lambda v: v.memset(kvd[:], 0.0), writes=[buf(kvd.name)])
        sc_kv = 1.0 / (T4 * DH)
        for hp in range(C):
            ps = ps_kv[hp // 2]
            col = (hp % 2) * 256
            vin = (hp % 2) * 128
            P.op("ve", lambda v, hp=hp, ps=ps, col=col, vin=vin: v.tensor_scalar(
                kvd[0:64, hp * 128:hp * 128 + 64],
                ps[0:64, col + vin:col + vin + 64], sc_kv, None, op0=OP.mult),
                 reads=[buf(ps.name)], writes=[buf(kvd.name)])
            P.op("ve", lambda v, hp=hp, ps=ps, col=col, vin=vin: v.tensor_scalar(
                kvd[64:128, hp * 128 + 64:(hp + 1) * 128],
                ps[64:128, col + vin + 64:col + vin + 128], sc_kv, None,
                op0=OP.mult),
                 reads=[buf(ps.name)], writes=[buf(kvd.name)])
        cx.dbg("dbg_kvd", kvd[:], [buf(kvd.name)])

        for wname, w in [("l0_wq", wA), ("l0_wo", wB)]:
            src = g[wname][:].rearrange("(c p) n -> p c n", p=128)
            dst = w[:].rearrange("p (c n) -> p c n", c=C)
            P.op("sy", lambda s, src=src, dst=dst: s.dma_start(out=dst, in_=src),
                 reads=[buf("hbm_" + wname)], writes=[buf(w.name)], dma="dsy")

        for it in range(NTILE):
            xtile, bx = load_x(it)
            for which, rw in ((0, m_row), (1, r_row)):
                P.op("sy", lambda s, which=which, rw=rw, it=it: s.dma_start(
                    out=rw[:], in_=g["mr_hbm"][which:which + 1,
                                               it * NT:(it + 1) * NT]),
                     reads=[buf("hbm_mr")], writes=[buf(rw.name)], dma="dsy")
                P.op("te", lambda te, which=which, rw=rw: te.matmul(
                    ps_stats[:, which * NT:(which + 1) * NT],
                    cx.onescol[0:1, :], rw[:],
                    start=True, stop=True),
                     reads=[buf(rw.name), buf(cx.onescol.name)],
                     writes=[buf(ps_stats.name)])
            xn = S2
            _apply_ln_fm(cx, xtile, None, None, xn, NT,
                         m_b=buf(ps_stats.name), r_b=buf(ps_stats.name),
                         m_ap=ps_stats[:, 0:NT], r_ap=ps_stats[:, NT:2 * NT])

            q_fm, a_fm, x1t = S0, S1, xtile
            for co in range(C):
                ps = ps_proj[co % 2]
                psb = buf(ps.name)

                def emit(te, co=co, ps=ps):
                    i = None
                    for ci in range(C):
                        st = _fm_chunk(wA, ci, D)[:, co * 128:(co + 1) * 128]
                        mv = _fm_chunk(xn, ci, NT)
                        i = te.matmul(ps[:], st, mv, start=(ci == 0),
                                      stop=(ci == C - 1))
                    return i
                P.op("te", emit, reads=[buf(xn.name), buf(wA.name)], writes=[psb])
                P.op("sc", lambda s, co=co, ps=ps: s.activation(
                    _fm_chunk(q_fm, co, NT), ps[:], AF.Relu,
                    bias=cx.bpp0[:, 0, co:co + 1]),
                     reads=[psb, buf(cx.bpp0.name)], writes=[buf(q_fm.name)])

            for hp in range(C):
                ps = ps_proj[hp % 2]
                psb = buf(ps.name)
                P.op("te", lambda te, hp=hp, ps=ps: te.matmul(
                    ps[:], kvd[:, hp * 128:(hp + 1) * 128],
                    _fm_chunk(q_fm, hp, NT), start=True, stop=True),
                     reads=[buf(q_fm.name), buf(kvd.name)], writes=[psb])
                P.op("ve", lambda v, hp=hp, ps=ps: v.tensor_copy(
                    _fm_chunk(a_fm, hp, NT), ps[:]),
                     reads=[psb], writes=[buf(a_fm.name)])

            for co in range(C):
                ps = ps_proj[co % 2]
                psb = buf(ps.name)

                def emit(te, co=co, ps=ps):
                    i = None
                    for ci in range(C):
                        st = _fm_chunk(wB, ci, D)[:, co * 128:(co + 1) * 128]
                        mv = _fm_chunk(a_fm, ci, NT)
                        i = te.matmul(ps[:], st, mv, start=(ci == 0),
                                      stop=(ci == C - 1))
                    return i
                P.op("te", emit, reads=[buf(a_fm.name), buf(wB.name)],
                     writes=[psb])
                P.op("ve", lambda v, co=co, ps=ps, xtile=xtile:
                     v.scalar_tensor_tensor(
                         _fm_chunk(x1t, co, NT), ps[:], cx.bpp0[:, 3, co:co + 1],
                         _fm_chunk(xtile, co, NT), op0=OP.add, op1=OP.add),
                     reads=[psb, bx, buf(cx.bpp0.name)], writes=[buf(x1t.name)])
            P.op("gp", lambda gp, it=it: gp.dma_start(
                out=g["x1_hbm"][:].rearrange("p (c t) -> p c t", c=C)
                [:, :, it * NT:(it + 1) * NT],
                in_=x1t[:].rearrange("p (c t) -> p c t", c=C)),
                 reads=[buf(x1t.name)], writes=[buf("hbm_x1")], dma="dgp")
            cx.dbg_sl("dbg_x1", np.s_[:, it * NT * C:(it + 1) * NT * C],
                      x1t[:], [buf(x1t.name)])


def _mm3(te, ps_ap, st_hi, st_lo, mv_hi, mv_lo, first, last, skip=False):
    i = te.matmul(ps_ap, st_hi, mv_hi, start=first, stop=False,
                  skip_group_check=skip)
    i = te.matmul(ps_ap, st_lo, mv_hi, start=False, stop=False,
                  skip_group_check=skip)
    i = te.matmul(ps_ap, st_hi, mv_lo, start=False, stop=last,
                  skip_group_check=skip)
    return i


def _fp4_chain(cx, xb_ap, q_ap, W, s_a, s_g, s_p2, s_int, bm, rs, bufs):
    P = cx.P
    bx, ba, bg, bi, bbm, brs, bq = (bufs[k] for k in
                                    ("bx", "ba", "bg", "bi", "bbm", "brs", "bq"))
    a3 = s_a.rearrange("p (b k) -> p b k", k=16)
    P.op("sc", lambda s: s.activation(s_a, xb_ap, AF.Abs), reads=[bx], writes=[ba])
    P.op("ve", lambda v: v.tensor_reduce(bm, a3, axis=AX.X, op=OP.max),
         reads=[ba], writes=[bbm])
    P.op("ve", lambda v: v.tensor_scalar(bm, bm, cx.recip6[:], 1e-12,
                                         op0=OP.mult, op1=OP.max),
         reads=[bbm, cx.buf("c_recip6")], writes=[bbm])
    P.op("ve", lambda v: v.reciprocal(rs, bm), reads=[bbm], writes=[brs])
    rsb = rs.unsqueeze(2).broadcast_to((128, W // 16, 16))
    bmb = bm.unsqueeze(2).broadcast_to((128, W // 16, 16))
    P.op("ve", lambda v: v.tensor_tensor(a3, a3, rsb, op=OP.mult),
         reads=[ba, brs], writes=[ba])
    P.op("ve", lambda v: v.tensor_scalar(s_g, s_a, 0.25, None, op0=OP.is_gt),
         reads=[ba], writes=[bg])
    for thr in (0.75, 1.25, 1.75):
        P.op("ve", lambda v, thr=thr: v.scalar_tensor_tensor(
            s_g, s_a, thr, s_g, op0=OP.is_gt, op1=OP.add),
             reads=[ba, bg], writes=[bg])
    bp2 = bufs["bp2"]
    P.op("ve", lambda v: v.tensor_scalar(s_p2, s_a, 5.0, 2.0,
                                         op0=OP.is_gt, op1=OP.mult),
         reads=[ba], writes=[bp2])
    for thr in (2.5, 3.5):
        P.op("ve", lambda v, thr=thr: v.scalar_tensor_tensor(
            s_p2, s_a, thr, s_p2, op0=OP.is_gt, op1=OP.add),
             reads=[ba, bp2], writes=[bp2])
    P.op("ve", lambda v: v.scalar_tensor_tensor(
        s_g, s_g, 0.5, s_p2, op0=OP.mult, op1=OP.add),
         reads=[bg, bp2], writes=[bg])
    P.op("sc", lambda s: s.activation(s_a, xb_ap, AF.Sign),
         reads=[bx], writes=[ba])
    P.op("ve", lambda v: v.tensor_tensor(s_g, s_g, s_a, op=OP.mult),
         reads=[bg, ba], writes=[bg])
    q3 = q_ap.rearrange("p (b k) -> p b k", k=16)
    P.op("ve", lambda v: v.tensor_tensor(
        q3, s_g.rearrange("p (b k) -> p b k", k=16), bmb, op=OP.mult),
         reads=[bg, bbm], writes=[bq])


def _phase2(cx):
    nc, P, g, buf = cx.nc, cx.P, cx.g, cx.buf
    import contextlib
    NSLAB = FF0 // 128
    with contextlib.ExitStack() as ph:
        wslab = [_sb(nc, ph, f"p2_wsl{i}", [128, FF0]) for i in range(2)]
        s_a = _sb(nc, ph, "p2_sa", [128, FF0])
        s_g = _sb(nc, ph, "p2_sg", [128, FF0])
        s_int = _sb(nc, ph, "p2_si", [128, FF0], I32)
        bm_t = _sb(nc, ph, "p2_bm", [128, FF0 // 16])
        rs_t = _sb(nc, ph, "p2_rs", [128, FF0 // 16])
        ps_tp = _ps(nc, ph, "p2_ps_tp", [128, 512])
        ps_tp2 = _ps(nc, ph, "p2_ps_tp2", [128, 512])

        s_p2t = _sb(nc, ph, "p2_sp2", [128, FF0])

        def chain_bufs(bx, bq):
            return {"bx": bx, "ba": buf(s_a.name), "bg": buf(s_g.name),
                    "bp2": buf(s_p2t.name), "bi": buf(s_int.name),
                    "bbm": buf(bm_t.name), "brs": buf(rs_t.name), "bq": bq}

        for which, wname in ((0, "f0_wg"), (1, "f0_wu")):
            for n in range(NSLAB):
                w = wslab[n % 2]
                bw = buf(w.name)
                P.op("sy", lambda s, wname=wname, n=n, w=w: s.dma_start(
                    out=w[:, 0:D], in_=g[wname][n * 128:(n + 1) * 128, :]),
                     reads=[buf("hbm_" + wname)], writes=[bw], dma="dsy")
                _fp4_chain(cx, w[:, 0:D], w[:, D:2 * D], D, s_a[:, 0:D],
                           s_g[:, 0:D], s_p2t[:, 0:D], s_int[:, 0:D],
                           bm_t[:, 0:64], rs_t[:, 0:64], chain_bufs(bw, bw))
                for ci in range(C):
                    dst_ps = ps_tp if ci % 2 == 0 else ps_tp2
                    P.op("te", lambda te, w=w, ci=ci, dst_ps=dst_ps: te.transpose(
                        dst_ps[:, 0:128], w[:, D + ci * 128:D + (ci + 1) * 128],
                        cx.idn[:]),
                         reads=[bw, buf(cx.idn.name)], writes=[buf(dst_ps.name)])
                    P.op("ve", lambda v, ci=ci, dst_ps=dst_ps: v.tensor_copy(
                        s_a[:, ci * 128:(ci + 1) * 128], dst_ps[:, 0:128]),
                         reads=[buf(dst_ps.name)], writes=[buf(s_a.name)])
                P.op("gp", lambda gp, which=which, n=n: gp.dma_start(
                    out=g["wq_fm_hbm"][:].rearrange("p (c n) -> p c n", c=C)
                    [:, :, which * FF0 + n * 128: which * FF0 + (n + 1) * 128],
                    in_=s_a[:, 0:D].rearrange("p (c n) -> p c n", c=C)),
                     reads=[buf(s_a.name)], writes=[buf("hbm_wqfm")], dma="dgp")

        for n in range(C):
            w = wslab[n % 2]
            bw = buf(w.name)
            P.op("sy", lambda s, n=n, w=w: s.dma_start(
                out=w[:], in_=g["f0_wd"][n * 128:(n + 1) * 128, :]),
                 reads=[buf("hbm_f0_wd")], writes=[bw], dma="dsy")
            _fp4_chain(cx, w[:], s_g[:], FF0, s_a[:], s_g[:], s_p2t[:],
                       s_int[:], bm_t[:], rs_t[:], chain_bufs(bw, buf(s_g.name)))
            for k2 in range(NSLAB):
                dst_ps = ps_tp if k2 % 2 == 0 else ps_tp2
                P.op("te", lambda te, k2=k2, dst_ps=dst_ps: te.transpose(
                    dst_ps[:, 0:128], s_g[:, k2 * 128:(k2 + 1) * 128], cx.idn[:]),
                     reads=[buf(s_g.name), buf(cx.idn.name)],
                     writes=[buf(dst_ps.name)])
                P.op("ve", lambda v, k2=k2, dst_ps=dst_ps: v.tensor_copy(
                    s_a[:, k2 * 128:(k2 + 1) * 128], dst_ps[:, 0:128]),
                     reads=[buf(dst_ps.name)], writes=[buf(s_a.name)])
            P.op("gp", lambda gp, n=n: gp.dma_start(
                out=g["wd_fm_hbm"][:].rearrange("p (c n) -> p c n", c=NSLAB)
                [:, :, n * 128:(n + 1) * 128],
                in_=s_a[:, 0:FF0].rearrange("p (c n) -> p c n", c=NSLAB)),
                 reads=[buf(s_a.name)], writes=[buf("hbm_wdfm")], dma="dgp")

    P.fence()
    NSUP = 512
    NTP = NSUP // 128
    with contextlib.ExitStack() as ph:
        x1sup = _sb(nc, ph, "p2_x1s", [128, C * NSUP])
        xqT_hi = _sb(nc, ph, "p2_xqh", [128, NTP * C * 128], F32R)
        xqT_lo = _sb(nc, ph, "p2_xql", [128, NTP * C * 128], F32R)
        hqfm = _sb(nc, ph, "p2_hqfm", [128, NSLAB * NSUP])
        wsl_hi = [_sb(nc, ph, f"p2_wslh{i}", [128, C * 256], F32R) for i in range(2)]
        wsl_lo = [_sb(nc, ph, f"p2_wsll{i}", [128, C * 256], F32R) for i in range(2)]
        wsl_raw = [_sb(nc, ph, f"p2_wslr{i}", [128, C * 256]) for i in range(2)]
        wdcol = [_sb(nc, ph, f"p2_wdc{i}", [128, NSLAB * 128]) for i in range(2)]
        hpiece = _sb(nc, ph, "p2_hp", [128, NTP * 256])
        s_a = _sb(nc, ph, "p2m_sa", [128, NTP * 256])
        s_g = _sb(nc, ph, "p2m_sg", [128, NTP * 256])
        s_int = _sb(nc, ph, "p2m_si", [128, NTP * 256], I32)
        bm_t = _sb(nc, ph, "p2m_bm", [128, NTP * 16])
        rs_t = _sb(nc, ph, "p2m_rs", [128, NTP * 16])
        xn2 = _sb(nc, ph, "p2_xn2", [128, D])
        xq2 = _sb(nc, ph, "p2_xq2", [128, D])
        junk = _sb(nc, ph, "p2_junk", [128, D])
        mv_t = _sb(nc, ph, "p2_mv", [128, 4])
        ps_x1 = _ps(nc, ph, "p2_ps_x1", [128, D])
        ps_tp = _ps(nc, ph, "p2_ps_t", [128, 512])
        ps_gu = [_ps(nc, ph, f"p2_ps_gu{i}", [128, 256]) for i in range(2)]
        ps_dn = [_ps(nc, ph, f"p2_ps_dn{i}", [128, NTP * 128]) for i in range(2)]

        s_p2t = _sb(nc, ph, "p2m_sp2", [128, NTP * 256])

        def chain_bufs(bx, bq):
            return {"bx": bx, "ba": buf(s_a.name), "bg": buf(s_g.name),
                    "bp2": buf(s_p2t.name), "bi": buf(s_int.name),
                    "bbm": buf(bm_t.name), "brs": buf(rs_t.name), "bq": bq}

        for sup in range(T4 // NSUP):
            P.op("sy", lambda s, sup=sup: s.dma_start(
                out=x1sup[:].rearrange("p (c t) -> p c t", c=C),
                in_=g["x1_hbm"][:].rearrange("p (c t) -> p c t", c=C)
                [:, :, sup * NSUP:(sup + 1) * NSUP]),
                 reads=[buf("hbm_x1")], writes=[buf(x1sup.name)], dma="dsy")
            if sup == 0:
                cx.dbg("dbg_x1rd", x1sup[:], [buf(x1sup.name)])
            for tp in range(NTP):
                for ci in range(C):
                    P.op("te", lambda te, tp=tp, ci=ci: te.transpose(
                        ps_x1[:, ci * 128:(ci + 1) * 128],
                        x1sup[:, ci * NSUP + tp * 128: ci * NSUP + (tp + 1) * 128],
                        cx.idn[:]),
                         reads=[buf(x1sup.name), buf(cx.idn.name)],
                         writes=[buf(ps_x1.name)])
                P.op("ve", lambda v: v.tensor_reduce(mv_t[:, 0:1], ps_x1[:],
                                                     axis=AX.X, op=OP.add),
                     reads=[buf(ps_x1.name)], writes=[buf(mv_t.name)])
                P.op("sc", lambda s: s.activation(junk[:], ps_x1[:], AF.Square,
                                                  accum_out=mv_t[:, 1:2]),
                     reads=[buf(ps_x1.name)], writes=[buf(junk.name), buf(mv_t.name)])
                P.op("ve", lambda v: v.tensor_scalar(mv_t[:, 0:1], mv_t[:, 0:1],
                                                     1.0 / D, None, op0=OP.mult),
                     reads=[buf(mv_t.name)], writes=[buf(mv_t.name)])
                P.op("ve", lambda v: v.tensor_tensor(mv_t[:, 2:3], mv_t[:, 0:1],
                                                     mv_t[:, 0:1], op=OP.mult),
                     reads=[buf(mv_t.name)], writes=[buf(mv_t.name)])
                P.op("ve", lambda v: v.scalar_tensor_tensor(
                    mv_t[:, 2:3], mv_t[:, 1:2], 1.0 / D, mv_t[:, 2:3],
                    op0=OP.mult, op1=OP.subtract),
                     reads=[buf(mv_t.name)], writes=[buf(mv_t.name)])
                P.op("sc", lambda s: s.activation(mv_t[:, 2:3], mv_t[:, 2:3],
                                                  AF.Sqrt, bias=cx.eps_ap),
                     reads=[buf(mv_t.name), buf(cx.cpool.name)],
                     writes=[buf(mv_t.name)])
                P.op("ve", lambda v: v.reciprocal(mv_t[:, 2:3], mv_t[:, 2:3]),
                     reads=[buf(mv_t.name)], writes=[buf(mv_t.name)])
                P.op("ve", lambda v: v.tensor_scalar(
                    xn2[:], ps_x1[:], mv_t[:, 0:1], mv_t[:, 2:3],
                    op0=OP.subtract, op1=OP.mult),
                     reads=[buf(ps_x1.name), buf(mv_t.name)], writes=[buf(xn2.name)])
                _fp4_chain(cx, xn2[:], xq2[:], D, s_a[:, 0:D], s_g[:, 0:D],
                           s_p2t[:, 0:D], s_int[:, 0:D], bm_t[:, 0:64],
                           rs_t[:, 0:64],
                           chain_bufs(buf(xn2.name), buf(xq2.name)))
                if sup == 0 and tp == 0:
                    cx.dbg("dbg_xq", xq2[:], [buf(xq2.name)])
                    cx.dbg("dbg_xn2", xn2[:], [buf(xn2.name)])
                for ci in range(C):
                    P.op("te", lambda te, ci=ci: te.transpose(
                        ps_tp[:, 0:128], xq2[:, ci * 128:(ci + 1) * 128],
                        cx.idn[:]),
                         reads=[buf(xq2.name), buf(cx.idn.name)],
                         writes=[buf(ps_tp.name)])
                    off = (tp * C + ci) * 128
                    P.op("ve", lambda v, off=off: v.tensor_copy(
                        xqT_hi[:, off:off + 128], ps_tp[:, 0:128]),
                         reads=[buf(ps_tp.name)], writes=[buf(xqT_hi.name)])
                    P.op("ve", lambda v, off=off: v.tensor_tensor(
                        xqT_lo[:, off:off + 128], ps_tp[:, 0:128],
                        xqT_hi[:, off:off + 128].bitcast(F32), op=OP.subtract),
                         reads=[buf(ps_tp.name), buf(xqT_hi.name)],
                         writes=[buf(xqT_lo.name)])

            for sl in range(FF0 // 256):
                wr = wsl_raw[sl % 2]
                wh, wl = wsl_hi[sl % 2], wsl_lo[sl % 2]
                for which in range(2):
                    P.op("sy", lambda s, sl=sl, which=which, wr=wr: s.dma_start(
                        out=wr[:].rearrange("p (c n) -> p c n", c=C),
                        in_=g["wq_fm_hbm"][:].rearrange("p (c n) -> p c n", c=C)
                        [:, :, which * FF0 + sl * 256: which * FF0 + (sl + 1) * 256]),
                         reads=[buf("hbm_wqfm")], writes=[buf(wr.name)], dma="dsy")
                    if sup == 0 and sl == 0 and which == 0:
                        cx.dbg("dbg_wq", wr[:], [buf(wr.name)])
                    P.op("ve", lambda v, wr=wr, wh=wh: v.tensor_copy(wh[:], wr[:]),
                         reads=[buf(wr.name)], writes=[buf(wh.name)])
                    P.op("ve", lambda v, wr=wr, wh=wh, wl=wl: v.tensor_tensor(
                        wl[:], wr[:], wh[:].bitcast(F32), op=OP.subtract),
                         reads=[buf(wr.name), buf(wh.name)], writes=[buf(wl.name)])
                    for tp in range(NTP):
                        ps = ps_gu[tp % 2]

                        def emit(te, tp=tp, wh=wh, wl=wl, ps=ps):
                            i = None
                            for ci in range(C):
                                off = (tp * C + ci) * 128
                                sh = xqT_hi[:, off:off + 128]
                                sl_ = xqT_lo[:, off:off + 128]
                                mh = wh[:, ci * 256:(ci + 1) * 256]
                                ml = wl[:, ci * 256:(ci + 1) * 256]
                                i = _mm3(te, ps[:], sh, sl_, mh, ml,
                                         ci == 0, ci == C - 1)
                            return i
                        P.op("te", emit,
                             reads=[buf(xqT_hi.name), buf(xqT_lo.name),
                                    buf(wh.name), buf(wl.name)],
                             writes=[buf(ps.name)])
                        dst = hpiece[:, tp * 256:(tp + 1) * 256]
                        if which == 0:
                            P.op("sc", lambda s, ps=ps, dst=dst: s.activation(
                                dst, ps[:], AF.Silu),
                                 reads=[buf(ps.name)], writes=[buf(hpiece.name)])
                        else:
                            P.op("ve", lambda v, ps=ps, dst=dst: v.tensor_tensor(
                                dst, ps[:], dst, op=OP.mult),
                                 reads=[buf(ps.name), buf(hpiece.name)],
                                 writes=[buf(hpiece.name)])
                _fp4_chain(cx, hpiece[:], hpiece[:], NTP * 256, s_a[:], s_g[:],
                           s_p2t[:], s_int[:], bm_t[:], rs_t[:],
                           chain_bufs(buf(hpiece.name), buf(hpiece.name)))
                for tp in range(NTP):
                    for half in range(2):
                        k2c = sl * 2 + half
                        P.op("te", lambda te, tp=tp, half=half: te.transpose(
                            ps_tp[:, 0:128],
                            hpiece[:, tp * 256 + half * 128: tp * 256 + (half + 1) * 128],
                            cx.idn[:]),
                             reads=[buf(hpiece.name), buf(cx.idn.name)],
                             writes=[buf(ps_tp.name)])
                        P.op("ve", lambda v, k2c=k2c, tp=tp: v.tensor_copy(
                            hqfm[:, k2c * NSUP + tp * 128: k2c * NSUP + (tp + 1) * 128],
                            ps_tp[:, 0:128]),
                             reads=[buf(ps_tp.name)], writes=[buf(hqfm.name)])

            if sup == 0:
                cx.dbg("dbg_hq", hqfm[:], [buf(hqfm.name)])
            for co in range(C):
                wd = wdcol[co % 2]
                P.op("sy", lambda s, co=co, wd=wd: s.dma_start(
                    out=wd[:].rearrange("p (c n) -> p c n", c=NSLAB),
                    in_=g["wd_fm_hbm"][:].rearrange("p (c n) -> p c n", c=NSLAB)
                    [:, :, co * 128:(co + 1) * 128]),
                     reads=[buf("hbm_wdfm")], writes=[buf(wd.name)], dma="dsy")
                if sup == 0 and co == 0:
                    cx.dbg("dbg_wd", wd[:], [buf(wd.name)])
                ps = ps_dn[co % 2]

                def emit(te, co=co, wd=wd, ps=ps):
                    i = None
                    for k2 in range(NSLAB):
                        st = wd[:, k2 * 128:(k2 + 1) * 128]
                        mv = hqfm[:, k2 * NSUP:(k2 + 1) * NSUP]
                        i = te.matmul(ps[:], st, mv, start=(k2 == 0),
                                      stop=(k2 == NSLAB - 1))
                    return i
                P.op("te", emit, reads=[buf(wd.name), buf(hqfm.name)],
                     writes=[buf(ps.name)])
                P.op("ve", lambda v, co=co, ps=ps: v.tensor_tensor(
                    s_a[:, 0:NSUP], ps[:],
                    x1sup[:, co * NSUP:(co + 1) * NSUP], op=OP.add),
                     reads=[buf(ps.name), buf(x1sup.name)], writes=[buf(s_a.name)])
                P.op("gp", lambda gp, co=co, sup=sup: gp.dma_start(
                    out=g["x2_hbm"][:, co * T4 + sup * NSUP:
                                    co * T4 + (sup + 1) * NSUP],
                    in_=s_a[:, 0:NSUP]),
                     reads=[buf(s_a.name)], writes=[buf("hbm_x2")], dma="dgp")
                cx.dbg_sl("dbg_x2", np.s_[:, co * T4 + sup * NSUP:
                                          co * T4 + (sup + 1) * NSUP],
                          s_a[:, 0:NSUP], [buf(s_a.name)])


def _phase3(cx):
    nc, P, g, buf = cx.nc, cx.P, cx.g, cx.buf
    import contextlib
    NT = 512

    def proj_pass(kind):
        with contextlib.ExitStack() as ph:
            wname = {"k": "l1_wk", "v": "l1_wv", "q": "l1_wq"}[kind]
            w = _load_w(cx, ph, wname, f"p3_{kind}_w")
            ntile = (T2 if kind == "q" else T4) // NT
            xt = [_sb(nc, ph, f"p3{kind}_x{i}", [128, C * NT]) for i in range(2)]
            S0 = _sb(nc, ph, f"p3{kind}_S0", [128, C * NT])
            S1 = _sb(nc, ph, f"p3{kind}_S1", [128, C * NT])
            S2 = _sb(nc, ph, f"p3{kind}_S2", [128, C * NT])
            SHt = _sb(nc, ph, f"p3{kind}_SH", [128, C * NT], F32R)
            SLt = _sb(nc, ph, f"p3{kind}_SL", [128, C * NT], F32R)
            m_t = _sb(nc, ph, f"p3{kind}_m", [128, NT])
            r_t = _sb(nc, ph, f"p3{kind}_r", [128, NT])
            m_row = _sb(nc, ph, f"p3{kind}_mr", [1, NT])
            r_row = _sb(nc, ph, f"p3{kind}_rr", [1, NT])
            rope_raw = _sb(nc, ph, f"p3{kind}_rr4", [128, 4 * DH])
            rope_cs = _sb(nc, ph, f"p3{kind}_rcs", [128, 2 * DH])
            bv_bc = _sb(nc, ph, f"p3{kind}_bvb", [128, D])
            ps_stats = _ps(nc, ph, f"p3{kind}_pst", [128, 2 * NT])
            ps_a = [_ps(nc, ph, f"p3{kind}_psa{i}", [128, NT]) for i in range(2)]
            ps_tm = _ps(nc, ph, f"p3{kind}_ptm", [128, 2048])

            if kind == "v":
                brow1 = _sb(nc, ph, f"p3v_brow", [1, D])
                P.op("sy", lambda s: s.dma_start(
                    out=brow1[:], in_=g["l1_brow"][0:1, 2 * D:3 * D]),
                     reads=[buf("hbm_l1_brow")], writes=[buf(brow1.name)],
                     dma="dsy")
                for half in range(2):
                    P.op("te", lambda te, half=half: te.matmul(
                        ps_stats[:, 0:512], cx.onescol[0:1, :],
                        brow1[0:1, half * 512:(half + 1) * 512],
                        start=True, stop=True),
                         reads=[buf(cx.onescol.name), buf(brow1.name)],
                         writes=[buf(ps_stats.name)])
                    P.op("ve", lambda v, half=half: v.tensor_copy(
                        bv_bc[:, half * 512:(half + 1) * 512], ps_stats[:, 0:512]),
                         reads=[buf(ps_stats.name)], writes=[buf(bv_bc.name)])

            def hl_split(src_ap, sbuf):
                P.op("ve", lambda v: v.tensor_copy(SHt[:], src_ap),
                     reads=[sbuf], writes=[buf(SHt.name)])
                P.op("ve", lambda v: v.tensor_tensor(
                    SLt[:], src_ap, SHt[:].bitcast(F32), op=OP.subtract),
                     reads=[sbuf, buf(SHt.name)], writes=[buf(SLt.name)])

            for it in range(ntile):
                xtile = xt[it % 2]
                bx = buf(xtile.name)
                P.op("sy", lambda s, it=it, xtile=xtile: s.dma_start(
                    out=xtile[:].rearrange("p (c t) -> p c t", c=C),
                    in_=g["x2_hbm"][:].rearrange("p (c t) -> p c t", c=C)
                    [:, :, it * NT:(it + 1) * NT]),
                     reads=[buf("hbm_x2")], writes=[bx], dma="dsy")
                if kind == "k":
                    hl_split(xtile[:], bx)
                    psb = buf(ps_stats.name)

                    def mm_sum(col):
                        def e(te):
                            i = None
                            for cch in range(C):
                                i = te.matmul(ps_stats[:, col:col + NT],
                                              cx.ones_r[:],
                                              SHt[:, cch * NT:(cch + 1) * NT],
                                              start=(cch == 0), stop=False)
                                i = te.matmul(ps_stats[:, col:col + NT],
                                              cx.ones_r[:],
                                              SLt[:, cch * NT:(cch + 1) * NT],
                                              start=False, stop=(cch == C - 1))
                            return i
                        return e
                    P.op("te", mm_sum(0),
                         reads=[buf(SHt.name), buf(SLt.name),
                                buf(cx.ones_r.name)], writes=[psb])
                    P.op("sc", lambda s: s.activation(S2[:], xtile[:], AF.Square),
                         reads=[bx], writes=[buf(S2.name)])
                    hl_split(S2[:], buf(S2.name))
                    P.op("te", mm_sum(NT),
                         reads=[buf(SHt.name), buf(SLt.name),
                                buf(cx.ones_r.name)], writes=[psb])
                    bm, br = buf(m_t.name), buf(r_t.name)
                    P.op("ve", lambda v: v.tensor_scalar(
                        m_t[:], ps_stats[:, 0:NT], 1.0 / D, None, op0=OP.mult),
                         reads=[psb], writes=[bm])
                    P.op("ve", lambda v: v.tensor_tensor(
                        r_t[:], m_t[:], m_t[:], op=OP.mult),
                         reads=[bm], writes=[br])
                    P.op("ve", lambda v: v.scalar_tensor_tensor(
                        r_t[:], ps_stats[:, NT:2 * NT], 1.0 / D, r_t[:],
                        op0=OP.mult, op1=OP.subtract),
                         reads=[psb, br], writes=[br])
                    P.op("sc", lambda s: s.activation(
                        r_t[:], r_t[:], AF.Sqrt, bias=cx.eps_ap),
                         reads=[br, buf(cx.cpool.name)], writes=[br])
                    P.op("ve", lambda v: v.reciprocal(r_t[:], r_t[:]),
                         reads=[br], writes=[br])
                    P.op("gp", lambda gp, it=it: gp.dma_start(
                        out=g["mr_hbm"][0:1, it * NT:(it + 1) * NT],
                        in_=m_t[0:1, :]),
                         reads=[bm], writes=[buf("hbm_mr")], dma="dgp")
                    P.op("gp", lambda gp, it=it: gp.dma_start(
                        out=g["mr_hbm"][1:2, it * NT:(it + 1) * NT],
                        in_=r_t[0:1, :]),
                         reads=[br], writes=[buf("hbm_mr")], dma="dgp")
                    _apply_ln_fm(cx, xtile, m_t, r_t, S2, NT)
                else:
                    for which, rw in ((0, m_row), (1, r_row)):
                        P.op("sy", lambda s, which=which, rw=rw, it=it:
                             s.dma_start(out=rw[:], in_=g["mr_hbm"]
                                         [which:which + 1, it * NT:(it + 1) * NT]),
                             reads=[buf("hbm_mr")], writes=[buf(rw.name)],
                             dma="dsy")
                        P.op("te", lambda te, which=which, rw=rw: te.matmul(
                            ps_stats[:, which * NT:(which + 1) * NT],
                            cx.onescol[0:1, :], rw[:], start=True, stop=True),
                             reads=[buf(rw.name), buf(cx.onescol.name)],
                             writes=[buf(ps_stats.name)])
                    _apply_ln_fm(cx, xtile, None, None, S2, NT,
                                 m_b=buf(ps_stats.name), r_b=buf(ps_stats.name),
                                 m_ap=ps_stats[:, 0:NT],
                                 r_ap=ps_stats[:, NT:2 * NT])
                xn = S2

                if kind == "v":
                    for tp in range(4):
                        for dh2 in range(2):
                            ps = ps_a[(tp * 2 + dh2) % 2]
                            psb = buf(ps.name)

                            def emit(te, tp=tp, dh2=dh2, ps=ps):
                                i = None
                                for ci in range(C):
                                    st = xn[:, ci * NT + tp * 128:
                                            ci * NT + (tp + 1) * 128]
                                    mv = w[:, ci * D + dh2 * 512:
                                           ci * D + (dh2 + 1) * 512]
                                    i = te.matmul(ps[:], st, mv,
                                                  start=(ci == 0),
                                                  stop=(ci == C - 1))
                                return i
                            P.op("te", emit,
                                 reads=[buf(xn.name), buf(w.name)], writes=[psb])
                            dst = S0[:, tp * D + dh2 * 512:
                                     tp * D + (dh2 + 1) * 512]
                            P.op("ve", lambda v, ps=ps, dst=dst, dh2=dh2:
                                 v.tensor_tensor(
                                     dst, ps[:],
                                     bv_bc[:, dh2 * 512:(dh2 + 1) * 512],
                                     op=OP.add),
                                 reads=[psb, buf(bv_bc.name)],
                                 writes=[buf(S0.name)])
                    for tp in range(4):
                        for hp in range(C):
                            P.op("gp", lambda gp, tp=tp, hp=hp, it=it:
                                 gp.dma_start(
                                     out=g["v_hbm"][hp:hp + 1,
                                                    it * NT + tp * 128:
                                                    it * NT + (tp + 1) * 128, :]
                                     .squeeze(0),
                                     in_=S0[:, tp * D + hp * 128:
                                            tp * D + (hp + 1) * 128]),
                                 reads=[buf(S0.name)], writes=[buf("hbm_v")],
                                 dma="dgp")
                else:
                    bcol = 0 if kind == "q" else 1
                    for co in range(C):
                        ps = ps_a[co % 2]
                        psb = buf(ps.name)

                        def emit(te, co=co, ps=ps):
                            i = None
                            for ci in range(C):
                                st = w[:, ci * D + co * 128:
                                       ci * D + (co + 1) * 128]
                                mv = xn[:, ci * NT:(ci + 1) * NT]
                                i = te.matmul(ps[:], st, mv, start=(ci == 0),
                                              stop=(ci == C - 1))
                            return i
                        P.op("te", emit,
                             reads=[buf(xn.name), buf(w.name)], writes=[psb])
                        P.op("sc", lambda s, co=co, ps=ps: s.activation(
                            S0[:, co * NT:(co + 1) * NT], ps[:], AF.Identity,
                            bias=cx.bpp1[:, bcol, co:co + 1]),
                             reads=[psb, buf(cx.bpp1.name)], writes=[buf(S0.name)])
                    tbl = "ropeq" if kind == "q" else "ropek"
                    for tp in range(4):
                        P.op("sy", lambda s, tp=tp, it=it: s.dma_start(
                            out=rope_raw[:],
                            in_=g[tbl][it * NT + tp * 128:
                                       it * NT + (tp + 1) * 128, :]),
                             reads=[buf("hbm_" + tbl)],
                             writes=[buf(rope_raw.name)], dma="dsy")
                        P.op("ve", lambda v: v.tensor_tensor(
                            rope_cs[:, 0:DH], rope_raw[:, 0:DH],
                            rope_raw[:, DH:2 * DH], op=OP.add),
                             reads=[buf(rope_raw.name)], writes=[buf(rope_cs.name)])
                        P.op("ve", lambda v: v.tensor_tensor(
                            rope_cs[:, DH:2 * DH], rope_raw[:, 2 * DH:3 * DH],
                            rope_raw[:, 3 * DH:4 * DH], op=OP.add),
                             reads=[buf(rope_raw.name)], writes=[buf(rope_cs.name)])
                        for ci in range(C):
                            P.op("te", lambda te, ci=ci, tp=tp: te.transpose(
                                ps_tm[:, ci * 128:(ci + 1) * 128],
                                S0[:, ci * NT + tp * 128:ci * NT + (tp + 1) * 128],
                                cx.idn[:]),
                                 reads=[buf(S0.name), buf(cx.idn.name)],
                                 writes=[buf(ps_tm.name)])
                        p3 = ps_tm[:, 0:1024].rearrange("p (h d) -> p h d", d=DH)
                        o3 = S1[:, tp * 1024:(tp + 1) * 1024].rearrange(
                            "p (h d) -> p h d", d=DH)
                        cs = rope_cs[:, 0:DH]
                        sn = rope_cs[:, DH:2 * DH]
                        P.op("ve", lambda v, p3=p3, o3=o3, cs=cs: v.tensor_tensor(
                            o3, p3, cs.unsqueeze(1).broadcast_to((128, 2 * C, DH)),
                            op=OP.mult),
                             reads=[buf(ps_tm.name), buf(rope_cs.name)],
                             writes=[buf(S1.name)])
                        for gxx in range(2):
                            po = ps_tm[:, 0:1024].rearrange("p (h g d) -> p h g d", g=2, d=32)
                            oo = S1[:, tp * 1024:(tp + 1) * 1024].rearrange(
                                "p (h g d) -> p h g d", g=2, d=32)
                            src_g = po[:, :, 1 - gxx, :]
                            dst_g = oo[:, :, gxx, :]
                            s_g = sn[:, gxx * 32:(gxx + 1) * 32]
                            P.op("ve", lambda v, src_g=src_g, s_g=s_g:
                                 v.tensor_tensor(
                                     S2[:, 0:2 * C * 32].rearrange(
                                         "p (h d) -> p h d", d=32),
                                     src_g,
                                     s_g.unsqueeze(1).broadcast_to((128, 2 * C, 32)),
                                     op=OP.mult),
                                 reads=[buf(ps_tm.name), buf(rope_cs.name)],
                                 writes=[buf(S2.name)])
                            P.op("ve", lambda v, dst_g=dst_g: v.tensor_tensor(
                                dst_g, dst_g,
                                S2[:, 0:2 * C * 32].rearrange(
                                    "p (h d) -> p h d", d=32),
                                op=OP.add),
                                 reads=[buf(S2.name), buf(S1.name)],
                                 writes=[buf(S1.name)])
                        for ci in range(C):
                            P.op("te", lambda te, ci=ci, tp=tp: te.transpose(
                                ps_tm[:, 1024 + ci * 128:1024 + (ci + 1) * 128],
                                S1[:, tp * 1024 + ci * 128:
                                   tp * 1024 + (ci + 1) * 128],
                                cx.idn[:]),
                                 reads=[buf(S1.name), buf(cx.idn.name)],
                                 writes=[buf(ps_tm.name)])
                            P.op("ve", lambda v, ci=ci, tp=tp: v.tensor_copy(
                                S0[:, ci * NT + tp * 128:ci * NT + (tp + 1) * 128],
                                ps_tm[:, 1024 + ci * 128:1024 + (ci + 1) * 128]),
                                 reads=[buf(ps_tm.name)], writes=[buf(S0.name)])
                    dst_t = "q_hbm" if kind == "q" else "k_hbm"
                    for ci in range(C):
                        for hh in range(2):
                            P.op("gp", lambda gp, ci=ci, hh=hh, it=it:
                                 gp.dma_start(
                                     out=g[dst_t][2 * ci + hh:2 * ci + hh + 1, :,
                                                  it * NT:(it + 1) * NT]
                                     .squeeze(0),
                                     in_=S0[hh * 64:(hh + 1) * 64,
                                            ci * NT:(ci + 1) * NT]),
                                 reads=[buf(S0.name)],
                                 writes=[buf("hbm_" + dst_t)], dma="dgp")

        P.fence()

    proj_pass("k")
    proj_pass("v")
    proj_pass("q")

    NSUPQ = 512
    with contextlib.ExitStack() as ph:
        kraw = _sb(nc, ph, "p3a_kraw", [64, T4])
        k_hi = _sb(nc, ph, "p3a_kh", [64, T4], F32R)
        k_lo = _sb(nc, ph, "p3a_kl", [64, T4], F32R)
        vstr = _sb(nc, ph, "p3a_v", [128, 32 * 128])
        vext = _sb(nc, ph, "p3a_ve", [128, 32 * 130])
        qs = _sb(nc, ph, "p3a_q", [64, 2 * NSUPQ])
        q_hi = _sb(nc, ph, "p3a_qh", [64, 2 * NSUPQ], F32R)
        q_lo = _sb(nc, ph, "p3a_ql", [64, 2 * NSUPQ], F32R)
        ptile = [_sb(nc, ph, f"p3a_p{i}", [128, NSUPQ]) for i in range(2)]
        dnr = _sb(nc, ph, "p3a_dnr", [1, NSUPQ])
        dbc = _sb(nc, ph, "p3a_dbc", [64, NSUPQ])
        o_heads = _sb(nc, ph, "p3a_oh", [64, H * NSUPQ])
        wo = [_sb(nc, ph, f"p3a_wo{i}", [64, H * 128]) for i in range(2)]
        x2own = _sb(nc, ph, "p3a_x2o", [128, C * NSUPQ])
        x3t = _sb(nc, ph, "p3a_x3", [128, C * NSUPQ])
        ps_S = [_ps(nc, ph, f"p3a_psS{i}", [128, NSUPQ]) for i in range(2)]
        ps_OA = _ps(nc, ph, "p3a_psOA", [65, NSUPQ])
        ps_OB = _ps(nc, ph, "p3a_psOB", [65, NSUPQ])
        ps_bc = _ps(nc, ph, "p3a_psbc", [64, NSUPQ])
        ps_o = [_ps(nc, ph, f"p3a_pso{i}", [128, NSUPQ]) for i in range(2)]

        P.op("ve", lambda v: v.memset(vext[:], 1.0), writes=[buf(vext.name)])

        for sup in range(T2 // NSUPQ):
            for hp in range(C):
                P.op("sy", lambda s, hp=hp: s.dma_start(
                    out=vstr[:].rearrange("p (k n) -> p k n", k=32),
                    in_=g["v_hbm"][hp:hp + 1, :, :].squeeze(0)
                    .rearrange("(k p) n -> p k n", p=128)),
                     reads=[buf("hbm_v")], writes=[buf(vstr.name)], dma="dsy")
                for half in range(2):
                    P.op("ve", lambda v, half=half: v.tensor_copy(
                        vext[:].rearrange("p (k n) -> p k n", k=32)
                        [:, :, half * 65:half * 65 + 64],
                        vstr[:].rearrange("p (k n) -> p k n", k=32)
                        [:, :, half * 64:(half + 1) * 64]),
                         reads=[buf(vstr.name)], writes=[buf(vext.name)])
                P.op("sy", lambda s, hp=hp, sup=sup: s.dma_start(
                    out=qs[:].rearrange("p (h n) -> p h n", h=2),
                    in_=g["q_hbm"][2 * hp:2 * hp + 2, :,
                                   sup * NSUPQ:(sup + 1) * NSUPQ]
                    .rearrange("h p n -> p h n")),
                     reads=[buf("hbm_q_hbm")], writes=[buf(qs.name)], dma="dsy")
                P.op("ve", lambda v: v.tensor_copy(q_hi[:], qs[:]),
                     reads=[buf(qs.name)], writes=[buf(q_hi.name)])
                P.op("ve", lambda v: v.tensor_tensor(
                    q_lo[:], qs[:], q_hi[:].bitcast(F32), op=OP.subtract),
                     reads=[buf(qs.name), buf(q_hi.name)], writes=[buf(q_lo.name)])

                for hh, ps_O in ((0, ps_OA), (1, ps_OB)):
                    kh, kl = k_hi, k_lo
                    P.op("sy", lambda s, hp=hp, hh=hh: s.dma_start(
                        out=kraw[:], in_=g["k_hbm"][2 * hp + hh:2 * hp + hh + 1,
                                                    :, :].squeeze(0)),
                         reads=[buf("hbm_k_hbm")], writes=[buf(kraw.name)],
                         dma="dsy")
                    P.op("ve", lambda v: v.tensor_copy(k_hi[:], kraw[:]),
                         reads=[buf(kraw.name)], writes=[buf(k_hi.name)])
                    P.op("ve", lambda v: v.tensor_tensor(
                        k_lo[:], kraw[:], k_hi[:].bitcast(F32), op=OP.subtract),
                         reads=[buf(kraw.name), buf(k_hi.name)],
                         writes=[buf(k_lo.name)])
                    for kt in range(32):
                        ps = ps_S[kt % 2]
                        psb = buf(ps.name)

                        def emit(te, kh=kh, kl=kl, kt=kt, hh=hh, ps=ps):
                            return _mm3(
                                te, ps[:],
                                kh[:, kt * 128:(kt + 1) * 128],
                                kl[:, kt * 128:(kt + 1) * 128],
                                q_hi[:, hh * NSUPQ:(hh + 1) * NSUPQ],
                                q_lo[:, hh * NSUPQ:(hh + 1) * NSUPQ],
                                True, True)
                        P.op("te", emit,
                             reads=[buf(kh.name), buf(kl.name),
                                    buf(q_hi.name), buf(q_lo.name)],
                             writes=[psb])
                        pt = ptile[kt % 2]
                        P.op("sc", lambda s, ps=ps, pt=pt: s.activation(
                            pt[:], ps[:], AF.Exp),
                             reads=[psb], writes=[buf(pt.name)])
                        P.op("te", lambda te, kt=kt, hh=hh, pt=pt, ps_O=ps_O:
                             te.matmul(
                                 ps_O[:],
                                 vext[:, kt * 130 + hh * 65:
                                      kt * 130 + (hh + 1) * 65],
                                 pt[:], start=(kt == 0), stop=(kt == 31),
                                 skip_group_check=True),
                             reads=[buf(pt.name), buf(vext.name)],
                             writes=[buf(ps_O.name)])
                    P.op("ve", lambda v, ps_O=ps_O: v.reciprocal(
                        dnr[:], ps_O[64:65, :]),
                         reads=[buf(ps_O.name)], writes=[buf(dnr.name)])
                    P.op("te", lambda te: te.matmul(
                        ps_bc[:], cx.onescol[0:1, 0:64], dnr[:],
                        start=True, stop=True),
                         reads=[buf(dnr.name), buf(cx.onescol.name)],
                         writes=[buf(ps_bc.name)])
                    P.op("ve", lambda v: v.tensor_copy(dbc[:], ps_bc[:]),
                         reads=[buf(ps_bc.name)], writes=[buf(dbc.name)])
                    hidx = 2 * hp + hh
                    P.op("ve", lambda v, hidx=hidx, ps_O=ps_O: v.tensor_tensor(
                        o_heads[:, hidx * NSUPQ:(hidx + 1) * NSUPQ],
                        ps_O[0:64, :], dbc[:], op=OP.mult),
                         reads=[buf(ps_O.name), buf(dbc.name)],
                         writes=[buf(o_heads.name)])

            P.op("sy", lambda s, sup=sup: s.dma_start(
                out=x2own[:].rearrange("p (c t) -> p c t", c=C),
                in_=g["x2_hbm"][:].rearrange("p (c t) -> p c t", c=C)
                [:, :, sup * NSUPQ:(sup + 1) * NSUPQ]),
                 reads=[buf("hbm_x2")], writes=[buf(x2own.name)], dma="dsy")
            for co in range(C):
                ps = ps_o[co % 2]
                psb = buf(ps.name)
                wot = wo[co % 2]
                P.op("sy", lambda s, co=co, wot=wot: s.dma_start(
                    out=wot[:].rearrange("p (h n) -> p h n", h=H),
                    in_=g["l1_wo"][:].rearrange("(h p) n -> p h n", p=64)
                    [:, :, co * 128:(co + 1) * 128]),
                     reads=[buf("hbm_l1_wo")], writes=[buf(wot.name)], dma="dsy")

                def emit(te, co=co, ps=ps, wot=wot):
                    i = None
                    for hh in range(H):
                        i = te.matmul(ps[:],
                                      wot[:, hh * 128:(hh + 1) * 128],
                                      o_heads[:, hh * NSUPQ:(hh + 1) * NSUPQ],
                                      start=(hh == 0), stop=(hh == H - 1))
                    return i
                P.op("te", emit,
                     reads=[buf(wot.name), buf(o_heads.name)], writes=[psb])
                P.op("ve", lambda v, co=co, ps=ps: v.scalar_tensor_tensor(
                    x3t[:, co * NSUPQ:(co + 1) * NSUPQ], ps[:],
                    cx.bpp1[:, 3, co:co + 1],
                    x2own[:, co * NSUPQ:(co + 1) * NSUPQ],
                    op0=OP.add, op1=OP.add),
                     reads=[psb, buf(x2own.name), buf(cx.bpp1.name)],
                     writes=[buf(x3t.name)])
            P.op("gp", lambda gp, sup=sup: gp.dma_start(
                out=g["x3_hbm"][:].rearrange("p (c t) -> p c t", c=C)
                [:, :, sup * NSUPQ:(sup + 1) * NSUPQ],
                in_=x3t[:].rearrange("p (c t) -> p c t", c=C)),
                 reads=[buf(x3t.name)], writes=[buf("hbm_x3")], dma="dgp")
            cx.dbg_sl("dbg_x3", np.s_[:, sup * NSUPQ * C:(sup + 1) * NSUPQ * C],
                      x3t[:], [buf(x3t.name)])


def _phase4(cx):
    nc, P, g, buf = cx.nc, cx.P, cx.g, cx.buf
    import contextlib
    NK2 = FF1 // 128

    with contextlib.ExitStack() as ph:
        wsl = [_sb(nc, ph, f"p4_wsl{i}", [128, FF1]) for i in range(2)]
        acc = _sb(nc, ph, "p4_acc", [128, 8])
        scl = _sb(nc, ph, "p4_scl", [128, 4])
        rscl = _sb(nc, ph, "p4_rscl", [128, 4])
        junk = _sb(nc, ph, "p4_junk", [128, FF1])
        nint = _sb(nc, ph, "p4_ni", [128, FF1], I32)
        nf = _sb(nc, ph, "p4_nf", [128, FF1])
        dd = _sb(nc, ph, "p4_dd", [128, FF1])
        cc = _sb(nc, ph, "p4_cc", [128, FF1])
        wi8 = _sb(nc, ph, "p4_wi8", [128, FF1], BF16)
        ps_s = _ps(nc, ph, "p4_ps_s", [128, 8])

        P.op("ve", lambda v: v.memset(rscl[:], 0.0), writes=[buf(rscl.name)])
        P.op("ve", lambda v: v.memset(scl[:], 0.0), writes=[buf(scl.name)])
        specs = [("f1_wg", C, FF1, 0), ("f1_wu", C, FF1, 1), ("f1_wd", NK2, D, 2)]
        for wname, nslab, ww, col in specs:
            for n in range(nslab):
                w = wsl[n % 2]
                bw = buf(w.name)
                P.op("sy", lambda s, wname=wname, n=n, w=w, ww=ww: s.dma_start(
                    out=w[:, 0:ww], in_=g[wname][n * 128:(n + 1) * 128, :]),
                     reads=[buf("hbm_" + wname)], writes=[bw], dma="dsy")
                P.op("sc", lambda s, w=w, ww=ww, col=col, n=n: s.activation(
                    junk[:, 0:ww], w[:, 0:ww], AF.Abs,
                    accum_out=acc[:, 4 + col:5 + col]),
                     reads=[bw], writes=[buf(junk.name), buf(acc.name)])
                if n == 0:
                    P.op("ve", lambda v, col=col: v.tensor_copy(
                        acc[:, col:col + 1], acc[:, 4 + col:5 + col]),
                         reads=[buf(acc.name)], writes=[buf(acc.name)])
                else:
                    P.op("ve", lambda v, col=col: v.tensor_tensor(
                        acc[:, col:col + 1], acc[:, col:col + 1],
                        acc[:, 4 + col:5 + col], op=OP.add),
                         reads=[buf(acc.name)], writes=[buf(acc.name)])
        for wname, nslab, ww, col in specs:
            P.op("te", lambda te, col=col: te.matmul(
                ps_s[:, col:col + 1], cx.ones[:], acc[:, col:col + 1],
                start=True, stop=True),
                 reads=[buf(acc.name), buf(cx.ones.name)], writes=[buf(ps_s.name)])
            nel = float(nslab * 128 * ww)
            P.op("ve", lambda v, col=col, nel=nel: v.tensor_scalar(
                rscl[:, col:col + 1], ps_s[:, col:col + 1], 1.0 / nel, 1e-5,
                op0=OP.mult, op1=OP.max),
                 reads=[buf(ps_s.name)], writes=[buf(rscl.name)])
            P.op("ve", lambda v, col=col: v.reciprocal(
                scl[:, col:col + 1], rscl[:, col:col + 1]),
                 reads=[buf(rscl.name)], writes=[buf(scl.name)])
        for wname, nslab, ww, col in specs:
            for n in range(nslab):
                w = wsl[n % 2]
                bw = buf(w.name)
                P.op("sy", lambda s, wname=wname, n=n, w=w, ww=ww: s.dma_start(
                    out=w[:, 0:ww], in_=g[wname][n * 128:(n + 1) * 128, :]),
                     reads=[buf("hbm_" + wname)], writes=[bw], dma="dsy")
                P.op("ve", lambda v, w=w, ww=ww, col=col: v.tensor_scalar(
                    junk[:, 0:ww], w[:, 0:ww], scl[:, col:col + 1], None,
                    op0=OP.mult),
                     reads=[bw, buf(scl.name)], writes=[buf(junk.name)])
                P.op("sc", lambda s, ww=ww: s.activation(
                    nint[:, 0:ww], junk[:, 0:ww], AF.Copy),
                     reads=[buf(junk.name)], writes=[buf(nint.name)])
                P.op("sc", lambda s, ww=ww: s.activation(
                    nf[:, 0:ww], nint[:, 0:ww], AF.Copy),
                     reads=[buf(nint.name)], writes=[buf(nf.name)])
                P.op("ve", lambda v, ww=ww: v.tensor_tensor(
                    dd[:, 0:ww], junk[:, 0:ww], nf[:, 0:ww], op=OP.subtract),
                     reads=[buf(junk.name), buf(nf.name)], writes=[buf(dd.name)])
                P.op("ve", lambda v, ww=ww: v.tensor_scalar(
                    cc[:, 0:ww], dd[:, 0:ww], 0.5, None, op0=OP.is_gt),
                     reads=[buf(dd.name)], writes=[buf(cc.name)])
                P.op("ve", lambda v, ww=ww: v.scalar_tensor_tensor(
                    cc[:, 0:ww], dd[:, 0:ww], -0.5, cc[:, 0:ww],
                    op0=OP.is_lt, op1=OP.subtract),
                     reads=[buf(dd.name), buf(cc.name)], writes=[buf(cc.name)])
                P.op("ve", lambda v, ww=ww: v.tensor_tensor(
                    junk[:, 0:ww], nf[:, 0:ww], cc[:, 0:ww], op=OP.subtract),
                     reads=[buf(nf.name), buf(cc.name)], writes=[buf(junk.name)])
                P.op("ve", lambda v, ww=ww: v.tensor_scalar(
                    wi8[:, 0:ww], junk[:, 0:ww], 1.0, -1.0,
                    op0=OP.min, op1=OP.max),
                     reads=[buf(junk.name)], writes=[buf(wi8.name)])
                if col < 2:
                    P.op("gp", lambda gp, col=col, n=n: gp.dma_start(
                        out=g["wint_hbm"][n * 128:(n + 1) * 128,
                                          col * FF1:(col + 1) * FF1],
                        in_=wi8[:, 0:FF1]),
                         reads=[buf(wi8.name)], writes=[buf("hbm_wint")],
                         dma="dgp")
                else:
                    P.op("gp", lambda gp, n=n: gp.dma_start(
                        out=g["wdint_hbm"][n * 128:(n + 1) * 128, :],
                        in_=wi8[:, 0:D]),
                         reads=[buf(wi8.name)], writes=[buf("hbm_wdint")],
                         dma="dgp")
        P.op("gp", lambda gp: gp.dma_start(
            out=g["mr_hbm"][0:1, 0:4], in_=rscl[0:1, 0:4]),
             reads=[buf(rscl.name)], writes=[buf("hbm_mr")], dma="dgp")
    P.fence()

    NSUP = 256
    NTP = 2
    with contextlib.ExitStack() as ph:
        x3sup = _sb(nc, ph, "p4_x3s", [128, C * NSUP])
        xqaT_hi = _sb(nc, ph, "p4_xth", [128, NTP * C * 128], F32R)
        xqaT_lo = _sb(nc, ph, "p4_xtl", [128, NTP * C * 128], F32R)
        hsup = _sb(nc, ph, "p4_hs", [128, NTP * FF1])
        wsl_i8 = [_sb(nc, ph, "p4_wsli0", [128, C * 512], BF16)] * 2
        wsl_r = [_sb(nc, ph, "p4_wslr0", [128, C * 512], F32R)] * 2
        wdbf = [_sb(nc, ph, f"p4_wdbf{i}", [128, NK2 * 128], BF16) for i in range(2)]
        wdstage = _sb(nc, ph, "p4_wdst", [128, NK2 * 128], F32R)
        hqT_hi = _sb(nc, ph, "p4_hqh", [128, NK2 * NSUP], F32R)
        hqT_lo = _sb(nc, ph, "p4_hql", [128, NK2 * NSUP], F32R)
        rscl = _sb(nc, ph, "p4m_rscl", [128, 4])
        amax = _sb(nc, ph, "p4_am", [128, 4])
        u_s = _sb(nc, ph, "p4_u", [128, 512])
        nf_s = _sb(nc, ph, "p4_nfs", [128, 512])
        ni_s = _sb(nc, ph, "p4_nis", [128, 512], I32)
        cc_s = _sb(nc, ph, "p4_ccs", [128, 512])
        xn4 = _sb(nc, ph, "p4_xn4", [128, D])
        xqa = _sb(nc, ph, "p4_xqa", [128, D])
        junk = xn4
        ytile = _sb(nc, ph, "p4_y", [128, NTP * D])
        ps_x3 = _ps(nc, ph, "p4_ps_x3", [128, D])
        ps_tp = _ps(nc, ph, "p4_ps_t", [128, 512])
        ps_gu = [_ps(nc, ph, f"p4_ps_gu{i}", [128, 512]) for i in range(2)]
        ps_dn = [_ps(nc, ph, f"p4_ps_dn{i}", [128, NSUP]) for i in range(2)]

        P.op("sy", lambda s: s.dma_start(
            out=rscl[0:1, :], in_=g["mr_hbm"][0:1, 0:4]),
             reads=[buf("hbm_mr")], writes=[buf(rscl.name)], dma="dsy")
        P.op("te", lambda te: te.matmul(ps_x3[:, 0:4], cx.onescol[0:1, :],
                                        rscl[0:1, :], start=True, stop=True),
             reads=[buf(rscl.name), buf(cx.onescol.name)], writes=[buf(ps_x3.name)])
        P.op("ve", lambda v: v.tensor_copy(rscl[:], ps_x3[:, 0:4]),
             reads=[buf(ps_x3.name)], writes=[buf(rscl.name)])

        def act_quant(src_ap, dst_ap, Wn, bsrc, bdst, scol):
            P.op("ve", lambda v: v.tensor_scalar(
                u_s[:, 0:Wn], src_ap, amax[:, scol:scol + 1], None, op0=OP.mult),
                 reads=[bsrc, buf(amax.name)], writes=[buf(u_s.name)])
            P.op("sc", lambda s: s.activation(ni_s[:, 0:Wn], u_s[:, 0:Wn],
                                              AF.Copy),
                 reads=[buf(u_s.name)], writes=[buf(ni_s.name)])
            P.op("sc", lambda s: s.activation(nf_s[:, 0:Wn], ni_s[:, 0:Wn],
                                              AF.Copy),
                 reads=[buf(ni_s.name)], writes=[buf(nf_s.name)])
            P.op("ve", lambda v: v.tensor_tensor(
                u_s[:, 0:Wn], u_s[:, 0:Wn], nf_s[:, 0:Wn], op=OP.subtract),
                 reads=[buf(u_s.name), buf(nf_s.name)], writes=[buf(u_s.name)])
            P.op("ve", lambda v: v.tensor_scalar(
                cc_s[:, 0:Wn], u_s[:, 0:Wn], 0.5, None, op0=OP.is_gt),
                 reads=[buf(u_s.name)], writes=[buf(cc_s.name)])
            P.op("ve", lambda v: v.scalar_tensor_tensor(
                cc_s[:, 0:Wn], u_s[:, 0:Wn], -0.5, cc_s[:, 0:Wn],
                op0=OP.is_lt, op1=OP.subtract),
                 reads=[buf(u_s.name), buf(cc_s.name)], writes=[buf(cc_s.name)])
            P.op("ve", lambda v: v.tensor_tensor(
                u_s[:, 0:Wn], nf_s[:, 0:Wn], cc_s[:, 0:Wn], op=OP.subtract),
                 reads=[buf(nf_s.name), buf(cc_s.name)], writes=[buf(u_s.name)])
            P.op("ve", lambda v: v.tensor_scalar(
                dst_ap, u_s[:, 0:Wn], amax[:, scol + 1:scol + 2], None,
                op0=OP.mult),
                 reads=[buf(u_s.name), buf(amax.name)], writes=[bdst])

        for sup in range(T2 // NSUP):
            P.op("sy", lambda s, sup=sup: s.dma_start(
                out=x3sup[:].rearrange("p (c t) -> p c t", c=C),
                in_=g["x3_hbm"][:].rearrange("p (c t) -> p c t", c=C)
                [:, :, sup * NSUP:(sup + 1) * NSUP]),
                 reads=[buf("hbm_x3")], writes=[buf(x3sup.name)], dma="dsy")
            for tp in range(NTP):
                for ci in range(C):
                    P.op("te", lambda te, tp=tp, ci=ci: te.transpose(
                        ps_x3[:, ci * 128:(ci + 1) * 128],
                        x3sup[:, ci * NSUP + tp * 128:ci * NSUP + (tp + 1) * 128],
                        cx.idn[:]),
                         reads=[buf(x3sup.name), buf(cx.idn.name)],
                         writes=[buf(ps_x3.name)])
                P.op("ve", lambda v: v.tensor_reduce(amax[:, 0:1], ps_x3[:],
                                                     axis=AX.X, op=OP.add),
                     reads=[buf(ps_x3.name)], writes=[buf(amax.name)])
                P.op("sc", lambda s: s.activation(junk[:], ps_x3[:], AF.Square,
                                                  accum_out=amax[:, 1:2]),
                     reads=[buf(ps_x3.name)],
                     writes=[buf(junk.name), buf(amax.name)])
                P.op("ve", lambda v: v.tensor_scalar(
                    amax[:, 0:1], amax[:, 0:1], 1.0 / D, None, op0=OP.mult),
                     reads=[buf(amax.name)], writes=[buf(amax.name)])
                P.op("ve", lambda v: v.tensor_tensor(
                    amax[:, 2:3], amax[:, 0:1], amax[:, 0:1], op=OP.mult),
                     reads=[buf(amax.name)], writes=[buf(amax.name)])
                P.op("ve", lambda v: v.scalar_tensor_tensor(
                    amax[:, 2:3], amax[:, 1:2], 1.0 / D, amax[:, 2:3],
                    op0=OP.mult, op1=OP.subtract),
                     reads=[buf(amax.name)], writes=[buf(amax.name)])
                P.op("sc", lambda s: s.activation(amax[:, 2:3], amax[:, 2:3],
                                                  AF.Sqrt, bias=cx.eps_ap),
                     reads=[buf(amax.name), buf(cx.cpool.name)],
                     writes=[buf(amax.name)])
                P.op("ve", lambda v: v.reciprocal(amax[:, 2:3], amax[:, 2:3]),
                     reads=[buf(amax.name)], writes=[buf(amax.name)])
                P.op("ve", lambda v: v.tensor_scalar(
                    xn4[:], ps_x3[:], amax[:, 0:1], amax[:, 2:3],
                    op0=OP.subtract, op1=OP.mult),
                     reads=[buf(ps_x3.name), buf(amax.name)], writes=[buf(xn4.name)])
                P.op("ve", lambda v: v.tensor_reduce(
                    amax[:, 0:1], xn4[:], axis=AX.X, op=OP.max,
                    apply_absolute_value=True),
                     reads=[buf(xn4.name)], writes=[buf(amax.name)])
                P.op("ve", lambda v: v.tensor_scalar(
                    amax[:, 1:2], amax[:, 0:1], 1e-5, None, op0=OP.max),
                     reads=[buf(amax.name)], writes=[buf(amax.name)])
                P.op("ve", lambda v: v.reciprocal(amax[:, 3:4], amax[:, 1:2]),
                     reads=[buf(amax.name)], writes=[buf(amax.name)])
                P.op("ve", lambda v: v.tensor_scalar(
                    amax[:, 0:1], amax[:, 3:4], 127.0, None, op0=OP.mult),
                     reads=[buf(amax.name)], writes=[buf(amax.name)])
                P.op("ve", lambda v: v.reciprocal(amax[:, 1:2], amax[:, 0:1]),
                     reads=[buf(amax.name)], writes=[buf(amax.name)])
                for half in range(2):
                    act_quant(xn4[:, half * 512:(half + 1) * 512],
                              xqa[:, half * 512:(half + 1) * 512], 512,
                              buf(xn4.name), buf(xqa.name), 0)
                for ci in range(C):
                    P.op("te", lambda te, ci=ci: te.transpose(
                        ps_tp[:, 0:128], xqa[:, ci * 128:(ci + 1) * 128],
                        cx.idn[:]),
                         reads=[buf(xqa.name), buf(cx.idn.name)],
                         writes=[buf(ps_tp.name)])
                    off = (tp * C + ci) * 128
                    P.op("ve", lambda v, off=off: v.tensor_copy(
                        xqaT_hi[:, off:off + 128], ps_tp[:, 0:128]),
                         reads=[buf(ps_tp.name)], writes=[buf(xqaT_hi.name)])
                    P.op("ve", lambda v, off=off: v.tensor_tensor(
                        xqaT_lo[:, off:off + 128], ps_tp[:, 0:128],
                        xqaT_hi[:, off:off + 128].bitcast(F32), op=OP.subtract),
                         reads=[buf(ps_tp.name), buf(xqaT_hi.name)],
                         writes=[buf(xqaT_lo.name)])

            for sl in range(FF1 // 512):
                wi = wsl_i8[sl % 2]
                wr = wsl_r[sl % 2]
                for which in range(2):
                    P.op("sy", lambda s, sl=sl, which=which, wi=wi: s.dma_start(
                        out=wi[:].rearrange("p (c n) -> p c n", c=C),
                        in_=g["wint_hbm"][:, which * FF1 + sl * 512:
                                          which * FF1 + (sl + 1) * 512]
                        .rearrange("(c p) n -> p c n", p=128)),
                         reads=[buf("hbm_wint")], writes=[buf(wi.name)], dma="dsy")
                    P.op("ve", lambda v, wi=wi, wr=wr: v.tensor_copy(wr[:], wi[:]),
                         reads=[buf(wi.name)], writes=[buf(wr.name)])
                    for tp in range(NTP):
                        ps = ps_gu[tp % 2]

                        def emit(te, tp=tp, wr=wr, ps=ps):
                            i = None
                            for ci in range(C):
                                off = (tp * C + ci) * 128
                                i = te.matmul(ps[:], xqaT_hi[:, off:off + 128],
                                              wr[:, ci * 512:(ci + 1) * 512],
                                              start=(ci == 0), stop=False)
                                i = te.matmul(ps[:], xqaT_lo[:, off:off + 128],
                                              wr[:, ci * 512:(ci + 1) * 512],
                                              start=False, stop=(ci == C - 1))
                            return i
                        P.op("te", emit,
                             reads=[buf(xqaT_hi.name), buf(xqaT_lo.name),
                                    buf(wr.name)],
                             writes=[buf(ps.name)])
                        dst = hsup[:, tp * FF1 + sl * 512: tp * FF1 + (sl + 1) * 512]
                        if which == 0:
                            P.op("sc", lambda s, ps=ps, dst=dst: s.activation(
                                dst, ps[:], AF.Silu, scale=rscl[:, 0:1]),
                                 reads=[buf(ps.name), buf(rscl.name)],
                                 writes=[buf(hsup.name)])
                        else:
                            P.op("ve", lambda v, ps=ps, dst=dst:
                                 v.scalar_tensor_tensor(
                                     dst, ps[:], rscl[:, 1:2], dst,
                                     op0=OP.mult, op1=OP.mult),
                                 reads=[buf(ps.name), buf(hsup.name),
                                        buf(rscl.name)],
                                 writes=[buf(hsup.name)])

            for tp in range(NTP):
                hrow = hsup[:, tp * FF1:(tp + 1) * FF1]
                P.op("ve", lambda v, hrow=hrow: v.tensor_reduce(
                    amax[:, 0:1], hrow, axis=AX.X, op=OP.max,
                    apply_absolute_value=True),
                     reads=[buf(hsup.name)], writes=[buf(amax.name)])
                P.op("ve", lambda v: v.tensor_scalar(
                    amax[:, 1:2], amax[:, 0:1], 1e-5, None, op0=OP.max),
                     reads=[buf(amax.name)], writes=[buf(amax.name)])
                P.op("ve", lambda v: v.reciprocal(amax[:, 3:4], amax[:, 1:2]),
                     reads=[buf(amax.name)], writes=[buf(amax.name)])
                P.op("ve", lambda v: v.tensor_scalar(
                    amax[:, 0:1], amax[:, 3:4], 127.0, None, op0=OP.mult),
                     reads=[buf(amax.name)], writes=[buf(amax.name)])
                P.op("ve", lambda v: v.reciprocal(amax[:, 1:2], amax[:, 0:1]),
                     reads=[buf(amax.name)], writes=[buf(amax.name)])
                for q8 in range(FF1 // 512):
                    act_quant(hrow[:, q8 * 512:(q8 + 1) * 512],
                              hrow[:, q8 * 512:(q8 + 1) * 512], 512,
                              buf(hsup.name), buf(hsup.name), 0)
                for k2 in range(NK2):
                    P.op("te", lambda te, k2=k2, hrow=hrow: te.transpose(
                        ps_tp[:, 0:128], hrow[:, k2 * 128:(k2 + 1) * 128],
                        cx.idn[:]),
                         reads=[buf(hsup.name), buf(cx.idn.name)],
                         writes=[buf(ps_tp.name)])
                    off = k2 * NSUP + tp * 128
                    P.op("ve", lambda v, off=off: v.tensor_copy(
                        hqT_hi[:, off:off + 128], ps_tp[:, 0:128]),
                         reads=[buf(ps_tp.name)], writes=[buf(hqT_hi.name)])
                    P.op("ve", lambda v, off=off: v.tensor_tensor(
                        hqT_lo[:, off:off + 128], ps_tp[:, 0:128],
                        hqT_hi[:, off:off + 128].bitcast(F32), op=OP.subtract),
                         reads=[buf(ps_tp.name), buf(hqT_hi.name)],
                         writes=[buf(hqT_lo.name)])
            for co in range(C):
                wdb = wdbf[co % 2]
                P.op("sy", lambda s, co=co, wdb=wdb: s.dma_start(
                    out=wdb[:].rearrange("p (k n) -> p k n", k=NK2),
                    in_=g["wdint_hbm"][:, co * 128:(co + 1) * 128]
                    .rearrange("(k p) n -> p k n", p=128)),
                     reads=[buf("hbm_wdint")], writes=[buf(wdb.name)], dma="dsy")
                P.op("ve", lambda v, wdb=wdb: v.tensor_copy(wdstage[:], wdb[:]),
                     reads=[buf(wdb.name)], writes=[buf(wdstage.name)])
                ps = ps_dn[co % 2]

                def emit(te, co=co, ps=ps):
                    i = None
                    for k2 in range(NK2):
                        st = wdstage[:, k2 * 128:(k2 + 1) * 128]
                        i = te.matmul(ps[:], st, hqT_hi[:, k2 * NSUP:(k2 + 1) * NSUP],
                                      start=(k2 == 0), stop=False)
                        i = te.matmul(ps[:], st, hqT_lo[:, k2 * NSUP:(k2 + 1) * NSUP],
                                      start=False, stop=(k2 == NK2 - 1))
                    return i
                P.op("te", emit,
                     reads=[buf(wdstage.name), buf(hqT_hi.name),
                            buf(hqT_lo.name)],
                     writes=[buf(ps.name)])
                P.op("ve", lambda v, co=co, ps=ps: v.scalar_tensor_tensor(
                    xn4[:, 0:NSUP], ps[:], rscl[:, 2:3],
                    x3sup[:, co * NSUP:(co + 1) * NSUP],
                    op0=OP.mult, op1=OP.add),
                     reads=[buf(ps.name), buf(x3sup.name), buf(rscl.name)],
                     writes=[buf(xn4.name)])
                for tp in range(NTP):
                    P.op("te", lambda te, tp=tp: te.transpose(
                        ps_tp[:, 128:256], xn4[:, tp * 128:(tp + 1) * 128],
                        cx.idn[:]),
                         reads=[buf(xn4.name), buf(cx.idn.name)],
                         writes=[buf(ps_tp.name)])
                    P.op("ve", lambda v, tp=tp, co=co: v.tensor_copy(
                        ytile[:, tp * D + co * 128:tp * D + (co + 1) * 128],
                        ps_tp[:, 128:256]),
                         reads=[buf(ps_tp.name)], writes=[buf(ytile.name)])
            P.op("gp", lambda gp, sup=sup: gp.dma_start(
                out=g["y_tm"][sup * NSUP:(sup + 1) * NSUP, :]
                .rearrange("(tp p) n -> p tp n", p=128),
                in_=ytile[:].rearrange("p (tp n) -> p tp n", tp=NTP)),
                 reads=[buf(ytile.name)], writes=[buf("hbm_y")], dma="dgp")



def kernel(**inputs):
    in_maps = prepare_inputs(inputs)
    stage = _DEBUG.get("stage")
    nc = build_program(stage)
    res = run_bass_kernel_spmd(nc, in_maps, core_ids=list(range(NCORES)))
    if stage is not None:
        return res
    out = np.zeros((4, T4, D), np.float32)
    for c in range(NCORES):
        b, h = c // 2, c % 2
        out[b, h * T2:(h + 1) * T2, :] = res.results[c]["y_tm"]
    return out


# version 07
# speedup vs baseline: 1.0075x; 1.0038x over previous
import numpy as np

import concourse.bass as bass
import concourse.mybir as mybir
from concourse.alu_op_type import AluOpType
from concourse.bass_utils import run_bass_kernel_spmd

F32 = mybir.dt.float32
F32R = mybir.dt.float32r
I32 = mybir.dt.int32
I8 = mybir.dt.int8
BF16 = mybir.dt.bfloat16
AF = mybir.ActivationFunctionType
AX = mybir.AxisListType
OP = AluOpType

D = 1024
C = 8
H = 16
DH = 64
T4 = 4096
T2 = 2048
FF0 = 2816
FF1 = 4096
NCORES = 8

_DEBUG = {}



class Buf:
    __slots__ = ("w", "r", "tag")

    def __init__(self, tag=""):
        self.w = None
        self.r = []
        self.tag = tag


class Prog:
    ENG = {"te": "tensor", "ve": "vector", "sc": "scalar",
           "gp": "gpsimd", "sy": "sync"}

    NQ = 8

    def __init__(self, nc, block, sems):
        self.nc = nc
        self.block = block
        self.sem = sems
        self.cnt = {k: 0 for k in sems}
        self.wm = {}
        self.eng_sem = {"te": "te", "ve": "ve", "sc": "sc", "gp": "gp"}
        self.qcnt = {"dsy": 0, "dgp": 0}

    def op(self, eng, emit, reads=(), writes=(), dma=None):
        need = {}

        def add(h):
            if h is not None:
                s, v = h
                if need.get(s, 0) < v:
                    need[s] = v

        for b in reads:
            add(b.w)
        for b in writes:
            add(b.w)
            for h in b.r:
                add(h)
        if dma is not None:
            i = self.qcnt[dma]
            self.qcnt[dma] = i + 1
            semname = f"{dma}{i % self.NQ}"
            inc = 16
            if i >= self.NQ:
                add((semname, 16 * (i // self.NQ)))
        else:
            semname, inc = self.eng_sem[eng], 1
        waits = [(s, v) for s, v in need.items()
                 if self.wm.get((eng, s), 0) < v]
        semh = self.sem[semname]

        def body(proxy):
            for s, v in waits:
                proxy.wait_ge(self.sem[s], v)
            inst = emit(proxy)
            if inst is not None:
                inst.then_inc(semh, inc)

        getattr(self.block, self.ENG[eng])(body)
        for s, v in waits:
            self.wm[(eng, s)] = v
        self.cnt[semname] += inc
        h = (semname, self.cnt[semname])
        for b in reads:
            b.r.append(h)
        for b in writes:
            b.w = h
            b.r = []
        return h

    def fence(self):
        vals = [(s, v) for s, v in self.cnt.items() if v > 0]
        for eng in self.ENG:
            waits = [(s, v) for s, v in vals if self.wm.get((eng, s), 0) < v]
            if not waits:
                continue

            def body(proxy, waits=waits):
                for s, v in waits:
                    proxy.wait_ge(self.sem[s], v)

            getattr(self.block, self.ENG[eng])(body)
            for s, v in waits:
                self.wm[(eng, s)] = v

    def final_barrier(self, eng="gp"):
        vals = [(s, v) for s, v in self.cnt.items()
                if v and (s.startswith("dgp") or s.startswith("dsy"))]

        def body(proxy):
            for s, v in vals:
                proxy.wait_ge(self.sem[s], v)

        getattr(self.block, self.ENG[eng])(body)



_LIBM = None


def _rope_tables_tm(pos, scale):
    global _LIBM
    if _LIBM is None:
        import ctypes
        _LIBM = ctypes.CDLL("libm.so.6")
        _LIBM.cosf.restype = ctypes.c_float
        _LIBM.cosf.argtypes = [ctypes.c_float]
        _LIBM.sinf.restype = ctypes.c_float
        _LIBM.sinf.argtypes = [ctypes.c_float]
    f32 = np.float32
    e32 = np.arange(0, DH, 2, dtype=f32) / f32(DH)
    p = (10000.0 ** e32.astype(np.float64)).astype(f32)
    inv = (f32(1.0) / p).astype(f32)
    ang = (np.asarray(pos, f32)[:, None] * inv[None, :]).astype(f32)
    flat = ang.ravel()
    c = np.array([_LIBM.cosf(float(x)) for x in flat], f32).reshape(ang.shape)
    s = np.array([_LIBM.sinf(float(x)) for x in flat], f32).reshape(ang.shape)
    c = (c * f32(scale)).astype(f32)
    s = (s * f32(scale)).astype(f32)
    return np.ascontiguousarray(
        np.concatenate([c, c, -s, s], 1).astype(f32))


def _perm_rope_cols(w_fm, b):
    perm = np.concatenate([np.arange(0, DH, 2), np.arange(1, DH, 2)])
    full = np.concatenate([h * DH + perm for h in range(H)])
    return np.ascontiguousarray(w_fm[:, full]), np.ascontiguousarray(b[full])


def _bias_pp(biases):
    nb = len(biases)
    out = np.zeros((128, nb, C), np.float32)
    for i, b in enumerate(biases):
        out[:, i, :] = b.reshape(C, 128).T
    return out


def prepare_inputs(inp):
    f32 = np.float32
    x = np.ascontiguousarray(np.asarray(inp["L0_attn_x"], f32))

    common = {}
    for nm, key in [("l0_wq", "L0_attn_Wq"), ("l0_wk", "L0_attn_Wk"),
                    ("l0_wv", "L0_attn_Wv"), ("l0_wo", "L0_attn_Wo")]:
        common[nm] = np.ascontiguousarray(np.asarray(inp[key], f32))
    l0b = [np.asarray(inp[k], f32) for k in
           ("L0_attn_bq", "L0_attn_bk", "L0_attn_bv", "L0_attn_bo")]
    common["l0_bpp"] = _bias_pp(l0b)
    common["l0_brow"] = np.concatenate(l0b).reshape(1, 4 * D).astype(f32)

    common["f0_wg"] = np.ascontiguousarray(np.asarray(inp["L0_ffn_w_gate"], f32))
    common["f0_wu"] = np.ascontiguousarray(np.asarray(inp["L0_ffn_w_up"], f32))
    common["f0_wd"] = np.ascontiguousarray(np.asarray(inp["L0_ffn_w_down"], f32))

    wq1, bq1 = _perm_rope_cols(np.asarray(inp["L1_attn_Wq"], f32).T,
                               np.asarray(inp["L1_attn_bq"], f32))
    wk1, bk1 = _perm_rope_cols(np.asarray(inp["L1_attn_Wk"], f32).T,
                               np.asarray(inp["L1_attn_bk"], f32))
    common["l1_wq"] = wq1
    common["l1_wk"] = wk1
    common["l1_wv"] = np.ascontiguousarray(np.asarray(inp["L1_attn_Wv"], f32).T)
    common["l1_wo"] = np.ascontiguousarray(np.asarray(inp["L1_attn_Wo"], f32).T)
    bv1 = np.asarray(inp["L1_attn_bv"], f32)
    bo1 = np.asarray(inp["L1_attn_bo"], f32)
    common["l1_bpp"] = _bias_pp([bq1, bk1, bv1, bo1])
    common["l1_brow"] = np.concatenate([bq1, bk1, bv1, bo1]).reshape(1, 4 * D).astype(f32)

    common["f1_wg"] = np.ascontiguousarray(np.asarray(inp["L1_ffn_w_gate"], f32).T)
    common["f1_wu"] = np.ascontiguousarray(np.asarray(inp["L1_ffn_w_up"], f32).T)
    common["f1_wd"] = np.ascontiguousarray(np.asarray(inp["L1_ffn_w_down"], f32).T)

    common["idn"] = np.eye(128, dtype=f32)
    common["ones128"] = np.ones((128, 128), f32)
    common["onescol"] = np.ones((1, 128), f32)
    cp = np.zeros((128, 8), f32)
    cp[:, 0] = 1e-5
    cp[:, 1] = 1.0 / 1024
    common["cpool"] = cp
    common["signmask"] = np.full((128, 1), -2147483648, np.int32)

    in_maps = []
    for c in range(NCORES):
        b, h = c // 2, c % 2
        m = dict(common)
        xb = x[b]
        if h == 1:
            xb = np.concatenate([xb[T2:], xb[:T2]], 0)
        m["x_fm"] = np.ascontiguousarray(xb.T)
        pos = np.arange(T4)
        if h == 1:
            pos = np.concatenate([pos[T2:], pos[:T2]])
        m["ropek"] = np.ascontiguousarray(_rope_tables_tm(pos, 1.0))
        m["ropeq"] = np.ascontiguousarray(_rope_tables_tm(pos[:T2], 0.125))
        in_maps.append(m)
    return in_maps



_uid = [0]


def _nm(base):
    _uid[0] += 1
    return f"{base}_{_uid[0]}"


class Ctx:
    pass


def build_program(stage=None):
    nc = bass.Bass("TRN2", target_bir_lowering=False)
    g = {}

    def din(name, shape, dt=F32):
        g[name] = nc.dram_tensor(name, list(shape), dt, kind="ExternalInput").ap()

    def dout(name, shape, dt=F32):
        g[name] = nc.dram_tensor(name, list(shape), dt, kind="ExternalOutput").ap()

    def dscr(name, shape, dt=F32):
        g[name] = nc.dram_tensor(name, list(shape), dt).ap()

    din("x_fm", [D, T4])
    for nm in ("l0_wq", "l0_wk", "l0_wv", "l0_wo",
               "l1_wq", "l1_wk", "l1_wv", "l1_wo"):
        din(nm, [D, D])
    din("l0_bpp", [128, 4, C]); din("l1_bpp", [128, 4, C])
    din("l0_brow", [1, 4 * D]); din("l1_brow", [1, 4 * D])
    din("f0_wg", [FF0, D]); din("f0_wu", [FF0, D]); din("f0_wd", [D, FF0])
    din("f1_wg", [D, FF1]); din("f1_wu", [D, FF1]); din("f1_wd", [FF1, D])
    din("idn", [128, 128]); din("ones128", [128, 128]); din("onescol", [1, 128])
    din("cpool", [128, 8]); din("signmask", [128, 1], I32)
    din("ropek", [T4, 128]); din("ropeq", [T2, 128])

    dout("y_tm", [T2, D])
    for nm, shape in _DEBUG.get("outs", {}).items():
        dout(nm, shape)

    dscr("x1_hbm", [128, C * T4])
    dscr("mr_hbm", [2, T4])
    dscr("x2_hbm", [128, C * T4])
    dscr("x3_hbm", [128, C * T2])
    dscr("k_hbm", [H, DH, T4])
    dscr("q_hbm", [H, DH, T2])
    dscr("v_hbm", [C, T4, 128])
    dscr("wq_fm_hbm", [128, C * 2 * FF0])
    dscr("wd_fm_hbm", [128, (FF0 // 128) * D])
    dscr("wint_hbm", [D, 2 * FF1], BF16)
    dscr("wdint_hbm", [FF1, D], BF16)

    sems = {}
    ctxs = []
    names = ["te", "ve", "sc", "gp"]
    names += [f"dsy{i}" for i in range(Prog.NQ)]
    names += [f"dgp{i}" for i in range(Prog.NQ)]
    for s in names:
        cm = nc.semaphore("sem_" + s)
        sems[s] = cm.__enter__()
        ctxs.append(cm)

    blk_cm = nc.Block()
    block = blk_cm.__enter__()
    P = Prog(nc, block, sems)

    import contextlib
    with contextlib.ExitStack() as stack:
        _build_phases(nc, P, g, stack, stage)
        P.final_barrier()
    blk_cm.__exit__(None, None, None)
    for cm in reversed(ctxs):
        cm.__exit__(None, None, None)
    return nc


def _sb(nc, stack, name, shape, dt=F32):
    return stack.enter_context(nc.sbuf_tensor(name, list(shape), dt))


def _ps(nc, stack, name, shape, dt=F32):
    return stack.enter_context(nc.psum_tensor(name, list(shape), dt))


def _build_phases(nc, P, g, stack, stage):
    cx = Ctx()
    cx.nc, cx.P, cx.g, cx.stack = nc, P, g, stack
    B = cx.B = {}

    def buf(name):
        if name not in B:
            B[name] = Buf(name)
        return B[name]
    cx.buf = buf

    cx.idn = _sb(nc, stack, "c_idn", [128, 128])
    cx.ones = _sb(nc, stack, "c_ones", [128, 128])
    cx.ones_r = _sb(nc, stack, "c_ones_r", [128, 128], F32R)
    cx.onescol = _sb(nc, stack, "c_onescol", [1, 128])
    cx.cpool = _sb(nc, stack, "c_cpool", [128, 8])
    cx.signmask = _sb(nc, stack, "c_sgn", [128, 1], I32)
    cx.bpp0 = _sb(nc, stack, "c_bpp0", [128, 4, C])
    cx.bpp1 = _sb(nc, stack, "c_bpp1", [128, 4, C])
    for t, src in [(cx.idn, "idn"), (cx.ones, "ones128"), (cx.onescol, "onescol"),
                   (cx.cpool, "cpool"), (cx.signmask, "signmask"),
                   (cx.bpp0, "l0_bpp"), (cx.bpp1, "l1_bpp")]:
        P.op("sy", lambda s, t=t, src=src: s.dma_start(out=t[:], in_=g[src][:]),
             reads=[buf("hbm_" + src)], writes=[buf(t.name)], dma="dsy")
    P.op("ve", lambda v: v.tensor_copy(cx.ones_r[:], cx.ones[:]),
         reads=[buf(cx.ones.name)], writes=[buf(cx.ones_r.name)])

    cx.recip6 = _sb(nc, stack, "c_recip6", [128, 1])
    P.op("ve", lambda v: v.memset(cx.recip6[:], 6.0), writes=[buf("c_recip6")])
    P.op("ve", lambda v: v.reciprocal(cx.recip6[:], cx.recip6[:]),
         reads=[buf("c_recip6")], writes=[buf("c_recip6")])

    cx.eps_ap = cx.cpool[:, 0:1]

    def split(src_ap, hi_t, lo_t, rbuf, hi_ap=None, lo_ap=None):
        hi_ap = hi_ap if hi_ap is not None else hi_t[:]
        lo_ap = lo_ap if lo_ap is not None else lo_t[:]
        hb, lb = buf(hi_t.name), buf(lo_t.name)
        P.op("ve", lambda v: v.tensor_copy(hi_ap, src_ap),
             reads=[rbuf], writes=[hb])
        P.op("ve", lambda v: v.tensor_tensor(lo_ap, src_ap, hi_ap.bitcast(F32),
                                             op=OP.subtract),
             reads=[rbuf, hb], writes=[lb])
    cx.split = split

    def dbg(name, src_ap, reads):
        if name in _DEBUG.get("outs", {}):
            P.op("gp", lambda gp: gp.dma_start(out=g[name][:], in_=src_ap),
                 reads=reads, writes=[buf("dbg_" + name)], dma="dgp")
    cx.dbg = dbg

    def dbg_sl(name, dst_sl, src_ap, reads):
        if name in _DEBUG.get("outs", {}):
            P.op("gp", lambda gp: gp.dma_start(out=g[name][dst_sl], in_=src_ap),
                 reads=reads, writes=[buf("dbg_" + name)], dma="dgp")
    cx.dbg_sl = dbg_sl

    _phase1(cx)
    P.fence()
    if stage == "p1":
        return
    _phase2(cx)
    P.fence()
    if stage == "p2":
        return
    _phase3(cx)
    P.fence()
    if stage == "p3":
        return
    _phase4(cx)



def _fm_chunk(t, c, N):
    return t[:, c * N:(c + 1) * N]


def _ln_fm(cx, xtile, x_hi, x_lo, sq, sq_hi, sq_lo, ps_stats, m_t, r_t, NT,
           cache=None, it=0):
    nc, P, buf, split = cx.nc, cx.P, cx.buf, cx.split
    bx = buf(xtile.name)
    split(xtile[:], x_hi, x_lo, bx)
    P.op("sc", lambda s: s.activation(sq[:], xtile[:], AF.Square),
         reads=[bx], writes=[buf(sq.name)])
    split(sq[:], sq_hi, sq_lo, buf(sq.name))
    psb = buf(ps_stats.name)

    def mm_sum(hi, lo, col):
        def e(te):
            i = None
            for cch in range(C):
                i = te.matmul(ps_stats[:, col:col + NT], cx.ones_r[:],
                              _fm_chunk(hi, cch, NT), start=(cch == 0), stop=False)
                i = te.matmul(ps_stats[:, col:col + NT], cx.ones_r[:],
                              _fm_chunk(lo, cch, NT), start=False,
                              stop=(cch == C - 1))
            return i
        return e
    P.op("te", mm_sum(x_hi, x_lo, 0),
         reads=[buf(x_hi.name), buf(x_lo.name), buf(cx.ones_r.name)], writes=[psb])
    P.op("te", mm_sum(sq_hi, sq_lo, NT),
         reads=[buf(sq_hi.name), buf(sq_lo.name), buf(cx.ones_r.name)], writes=[psb])
    bm, br = buf(m_t.name), buf(r_t.name)
    P.op("ve", lambda v: v.tensor_scalar(m_t[:], ps_stats[:, 0:NT], 1.0 / D, None,
                                         op0=OP.mult), reads=[psb], writes=[bm])
    P.op("ve", lambda v: v.tensor_tensor(r_t[:], m_t[:], m_t[:], op=OP.mult),
         reads=[bm], writes=[br])
    P.op("ve", lambda v: v.scalar_tensor_tensor(r_t[:], ps_stats[:, NT:2 * NT],
                                                1.0 / D, r_t[:],
                                                op0=OP.mult, op1=OP.subtract),
         reads=[psb, br], writes=[br])
    P.op("sc", lambda s: s.activation(r_t[:], r_t[:], AF.Sqrt, bias=cx.eps_ap),
         reads=[br, buf(cx.cpool.name)], writes=[br])
    P.op("ve", lambda v: v.reciprocal(r_t[:], r_t[:]), reads=[br], writes=[br])
    if cache is not None:
        m_cache, r_cache = cache
        P.op("ve", lambda v: v.tensor_copy(m_cache[:, it * NT:(it + 1) * NT],
                                           m_t[0:1, :]),
             reads=[bm], writes=[buf(m_cache.name)])
        P.op("ve", lambda v: v.tensor_copy(r_cache[:, it * NT:(it + 1) * NT],
                                           r_t[0:1, :]),
             reads=[br], writes=[buf(r_cache.name)])


def _apply_ln_fm(cx, xtile, m_t, r_t, xn, NT, m_b=None, r_b=None, m_ap=None,
                 r_ap=None):
    P, buf = cx.P, cx.buf
    m_ap = m_ap if m_ap is not None else m_t[:]
    r_ap = r_ap if r_ap is not None else r_t[:]
    m_b = m_b if m_b is not None else buf(m_t.name)
    r_b = r_b if r_b is not None else buf(r_t.name)
    x3 = xtile[:].rearrange("p (c t) -> p c t", c=C)
    o3 = xn[:].rearrange("p (c t) -> p c t", c=C)
    mb = m_ap.unsqueeze(1).broadcast_to((128, C, NT))
    rb = r_ap.unsqueeze(1).broadcast_to((128, C, NT))
    P.op("ve", lambda v: v.tensor_tensor(o3, x3, mb, op=OP.subtract),
         reads=[buf(xtile.name), m_b], writes=[buf(xn.name)])
    P.op("ve", lambda v: v.tensor_tensor(o3, o3, rb, op=OP.mult),
         reads=[buf(xn.name), r_b], writes=[buf(xn.name)])


def _load_w(cx, ph, wname, tag):
    nc, P, g, buf = cx.nc, cx.P, cx.g, cx.buf
    w = _sb(nc, ph, tag, [128, C * D])
    src = g[wname][:].rearrange("(c p) n -> p c n", p=128)
    dst = w[:].rearrange("p (c n) -> p c n", c=C)
    P.op("sy", lambda s: s.dma_start(out=dst, in_=src),
         reads=[buf("hbm_" + wname)], writes=[buf(w.name)], dma="dsy")
    return w



def _phase1(cx):
    nc, P, g, buf, split = cx.nc, cx.P, cx.g, cx.buf, cx.split
    NT = 512
    NTILE = T4 // NT
    import contextlib
    with contextlib.ExitStack() as ph:
        wA = _load_w(cx, ph, "l0_wk", "p1_wA")
        wB = _load_w(cx, ph, "l0_wv", "p1_wB")

        xt = [_sb(nc, ph, f"p1_x{i}", [128, C * NT]) for i in range(2)]
        S0 = _sb(nc, ph, "p1_S0", [128, C * NT])
        S1 = _sb(nc, ph, "p1_S1", [128, C * NT])
        S2 = _sb(nc, ph, "p1_S2", [128, C * NT])
        SHt = _sb(nc, ph, "p1_SH", [128, C * NT], F32R)
        SLt = _sb(nc, ph, "p1_SL", [128, C * NT], F32R)
        SH = SHt[:]
        SL = SLt[:]
        m_t = _sb(nc, ph, "p1_m", [128, NT])
        r_t = _sb(nc, ph, "p1_r", [128, NT])
        m_row = _sb(nc, ph, "p1_mrow", [1, NT])
        r_row = _sb(nc, ph, "p1_rrow", [1, NT])
        bkv_bc = _sb(nc, ph, "p1_bkv", [128, 2 * D])
        kvd = _sb(nc, ph, "p1_kvd", [128, C * 128])

        ps_stats = _ps(nc, ph, "p1_ps_st", [128, 2 * NT])
        ps_proj = [_ps(nc, ph, f"p1_ps_p{i}", [128, NT]) for i in range(2)]
        ps_kv = [_ps(nc, ph, f"p1_ps_kv{i}", [128, 512]) for i in range(4)]

        ph_brow = contextlib.ExitStack()
        brow0 = _sb(nc, ph_brow, "p1_brow0", [1, D])
        for i, off in enumerate((D, 2 * D)):
            P.op("sy", lambda s, off=off: s.dma_start(
                out=brow0[:], in_=g["l0_brow"][0:1, off:off + D]),
                 reads=[buf("hbm_l0_brow")], writes=[buf(brow0.name)], dma="dsy")
            for half in range(2):
                P.op("te", lambda te, half=half: te.matmul(
                    ps_stats[:, 0:512], cx.onescol[0:1, :],
                    brow0[0:1, half * 512:(half + 1) * 512],
                    start=True, stop=True),
                     reads=[buf(cx.onescol.name), buf(brow0.name)],
                     writes=[buf(ps_stats.name)])
                P.op("ve", lambda v, i=i, half=half: v.tensor_copy(
                    bkv_bc[:, i * D + half * 512: i * D + (half + 1) * 512],
                    ps_stats[:, 0:512]),
                     reads=[buf(ps_stats.name)], writes=[buf(bkv_bc.name)])
        ph_brow.close()

        def hl_split(src_ap, sbuf):
            P.op("ve", lambda v: v.tensor_copy(SH, src_ap),
                 reads=[sbuf], writes=[buf(SHt.name)])
            P.op("ve", lambda v: v.tensor_tensor(SL, src_ap, SH.bitcast(F32),
                                                 op=OP.subtract),
                 reads=[sbuf, buf(SHt.name)], writes=[buf(SLt.name)])

        def fmchunk_ap(ap, cch):
            return ap[:, cch * NT:(cch + 1) * NT]

        def ln_stats(xtile, it, cache=True):
            bx = buf(xtile.name)
            psb = buf(ps_stats.name)

            def mm_sum(col):
                def e(te):
                    i = None
                    for cch in range(C):
                        i = te.matmul(ps_stats[:, col:col + NT], cx.ones_r[:],
                                      fmchunk_ap(SH, cch), start=(cch == 0),
                                      stop=False)
                        i = te.matmul(ps_stats[:, col:col + NT], cx.ones_r[:],
                                      fmchunk_ap(SL, cch), start=False,
                                      stop=(cch == C - 1))
                    return i
                return e
            hl_split(xtile[:], bx)
            P.op("te", mm_sum(0),
                 reads=[buf(SHt.name), buf(SLt.name), buf(cx.ones_r.name)],
                 writes=[psb])
            P.op("sc", lambda s: s.activation(S2[:], xtile[:], AF.Square),
                 reads=[bx], writes=[buf(S2.name)])
            hl_split(S2[:], buf(S2.name))
            P.op("te", mm_sum(NT),
                 reads=[buf(SHt.name), buf(SLt.name), buf(cx.ones_r.name)],
                 writes=[psb])
            bm, br = buf(m_t.name), buf(r_t.name)
            P.op("ve", lambda v: v.tensor_scalar(m_t[:], ps_stats[:, 0:NT],
                                                 1.0 / D, None, op0=OP.mult),
                 reads=[psb], writes=[bm])
            P.op("ve", lambda v: v.tensor_tensor(r_t[:], m_t[:], m_t[:],
                                                 op=OP.mult),
                 reads=[bm], writes=[br])
            P.op("ve", lambda v: v.scalar_tensor_tensor(
                r_t[:], ps_stats[:, NT:2 * NT], 1.0 / D, r_t[:],
                op0=OP.mult, op1=OP.subtract),
                 reads=[psb, br], writes=[br])
            P.op("sc", lambda s: s.activation(r_t[:], r_t[:], AF.Sqrt,
                                              bias=cx.eps_ap),
                 reads=[br, buf(cx.cpool.name)], writes=[br])
            P.op("ve", lambda v: v.reciprocal(r_t[:], r_t[:]),
                 reads=[br], writes=[br])
            if cache:
                P.op("gp", lambda gp, it=it: gp.dma_start(
                    out=g["mr_hbm"][0:1, it * NT:(it + 1) * NT], in_=m_t[0:1, :]),
                     reads=[bm], writes=[buf("hbm_mr")], dma="dgp")
                P.op("gp", lambda gp, it=it: gp.dma_start(
                    out=g["mr_hbm"][1:2, it * NT:(it + 1) * NT], in_=r_t[0:1, :]),
                     reads=[br], writes=[buf("hbm_mr")], dma="dgp")

        def load_x(it, src_name="x_fm"):
            xtile = xt[it % 2]
            bx = buf(xtile.name)
            if src_name == "x_fm":
                src = g["x_fm"][:].rearrange("(c p) t -> p c t", p=128)
            else:
                src = g[src_name][:].rearrange("p (c t) -> p c t", c=C)
            P.op("sy", lambda s, it=it, xtile=xtile, src=src: s.dma_start(
                out=xtile[:].rearrange("p (c t) -> p c t", c=C),
                in_=src[:, :, it * NT:(it + 1) * NT]),
                 reads=[buf("hbm_" + src_name)], writes=[bx], dma="dsy")
            return xtile, bx

        for it in range(NTILE):
            xtile, bx = load_x(it)
            ln_stats(xtile, it)
            xn = S2
            _apply_ln_fm(cx, xtile, m_t, r_t, xn, NT)

            ktile, vtile = S0, S1
            for which, w, otile in [(0, wA, ktile), (1, wB, vtile)]:
                for tp in range(4):
                    for dh2 in range(2):
                        ps = ps_proj[(tp * 2 + dh2) % 2]
                        psb = buf(ps.name)

                        def emit(te, tp=tp, dh2=dh2, w=w, ps=ps):
                            i = None
                            for ci in range(C):
                                st = _fm_chunk(xn, ci, NT)[:, tp * 128:(tp + 1) * 128]
                                mv = _fm_chunk(w, ci, D)[:, dh2 * 512:(dh2 + 1) * 512]
                                i = te.matmul(ps[:], st, mv, start=(ci == 0),
                                              stop=(ci == C - 1))
                            return i
                        P.op("te", emit,
                             reads=[buf(xn.name), buf(w.name)], writes=[psb])
                        dst = otile[:, tp * D + dh2 * 512: tp * D + (dh2 + 1) * 512]
                        bias_ap = bkv_bc[:, which * D + dh2 * 512:
                                         which * D + (dh2 + 1) * 512]
                        P.op("ve", lambda v, ps=ps, dst=dst, bias_ap=bias_ap:
                             v.tensor_tensor(dst, ps[:], bias_ap, op=OP.add),
                             reads=[psb, buf(bkv_bc.name)], writes=[buf(otile.name)])
                        if which == 0:
                            P.op("ve", lambda v, dst=dst: v.tensor_scalar(
                                dst, dst, 0.0, None, op0=OP.max),
                                 reads=[buf(otile.name)], writes=[buf(otile.name)])

            for tp in range(4):
                for hp in range(C):
                    ps = ps_kv[hp // 2]
                    col = (hp % 2) * 256
                    vblk = (hp // 2) * 256

                    def emit(te, tp=tp, hp=hp, ps=ps, col=col, vblk=vblk, it=it):
                        first = (it == 0 and tp == 0 and hp % 2 == 0)
                        last = (it == NTILE - 1 and tp == 3)
                        st = ktile[:, tp * D + hp * 128: tp * D + (hp + 1) * 128]
                        mv = vtile[:, tp * D + vblk: tp * D + vblk + 256]
                        return te.matmul(ps[:, col:col + 256], st, mv,
                                         start=first, stop=last,
                                         skip_group_check=True)
                    P.op("te", emit,
                         reads=[buf(ktile.name), buf(vtile.name)],
                         writes=[buf(ps.name)])

        P.op("ve", lambda v: v.memset(kvd[:], 0.0), writes=[buf(kvd.name)])
        sc_kv = 1.0 / (T4 * DH)
        for hp in range(C):
            ps = ps_kv[hp // 2]
            col = (hp % 2) * 256
            vin = (hp % 2) * 128
            P.op("ve", lambda v, hp=hp, ps=ps, col=col, vin=vin: v.tensor_scalar(
                kvd[0:64, hp * 128:hp * 128 + 64],
                ps[0:64, col + vin:col + vin + 64], sc_kv, None, op0=OP.mult),
                 reads=[buf(ps.name)], writes=[buf(kvd.name)])
            P.op("ve", lambda v, hp=hp, ps=ps, col=col, vin=vin: v.tensor_scalar(
                kvd[64:128, hp * 128 + 64:(hp + 1) * 128],
                ps[64:128, col + vin + 64:col + vin + 128], sc_kv, None,
                op0=OP.mult),
                 reads=[buf(ps.name)], writes=[buf(kvd.name)])
        cx.dbg("dbg_kvd", kvd[:], [buf(kvd.name)])

        for wname, w in [("l0_wq", wA), ("l0_wo", wB)]:
            src = g[wname][:].rearrange("(c p) n -> p c n", p=128)
            dst = w[:].rearrange("p (c n) -> p c n", c=C)
            P.op("sy", lambda s, src=src, dst=dst: s.dma_start(out=dst, in_=src),
                 reads=[buf("hbm_" + wname)], writes=[buf(w.name)], dma="dsy")

        for it in range(NTILE):
            xtile, bx = load_x(it)
            for which, rw in ((0, m_row), (1, r_row)):
                P.op("sy", lambda s, which=which, rw=rw, it=it: s.dma_start(
                    out=rw[:], in_=g["mr_hbm"][which:which + 1,
                                               it * NT:(it + 1) * NT]),
                     reads=[buf("hbm_mr")], writes=[buf(rw.name)], dma="dsy")
                P.op("te", lambda te, which=which, rw=rw: te.matmul(
                    ps_stats[:, which * NT:(which + 1) * NT],
                    cx.onescol[0:1, :], rw[:],
                    start=True, stop=True),
                     reads=[buf(rw.name), buf(cx.onescol.name)],
                     writes=[buf(ps_stats.name)])
            xn = S2
            _apply_ln_fm(cx, xtile, None, None, xn, NT,
                         m_b=buf(ps_stats.name), r_b=buf(ps_stats.name),
                         m_ap=ps_stats[:, 0:NT], r_ap=ps_stats[:, NT:2 * NT])

            q_fm, a_fm, x1t = S0, S1, xtile
            for co in range(C):
                ps = ps_proj[co % 2]
                psb = buf(ps.name)

                def emit(te, co=co, ps=ps):
                    i = None
                    for ci in range(C):
                        st = _fm_chunk(wA, ci, D)[:, co * 128:(co + 1) * 128]
                        mv = _fm_chunk(xn, ci, NT)
                        i = te.matmul(ps[:], st, mv, start=(ci == 0),
                                      stop=(ci == C - 1))
                    return i
                P.op("te", emit, reads=[buf(xn.name), buf(wA.name)], writes=[psb])
                P.op("sc", lambda s, co=co, ps=ps: s.activation(
                    _fm_chunk(q_fm, co, NT), ps[:], AF.Relu,
                    bias=cx.bpp0[:, 0, co:co + 1]),
                     reads=[psb, buf(cx.bpp0.name)], writes=[buf(q_fm.name)])

            for hp in range(C):
                ps = ps_proj[hp % 2]
                psb = buf(ps.name)
                P.op("te", lambda te, hp=hp, ps=ps: te.matmul(
                    ps[:], kvd[:, hp * 128:(hp + 1) * 128],
                    _fm_chunk(q_fm, hp, NT), start=True, stop=True),
                     reads=[buf(q_fm.name), buf(kvd.name)], writes=[psb])
                P.op("ve", lambda v, hp=hp, ps=ps: v.tensor_copy(
                    _fm_chunk(a_fm, hp, NT), ps[:]),
                     reads=[psb], writes=[buf(a_fm.name)])

            for co in range(C):
                ps = ps_proj[co % 2]
                psb = buf(ps.name)

                def emit(te, co=co, ps=ps):
                    i = None
                    for ci in range(C):
                        st = _fm_chunk(wB, ci, D)[:, co * 128:(co + 1) * 128]
                        mv = _fm_chunk(a_fm, ci, NT)
                        i = te.matmul(ps[:], st, mv, start=(ci == 0),
                                      stop=(ci == C - 1))
                    return i
                P.op("te", emit, reads=[buf(a_fm.name), buf(wB.name)],
                     writes=[psb])
                P.op("ve", lambda v, co=co, ps=ps, xtile=xtile:
                     v.scalar_tensor_tensor(
                         _fm_chunk(x1t, co, NT), ps[:], cx.bpp0[:, 3, co:co + 1],
                         _fm_chunk(xtile, co, NT), op0=OP.add, op1=OP.add),
                     reads=[psb, bx, buf(cx.bpp0.name)], writes=[buf(x1t.name)])
            P.op("gp", lambda gp, it=it: gp.dma_start(
                out=g["x1_hbm"][:].rearrange("p (c t) -> p c t", c=C)
                [:, :, it * NT:(it + 1) * NT],
                in_=x1t[:].rearrange("p (c t) -> p c t", c=C)),
                 reads=[buf(x1t.name)], writes=[buf("hbm_x1")], dma="dgp")
            cx.dbg_sl("dbg_x1", np.s_[:, it * NT * C:(it + 1) * NT * C],
                      x1t[:], [buf(x1t.name)])


def _mm3(te, ps_ap, st_hi, st_lo, mv_hi, mv_lo, first, last, skip=False):
    i = te.matmul(ps_ap, st_hi, mv_hi, start=first, stop=False,
                  skip_group_check=skip)
    i = te.matmul(ps_ap, st_lo, mv_hi, start=False, stop=False,
                  skip_group_check=skip)
    i = te.matmul(ps_ap, st_hi, mv_lo, start=False, stop=last,
                  skip_group_check=skip)
    return i


def _fp4_chain(cx, xb_ap, q_ap, W, s_a, s_g, s_p2, s_int, bm, rs, bufs):
    P = cx.P
    bx, ba, bg, bi, bbm, brs, bq = (bufs[k] for k in
                                    ("bx", "ba", "bg", "bi", "bbm", "brs", "bq"))
    a3 = s_a.rearrange("p (b k) -> p b k", k=16)
    P.op("sc", lambda s: s.activation(s_a, xb_ap, AF.Abs), reads=[bx], writes=[ba])
    P.op("ve", lambda v: v.tensor_reduce(bm, a3, axis=AX.X, op=OP.max),
         reads=[ba], writes=[bbm])
    P.op("ve", lambda v: v.tensor_scalar(bm, bm, cx.recip6[:], 1e-12,
                                         op0=OP.mult, op1=OP.max),
         reads=[bbm, cx.buf("c_recip6")], writes=[bbm])
    P.op("ve", lambda v: v.reciprocal(rs, bm), reads=[bbm], writes=[brs])
    rsb = rs.unsqueeze(2).broadcast_to((128, W // 16, 16))
    bmb = bm.unsqueeze(2).broadcast_to((128, W // 16, 16))
    P.op("ve", lambda v: v.tensor_tensor(a3, a3, rsb, op=OP.mult),
         reads=[ba, brs], writes=[ba])
    P.op("ve", lambda v: v.tensor_scalar(s_g, s_a, 0.25, None, op0=OP.is_gt),
         reads=[ba], writes=[bg])
    for thr in (0.75, 1.25, 1.75):
        P.op("ve", lambda v, thr=thr: v.scalar_tensor_tensor(
            s_g, s_a, thr, s_g, op0=OP.is_gt, op1=OP.add),
             reads=[ba, bg], writes=[bg])
    bp2 = bufs["bp2"]
    P.op("ve", lambda v: v.tensor_scalar(s_p2, s_a, 5.0, 2.0,
                                         op0=OP.is_gt, op1=OP.mult),
         reads=[ba], writes=[bp2])
    for thr in (2.5, 3.5):
        P.op("ve", lambda v, thr=thr: v.scalar_tensor_tensor(
            s_p2, s_a, thr, s_p2, op0=OP.is_gt, op1=OP.add),
             reads=[ba, bp2], writes=[bp2])
    P.op("ve", lambda v: v.scalar_tensor_tensor(
        s_g, s_g, 0.5, s_p2, op0=OP.mult, op1=OP.add),
         reads=[bg, bp2], writes=[bg])
    P.op("sc", lambda s: s.activation(s_a, xb_ap, AF.Sign),
         reads=[bx], writes=[ba])
    P.op("ve", lambda v: v.tensor_tensor(s_g, s_g, s_a, op=OP.mult),
         reads=[bg, ba], writes=[bg])
    q3 = q_ap.rearrange("p (b k) -> p b k", k=16)
    P.op("ve", lambda v: v.tensor_tensor(
        q3, s_g.rearrange("p (b k) -> p b k", k=16), bmb, op=OP.mult),
         reads=[bg, bbm], writes=[bq])


def _phase2(cx):
    nc, P, g, buf = cx.nc, cx.P, cx.g, cx.buf
    import contextlib
    NSLAB = FF0 // 128
    with contextlib.ExitStack() as ph:
        wslab = [_sb(nc, ph, f"p2_wsl{i}", [128, FF0]) for i in range(2)]
        s_a = _sb(nc, ph, "p2_sa", [128, FF0])
        s_g = _sb(nc, ph, "p2_sg", [128, FF0])
        s_int = _sb(nc, ph, "p2_si", [128, FF0], I32)
        bm_t = _sb(nc, ph, "p2_bm", [128, FF0 // 16])
        rs_t = _sb(nc, ph, "p2_rs", [128, FF0 // 16])
        ps_tp = _ps(nc, ph, "p2_ps_tp", [128, 512])
        ps_tp2 = _ps(nc, ph, "p2_ps_tp2", [128, 512])

        s_p2t = _sb(nc, ph, "p2_sp2", [128, FF0])

        def chain_bufs(bx, bq):
            return {"bx": bx, "ba": buf(s_a.name), "bg": buf(s_g.name),
                    "bp2": buf(s_p2t.name), "bi": buf(s_int.name),
                    "bbm": buf(bm_t.name), "brs": buf(rs_t.name), "bq": bq}

        for which, wname in ((0, "f0_wg"), (1, "f0_wu")):
            for n in range(NSLAB):
                w = wslab[n % 2]
                bw = buf(w.name)
                P.op("sy", lambda s, wname=wname, n=n, w=w: s.dma_start(
                    out=w[:, 0:D], in_=g[wname][n * 128:(n + 1) * 128, :]),
                     reads=[buf("hbm_" + wname)], writes=[bw], dma="dsy")
                _fp4_chain(cx, w[:, 0:D], w[:, D:2 * D], D, s_a[:, 0:D],
                           s_g[:, 0:D], s_p2t[:, 0:D], s_int[:, 0:D],
                           bm_t[:, 0:64], rs_t[:, 0:64], chain_bufs(bw, bw))
                for ci in range(C):
                    dst_ps = ps_tp if ci % 2 == 0 else ps_tp2
                    P.op("te", lambda te, w=w, ci=ci, dst_ps=dst_ps: te.transpose(
                        dst_ps[:, 0:128], w[:, D + ci * 128:D + (ci + 1) * 128],
                        cx.idn[:]),
                         reads=[bw, buf(cx.idn.name)], writes=[buf(dst_ps.name)])
                    P.op("ve", lambda v, ci=ci, dst_ps=dst_ps: v.tensor_copy(
                        s_a[:, ci * 128:(ci + 1) * 128], dst_ps[:, 0:128]),
                         reads=[buf(dst_ps.name)], writes=[buf(s_a.name)])
                P.op("gp", lambda gp, which=which, n=n: gp.dma_start(
                    out=g["wq_fm_hbm"][:].rearrange("p (c n) -> p c n", c=C)
                    [:, :, which * FF0 + n * 128: which * FF0 + (n + 1) * 128],
                    in_=s_a[:, 0:D].rearrange("p (c n) -> p c n", c=C)),
                     reads=[buf(s_a.name)], writes=[buf("hbm_wqfm")], dma="dgp")

        for n in range(C):
            w = wslab[n % 2]
            bw = buf(w.name)
            P.op("sy", lambda s, n=n, w=w: s.dma_start(
                out=w[:], in_=g["f0_wd"][n * 128:(n + 1) * 128, :]),
                 reads=[buf("hbm_f0_wd")], writes=[bw], dma="dsy")
            _fp4_chain(cx, w[:], s_g[:], FF0, s_a[:], s_g[:], s_p2t[:],
                       s_int[:], bm_t[:], rs_t[:], chain_bufs(bw, buf(s_g.name)))
            for k2 in range(NSLAB):
                dst_ps = ps_tp if k2 % 2 == 0 else ps_tp2
                P.op("te", lambda te, k2=k2, dst_ps=dst_ps: te.transpose(
                    dst_ps[:, 0:128], s_g[:, k2 * 128:(k2 + 1) * 128], cx.idn[:]),
                     reads=[buf(s_g.name), buf(cx.idn.name)],
                     writes=[buf(dst_ps.name)])
                P.op("ve", lambda v, k2=k2, dst_ps=dst_ps: v.tensor_copy(
                    s_a[:, k2 * 128:(k2 + 1) * 128], dst_ps[:, 0:128]),
                     reads=[buf(dst_ps.name)], writes=[buf(s_a.name)])
            P.op("gp", lambda gp, n=n: gp.dma_start(
                out=g["wd_fm_hbm"][:].rearrange("p (c n) -> p c n", c=NSLAB)
                [:, :, n * 128:(n + 1) * 128],
                in_=s_a[:, 0:FF0].rearrange("p (c n) -> p c n", c=NSLAB)),
                 reads=[buf(s_a.name)], writes=[buf("hbm_wdfm")], dma="dgp")

    P.fence()
    NSUP = 512
    NTP = NSUP // 128
    with contextlib.ExitStack() as ph:
        x1sup = _sb(nc, ph, "p2_x1s", [128, C * NSUP])
        xqT_hi = _sb(nc, ph, "p2_xqh", [128, NTP * C * 128], F32R)
        xqT_lo = _sb(nc, ph, "p2_xql", [128, NTP * C * 128], F32R)
        hqfm = _sb(nc, ph, "p2_hqfm", [128, NSLAB * NSUP])
        wsl_hi = [_sb(nc, ph, f"p2_wslh{i}", [128, C * 256], F32R) for i in range(2)]
        wsl_lo = [_sb(nc, ph, f"p2_wsll{i}", [128, C * 256], F32R) for i in range(2)]
        wsl_raw = [_sb(nc, ph, f"p2_wslr{i}", [128, C * 256]) for i in range(2)]
        wdcol = [_sb(nc, ph, f"p2_wdc{i}", [128, NSLAB * 128]) for i in range(2)]
        hpiece = _sb(nc, ph, "p2_hp", [128, NTP * 256])
        s_a = _sb(nc, ph, "p2m_sa", [128, NTP * 256])
        s_g = _sb(nc, ph, "p2m_sg", [128, NTP * 256])
        s_int = _sb(nc, ph, "p2m_si", [128, NTP * 256], I32)
        bm_t = _sb(nc, ph, "p2m_bm", [128, NTP * 16])
        rs_t = _sb(nc, ph, "p2m_rs", [128, NTP * 16])
        xn2 = _sb(nc, ph, "p2_xn2", [128, D])
        xq2 = _sb(nc, ph, "p2_xq2", [128, D])
        junk = _sb(nc, ph, "p2_junk", [128, D])
        mv_t = _sb(nc, ph, "p2_mv", [128, 4])
        ps_x1 = _ps(nc, ph, "p2_ps_x1", [128, D])
        ps_tp = _ps(nc, ph, "p2_ps_t", [128, 512])
        ps_gu = [_ps(nc, ph, f"p2_ps_gu{i}", [128, 256]) for i in range(2)]
        ps_dn = [_ps(nc, ph, f"p2_ps_dn{i}", [128, NTP * 128]) for i in range(2)]

        s_p2t = _sb(nc, ph, "p2m_sp2", [128, NTP * 256])

        def chain_bufs(bx, bq):
            return {"bx": bx, "ba": buf(s_a.name), "bg": buf(s_g.name),
                    "bp2": buf(s_p2t.name), "bi": buf(s_int.name),
                    "bbm": buf(bm_t.name), "brs": buf(rs_t.name), "bq": bq}

        for sup in range(T4 // NSUP):
            P.op("sy", lambda s, sup=sup: s.dma_start(
                out=x1sup[:].rearrange("p (c t) -> p c t", c=C),
                in_=g["x1_hbm"][:].rearrange("p (c t) -> p c t", c=C)
                [:, :, sup * NSUP:(sup + 1) * NSUP]),
                 reads=[buf("hbm_x1")], writes=[buf(x1sup.name)], dma="dsy")
            if sup == 0:
                cx.dbg("dbg_x1rd", x1sup[:], [buf(x1sup.name)])
            for tp in range(NTP):
                for ci in range(C):
                    P.op("te", lambda te, tp=tp, ci=ci: te.transpose(
                        ps_x1[:, ci * 128:(ci + 1) * 128],
                        x1sup[:, ci * NSUP + tp * 128: ci * NSUP + (tp + 1) * 128],
                        cx.idn[:]),
                         reads=[buf(x1sup.name), buf(cx.idn.name)],
                         writes=[buf(ps_x1.name)])
                P.op("ve", lambda v: v.tensor_reduce(mv_t[:, 0:1], ps_x1[:],
                                                     axis=AX.X, op=OP.add),
                     reads=[buf(ps_x1.name)], writes=[buf(mv_t.name)])
                P.op("sc", lambda s: s.activation(junk[:], ps_x1[:], AF.Square,
                                                  accum_out=mv_t[:, 1:2]),
                     reads=[buf(ps_x1.name)], writes=[buf(junk.name), buf(mv_t.name)])
                P.op("ve", lambda v: v.tensor_scalar(mv_t[:, 0:1], mv_t[:, 0:1],
                                                     1.0 / D, None, op0=OP.mult),
                     reads=[buf(mv_t.name)], writes=[buf(mv_t.name)])
                P.op("ve", lambda v: v.tensor_tensor(mv_t[:, 2:3], mv_t[:, 0:1],
                                                     mv_t[:, 0:1], op=OP.mult),
                     reads=[buf(mv_t.name)], writes=[buf(mv_t.name)])
                P.op("ve", lambda v: v.scalar_tensor_tensor(
                    mv_t[:, 2:3], mv_t[:, 1:2], 1.0 / D, mv_t[:, 2:3],
                    op0=OP.mult, op1=OP.subtract),
                     reads=[buf(mv_t.name)], writes=[buf(mv_t.name)])
                P.op("sc", lambda s: s.activation(mv_t[:, 2:3], mv_t[:, 2:3],
                                                  AF.Sqrt, bias=cx.eps_ap),
                     reads=[buf(mv_t.name), buf(cx.cpool.name)],
                     writes=[buf(mv_t.name)])
                P.op("ve", lambda v: v.reciprocal(mv_t[:, 2:3], mv_t[:, 2:3]),
                     reads=[buf(mv_t.name)], writes=[buf(mv_t.name)])
                P.op("ve", lambda v: v.tensor_scalar(
                    xn2[:], ps_x1[:], mv_t[:, 0:1], mv_t[:, 2:3],
                    op0=OP.subtract, op1=OP.mult),
                     reads=[buf(ps_x1.name), buf(mv_t.name)], writes=[buf(xn2.name)])
                _fp4_chain(cx, xn2[:], xq2[:], D, s_a[:, 0:D], s_g[:, 0:D],
                           s_p2t[:, 0:D], s_int[:, 0:D], bm_t[:, 0:64],
                           rs_t[:, 0:64],
                           chain_bufs(buf(xn2.name), buf(xq2.name)))
                if sup == 0 and tp == 0:
                    cx.dbg("dbg_xq", xq2[:], [buf(xq2.name)])
                    cx.dbg("dbg_xn2", xn2[:], [buf(xn2.name)])
                for ci in range(C):
                    P.op("te", lambda te, ci=ci: te.transpose(
                        ps_tp[:, 0:128], xq2[:, ci * 128:(ci + 1) * 128],
                        cx.idn[:]),
                         reads=[buf(xq2.name), buf(cx.idn.name)],
                         writes=[buf(ps_tp.name)])
                    off = (tp * C + ci) * 128
                    P.op("ve", lambda v, off=off: v.tensor_copy(
                        xqT_hi[:, off:off + 128], ps_tp[:, 0:128]),
                         reads=[buf(ps_tp.name)], writes=[buf(xqT_hi.name)])
                    P.op("ve", lambda v, off=off: v.tensor_tensor(
                        xqT_lo[:, off:off + 128], ps_tp[:, 0:128],
                        xqT_hi[:, off:off + 128].bitcast(F32), op=OP.subtract),
                         reads=[buf(ps_tp.name), buf(xqT_hi.name)],
                         writes=[buf(xqT_lo.name)])

            for sl in range(FF0 // 256):
                wr = wsl_raw[sl % 2]
                wh, wl = wsl_hi[sl % 2], wsl_lo[sl % 2]
                for which in range(2):
                    P.op("sy", lambda s, sl=sl, which=which, wr=wr: s.dma_start(
                        out=wr[:].rearrange("p (c n) -> p c n", c=C),
                        in_=g["wq_fm_hbm"][:].rearrange("p (c n) -> p c n", c=C)
                        [:, :, which * FF0 + sl * 256: which * FF0 + (sl + 1) * 256]),
                         reads=[buf("hbm_wqfm")], writes=[buf(wr.name)], dma="dsy")
                    if sup == 0 and sl == 0 and which == 0:
                        cx.dbg("dbg_wq", wr[:], [buf(wr.name)])
                    P.op("ve", lambda v, wr=wr, wh=wh: v.tensor_copy(wh[:], wr[:]),
                         reads=[buf(wr.name)], writes=[buf(wh.name)])
                    P.op("ve", lambda v, wr=wr, wh=wh, wl=wl: v.tensor_tensor(
                        wl[:], wr[:], wh[:].bitcast(F32), op=OP.subtract),
                         reads=[buf(wr.name), buf(wh.name)], writes=[buf(wl.name)])
                    for tp in range(NTP):
                        ps = ps_gu[tp % 2]

                        def emit(te, tp=tp, wh=wh, wl=wl, ps=ps):
                            i = None
                            for ci in range(C):
                                off = (tp * C + ci) * 128
                                sh = xqT_hi[:, off:off + 128]
                                sl_ = xqT_lo[:, off:off + 128]
                                mh = wh[:, ci * 256:(ci + 1) * 256]
                                ml = wl[:, ci * 256:(ci + 1) * 256]
                                i = _mm3(te, ps[:], sh, sl_, mh, ml,
                                         ci == 0, ci == C - 1)
                            return i
                        P.op("te", emit,
                             reads=[buf(xqT_hi.name), buf(xqT_lo.name),
                                    buf(wh.name), buf(wl.name)],
                             writes=[buf(ps.name)])
                        dst = hpiece[:, tp * 256:(tp + 1) * 256]
                        if which == 0:
                            P.op("sc", lambda s, ps=ps, dst=dst: s.activation(
                                dst, ps[:], AF.Silu),
                                 reads=[buf(ps.name)], writes=[buf(hpiece.name)])
                        else:
                            P.op("ve", lambda v, ps=ps, dst=dst: v.tensor_tensor(
                                dst, ps[:], dst, op=OP.mult),
                                 reads=[buf(ps.name), buf(hpiece.name)],
                                 writes=[buf(hpiece.name)])
                _fp4_chain(cx, hpiece[:], hpiece[:], NTP * 256, s_a[:], s_g[:],
                           s_p2t[:], s_int[:], bm_t[:], rs_t[:],
                           chain_bufs(buf(hpiece.name), buf(hpiece.name)))
                for tp in range(NTP):
                    for half in range(2):
                        k2c = sl * 2 + half
                        P.op("te", lambda te, tp=tp, half=half: te.transpose(
                            ps_tp[:, 0:128],
                            hpiece[:, tp * 256 + half * 128: tp * 256 + (half + 1) * 128],
                            cx.idn[:]),
                             reads=[buf(hpiece.name), buf(cx.idn.name)],
                             writes=[buf(ps_tp.name)])
                        P.op("ve", lambda v, k2c=k2c, tp=tp: v.tensor_copy(
                            hqfm[:, k2c * NSUP + tp * 128: k2c * NSUP + (tp + 1) * 128],
                            ps_tp[:, 0:128]),
                             reads=[buf(ps_tp.name)], writes=[buf(hqfm.name)])

            if sup == 0:
                cx.dbg("dbg_hq", hqfm[:], [buf(hqfm.name)])
            for co in range(C):
                wd = wdcol[co % 2]
                P.op("sy", lambda s, co=co, wd=wd: s.dma_start(
                    out=wd[:].rearrange("p (c n) -> p c n", c=NSLAB),
                    in_=g["wd_fm_hbm"][:].rearrange("p (c n) -> p c n", c=NSLAB)
                    [:, :, co * 128:(co + 1) * 128]),
                     reads=[buf("hbm_wdfm")], writes=[buf(wd.name)], dma="dsy")
                if sup == 0 and co == 0:
                    cx.dbg("dbg_wd", wd[:], [buf(wd.name)])
                ps = ps_dn[co % 2]

                def emit(te, co=co, wd=wd, ps=ps):
                    i = None
                    for k2 in range(NSLAB):
                        st = wd[:, k2 * 128:(k2 + 1) * 128]
                        mv = hqfm[:, k2 * NSUP:(k2 + 1) * NSUP]
                        i = te.matmul(ps[:], st, mv, start=(k2 == 0),
                                      stop=(k2 == NSLAB - 1))
                    return i
                P.op("te", emit, reads=[buf(wd.name), buf(hqfm.name)],
                     writes=[buf(ps.name)])
                P.op("ve", lambda v, co=co, ps=ps: v.tensor_tensor(
                    s_a[:, 0:NSUP], ps[:],
                    x1sup[:, co * NSUP:(co + 1) * NSUP], op=OP.add),
                     reads=[buf(ps.name), buf(x1sup.name)], writes=[buf(s_a.name)])
                P.op("gp", lambda gp, co=co, sup=sup: gp.dma_start(
                    out=g["x2_hbm"][:, co * T4 + sup * NSUP:
                                    co * T4 + (sup + 1) * NSUP],
                    in_=s_a[:, 0:NSUP]),
                     reads=[buf(s_a.name)], writes=[buf("hbm_x2")], dma="dgp")
                cx.dbg_sl("dbg_x2", np.s_[:, co * T4 + sup * NSUP:
                                          co * T4 + (sup + 1) * NSUP],
                          s_a[:, 0:NSUP], [buf(s_a.name)])


def _phase3(cx):
    nc, P, g, buf = cx.nc, cx.P, cx.g, cx.buf
    import contextlib
    NT = 512

    def proj_pass(kind):
        with contextlib.ExitStack() as ph:
            wname = {"k": "l1_wk", "v": "l1_wv", "q": "l1_wq"}[kind]
            w = _load_w(cx, ph, wname, f"p3_{kind}_w")
            ntile = (T2 if kind == "q" else T4) // NT
            xt = [_sb(nc, ph, f"p3{kind}_x{i}", [128, C * NT]) for i in range(2)]
            S0 = _sb(nc, ph, f"p3{kind}_S0", [128, C * NT])
            S1 = _sb(nc, ph, f"p3{kind}_S1", [128, C * NT])
            S2 = _sb(nc, ph, f"p3{kind}_S2", [128, C * NT])
            SHt = _sb(nc, ph, f"p3{kind}_SH", [128, C * NT], F32R)
            SLt = _sb(nc, ph, f"p3{kind}_SL", [128, C * NT], F32R)
            m_t = _sb(nc, ph, f"p3{kind}_m", [128, NT])
            r_t = _sb(nc, ph, f"p3{kind}_r", [128, NT])
            m_row = _sb(nc, ph, f"p3{kind}_mr", [1, NT])
            r_row = _sb(nc, ph, f"p3{kind}_rr", [1, NT])
            rope_cs = _sb(nc, ph, f"p3{kind}_rcs", [128, 2 * DH])
            bv_bc = _sb(nc, ph, f"p3{kind}_bvb", [128, D])
            ps_stats = _ps(nc, ph, f"p3{kind}_pst", [128, 2 * NT])
            ps_a = [_ps(nc, ph, f"p3{kind}_psa{i}", [128, NT]) for i in range(2)]
            ps_tm = _ps(nc, ph, f"p3{kind}_ptm", [128, 2048])

            if kind == "v":
                brow1 = _sb(nc, ph, f"p3v_brow", [1, D])
                P.op("sy", lambda s: s.dma_start(
                    out=brow1[:], in_=g["l1_brow"][0:1, 2 * D:3 * D]),
                     reads=[buf("hbm_l1_brow")], writes=[buf(brow1.name)],
                     dma="dsy")
                for half in range(2):
                    P.op("te", lambda te, half=half: te.matmul(
                        ps_stats[:, 0:512], cx.onescol[0:1, :],
                        brow1[0:1, half * 512:(half + 1) * 512],
                        start=True, stop=True),
                         reads=[buf(cx.onescol.name), buf(brow1.name)],
                         writes=[buf(ps_stats.name)])
                    P.op("ve", lambda v, half=half: v.tensor_copy(
                        bv_bc[:, half * 512:(half + 1) * 512], ps_stats[:, 0:512]),
                         reads=[buf(ps_stats.name)], writes=[buf(bv_bc.name)])

            def hl_split(src_ap, sbuf):
                P.op("ve", lambda v: v.tensor_copy(SHt[:], src_ap),
                     reads=[sbuf], writes=[buf(SHt.name)])
                P.op("ve", lambda v: v.tensor_tensor(
                    SLt[:], src_ap, SHt[:].bitcast(F32), op=OP.subtract),
                     reads=[sbuf, buf(SHt.name)], writes=[buf(SLt.name)])

            for it in range(ntile):
                xtile = xt[it % 2]
                bx = buf(xtile.name)
                P.op("sy", lambda s, it=it, xtile=xtile: s.dma_start(
                    out=xtile[:].rearrange("p (c t) -> p c t", c=C),
                    in_=g["x2_hbm"][:].rearrange("p (c t) -> p c t", c=C)
                    [:, :, it * NT:(it + 1) * NT]),
                     reads=[buf("hbm_x2")], writes=[bx], dma="dsy")
                if kind == "k":
                    hl_split(xtile[:], bx)
                    psb = buf(ps_stats.name)

                    def mm_sum(col):
                        def e(te):
                            i = None
                            for cch in range(C):
                                i = te.matmul(ps_stats[:, col:col + NT],
                                              cx.ones_r[:],
                                              SHt[:, cch * NT:(cch + 1) * NT],
                                              start=(cch == 0), stop=False)
                                i = te.matmul(ps_stats[:, col:col + NT],
                                              cx.ones_r[:],
                                              SLt[:, cch * NT:(cch + 1) * NT],
                                              start=False, stop=(cch == C - 1))
                            return i
                        return e
                    P.op("te", mm_sum(0),
                         reads=[buf(SHt.name), buf(SLt.name),
                                buf(cx.ones_r.name)], writes=[psb])
                    P.op("sc", lambda s: s.activation(S2[:], xtile[:], AF.Square),
                         reads=[bx], writes=[buf(S2.name)])
                    hl_split(S2[:], buf(S2.name))
                    P.op("te", mm_sum(NT),
                         reads=[buf(SHt.name), buf(SLt.name),
                                buf(cx.ones_r.name)], writes=[psb])
                    bm, br = buf(m_t.name), buf(r_t.name)
                    P.op("ve", lambda v: v.tensor_scalar(
                        m_t[:], ps_stats[:, 0:NT], 1.0 / D, None, op0=OP.mult),
                         reads=[psb], writes=[bm])
                    P.op("ve", lambda v: v.tensor_tensor(
                        r_t[:], m_t[:], m_t[:], op=OP.mult),
                         reads=[bm], writes=[br])
                    P.op("ve", lambda v: v.scalar_tensor_tensor(
                        r_t[:], ps_stats[:, NT:2 * NT], 1.0 / D, r_t[:],
                        op0=OP.mult, op1=OP.subtract),
                         reads=[psb, br], writes=[br])
                    P.op("sc", lambda s: s.activation(
                        r_t[:], r_t[:], AF.Sqrt, bias=cx.eps_ap),
                         reads=[br, buf(cx.cpool.name)], writes=[br])
                    P.op("ve", lambda v: v.reciprocal(r_t[:], r_t[:]),
                         reads=[br], writes=[br])
                    P.op("gp", lambda gp, it=it: gp.dma_start(
                        out=g["mr_hbm"][0:1, it * NT:(it + 1) * NT],
                        in_=m_t[0:1, :]),
                         reads=[bm], writes=[buf("hbm_mr")], dma="dgp")
                    P.op("gp", lambda gp, it=it: gp.dma_start(
                        out=g["mr_hbm"][1:2, it * NT:(it + 1) * NT],
                        in_=r_t[0:1, :]),
                         reads=[br], writes=[buf("hbm_mr")], dma="dgp")
                    _apply_ln_fm(cx, xtile, m_t, r_t, S2, NT)
                else:
                    for which, rw in ((0, m_row), (1, r_row)):
                        P.op("sy", lambda s, which=which, rw=rw, it=it:
                             s.dma_start(out=rw[:], in_=g["mr_hbm"]
                                         [which:which + 1, it * NT:(it + 1) * NT]),
                             reads=[buf("hbm_mr")], writes=[buf(rw.name)],
                             dma="dsy")
                        P.op("te", lambda te, which=which, rw=rw: te.matmul(
                            ps_stats[:, which * NT:(which + 1) * NT],
                            cx.onescol[0:1, :], rw[:], start=True, stop=True),
                             reads=[buf(rw.name), buf(cx.onescol.name)],
                             writes=[buf(ps_stats.name)])
                    _apply_ln_fm(cx, xtile, None, None, S2, NT,
                                 m_b=buf(ps_stats.name), r_b=buf(ps_stats.name),
                                 m_ap=ps_stats[:, 0:NT],
                                 r_ap=ps_stats[:, NT:2 * NT])
                xn = S2

                if kind == "v":
                    for tp in range(4):
                        for dh2 in range(2):
                            ps = ps_a[(tp * 2 + dh2) % 2]
                            psb = buf(ps.name)

                            def emit(te, tp=tp, dh2=dh2, ps=ps):
                                i = None
                                for ci in range(C):
                                    st = xn[:, ci * NT + tp * 128:
                                            ci * NT + (tp + 1) * 128]
                                    mv = w[:, ci * D + dh2 * 512:
                                           ci * D + (dh2 + 1) * 512]
                                    i = te.matmul(ps[:], st, mv,
                                                  start=(ci == 0),
                                                  stop=(ci == C - 1))
                                return i
                            P.op("te", emit,
                                 reads=[buf(xn.name), buf(w.name)], writes=[psb])
                            dst = S0[:, tp * D + dh2 * 512:
                                     tp * D + (dh2 + 1) * 512]
                            P.op("ve", lambda v, ps=ps, dst=dst, dh2=dh2:
                                 v.tensor_tensor(
                                     dst, ps[:],
                                     bv_bc[:, dh2 * 512:(dh2 + 1) * 512],
                                     op=OP.add),
                                 reads=[psb, buf(bv_bc.name)],
                                 writes=[buf(S0.name)])
                    for tp in range(4):
                        for hp in range(C):
                            P.op("gp", lambda gp, tp=tp, hp=hp, it=it:
                                 gp.dma_start(
                                     out=g["v_hbm"][hp:hp + 1,
                                                    it * NT + tp * 128:
                                                    it * NT + (tp + 1) * 128, :]
                                     .squeeze(0),
                                     in_=S0[:, tp * D + hp * 128:
                                            tp * D + (hp + 1) * 128]),
                                 reads=[buf(S0.name)], writes=[buf("hbm_v")],
                                 dma="dgp")
                else:
                    bcol = 0 if kind == "q" else 1
                    for co in range(C):
                        ps = ps_a[co % 2]
                        psb = buf(ps.name)

                        def emit(te, co=co, ps=ps):
                            i = None
                            for ci in range(C):
                                st = w[:, ci * D + co * 128:
                                       ci * D + (co + 1) * 128]
                                mv = xn[:, ci * NT:(ci + 1) * NT]
                                i = te.matmul(ps[:], st, mv, start=(ci == 0),
                                              stop=(ci == C - 1))
                            return i
                        P.op("te", emit,
                             reads=[buf(xn.name), buf(w.name)], writes=[psb])
                        P.op("sc", lambda s, co=co, ps=ps: s.activation(
                            S0[:, co * NT:(co + 1) * NT], ps[:], AF.Identity,
                            bias=cx.bpp1[:, bcol, co:co + 1]),
                             reads=[psb, buf(cx.bpp1.name)], writes=[buf(S0.name)])
                    tbl = "ropeq" if kind == "q" else "ropek"
                    for tp in range(4):
                        P.op("sy", lambda s, tp=tp, it=it: s.dma_start(
                            out=rope_cs[:],
                            in_=g[tbl][it * NT + tp * 128:
                                       it * NT + (tp + 1) * 128, :]),
                             reads=[buf("hbm_" + tbl)],
                             writes=[buf(rope_cs.name)], dma="dsy")
                        for ci in range(C):
                            P.op("te", lambda te, ci=ci, tp=tp: te.transpose(
                                ps_tm[:, ci * 128:(ci + 1) * 128],
                                S0[:, ci * NT + tp * 128:ci * NT + (tp + 1) * 128],
                                cx.idn[:]),
                                 reads=[buf(S0.name), buf(cx.idn.name)],
                                 writes=[buf(ps_tm.name)])
                        p3 = ps_tm[:, 0:1024].rearrange("p (h d) -> p h d", d=DH)
                        o3 = S1[:, tp * 1024:(tp + 1) * 1024].rearrange(
                            "p (h d) -> p h d", d=DH)
                        cs = rope_cs[:, 0:DH]
                        sn = rope_cs[:, DH:2 * DH]
                        P.op("ve", lambda v, p3=p3, o3=o3, cs=cs: v.tensor_tensor(
                            o3, p3, cs.unsqueeze(1).broadcast_to((128, 2 * C, DH)),
                            op=OP.mult),
                             reads=[buf(ps_tm.name), buf(rope_cs.name)],
                             writes=[buf(S1.name)])
                        for gxx in range(2):
                            po = ps_tm[:, 0:1024].rearrange("p (h g d) -> p h g d", g=2, d=32)
                            oo = S1[:, tp * 1024:(tp + 1) * 1024].rearrange(
                                "p (h g d) -> p h g d", g=2, d=32)
                            src_g = po[:, :, 1 - gxx, :]
                            dst_g = oo[:, :, gxx, :]
                            s_g = sn[:, gxx * 32:(gxx + 1) * 32]
                            P.op("ve", lambda v, src_g=src_g, s_g=s_g:
                                 v.tensor_tensor(
                                     S2[:, 0:2 * C * 32].rearrange(
                                         "p (h d) -> p h d", d=32),
                                     src_g,
                                     s_g.unsqueeze(1).broadcast_to((128, 2 * C, 32)),
                                     op=OP.mult),
                                 reads=[buf(ps_tm.name), buf(rope_cs.name)],
                                 writes=[buf(S2.name)])
                            P.op("ve", lambda v, dst_g=dst_g: v.tensor_tensor(
                                dst_g, dst_g,
                                S2[:, 0:2 * C * 32].rearrange(
                                    "p (h d) -> p h d", d=32),
                                op=OP.add),
                                 reads=[buf(S2.name), buf(S1.name)],
                                 writes=[buf(S1.name)])
                        for ci in range(C):
                            P.op("te", lambda te, ci=ci, tp=tp: te.transpose(
                                ps_tm[:, 1024 + ci * 128:1024 + (ci + 1) * 128],
                                S1[:, tp * 1024 + ci * 128:
                                   tp * 1024 + (ci + 1) * 128],
                                cx.idn[:]),
                                 reads=[buf(S1.name), buf(cx.idn.name)],
                                 writes=[buf(ps_tm.name)])
                            P.op("ve", lambda v, ci=ci, tp=tp: v.tensor_copy(
                                S0[:, ci * NT + tp * 128:ci * NT + (tp + 1) * 128],
                                ps_tm[:, 1024 + ci * 128:1024 + (ci + 1) * 128]),
                                 reads=[buf(ps_tm.name)], writes=[buf(S0.name)])
                    dst_t = "q_hbm" if kind == "q" else "k_hbm"
                    for ci in range(C):
                        for hh in range(2):
                            P.op("gp", lambda gp, ci=ci, hh=hh, it=it:
                                 gp.dma_start(
                                     out=g[dst_t][2 * ci + hh:2 * ci + hh + 1, :,
                                                  it * NT:(it + 1) * NT]
                                     .squeeze(0),
                                     in_=S0[hh * 64:(hh + 1) * 64,
                                            ci * NT:(ci + 1) * NT]),
                                 reads=[buf(S0.name)],
                                 writes=[buf("hbm_" + dst_t)], dma="dgp")

        P.fence()

    proj_pass("k")
    proj_pass("v")
    proj_pass("q")

    NSUPQ = 512
    with contextlib.ExitStack() as ph:
        kraw = _sb(nc, ph, "p3a_kraw", [64, T4])
        k_hi = _sb(nc, ph, "p3a_kh", [64, T4], F32R)
        k_lo = _sb(nc, ph, "p3a_kl", [64, T4], F32R)
        vstr = _sb(nc, ph, "p3a_v", [128, 32 * 128])
        vext = _sb(nc, ph, "p3a_ve", [128, 32 * 130])
        qs = _sb(nc, ph, "p3a_q", [64, 2 * NSUPQ])
        q_hi = _sb(nc, ph, "p3a_qh", [64, 2 * NSUPQ], F32R)
        q_lo = _sb(nc, ph, "p3a_ql", [64, 2 * NSUPQ], F32R)
        ptile = [_sb(nc, ph, f"p3a_p{i}", [128, NSUPQ]) for i in range(2)]
        dnr = _sb(nc, ph, "p3a_dnr", [1, NSUPQ])
        dbc = _sb(nc, ph, "p3a_dbc", [64, NSUPQ])
        o_heads = _sb(nc, ph, "p3a_oh", [64, H * NSUPQ])
        wo = [_sb(nc, ph, f"p3a_wo{i}", [64, H * 128]) for i in range(2)]
        x2own = _sb(nc, ph, "p3a_x2o", [128, C * NSUPQ])
        x3t = _sb(nc, ph, "p3a_x3", [128, C * NSUPQ])
        ps_S = [_ps(nc, ph, f"p3a_psS{i}", [128, NSUPQ]) for i in range(2)]
        ps_OA = _ps(nc, ph, "p3a_psOA", [65, NSUPQ])
        ps_OB = _ps(nc, ph, "p3a_psOB", [65, NSUPQ])
        ps_bc = _ps(nc, ph, "p3a_psbc", [64, NSUPQ])
        ps_o = [_ps(nc, ph, f"p3a_pso{i}", [128, NSUPQ]) for i in range(2)]

        P.op("ve", lambda v: v.memset(vext[:], 1.0), writes=[buf(vext.name)])

        for sup in range(T2 // NSUPQ):
            for hp in range(C):
                P.op("sy", lambda s, hp=hp: s.dma_start(
                    out=vstr[:].rearrange("p (k n) -> p k n", k=32),
                    in_=g["v_hbm"][hp:hp + 1, :, :].squeeze(0)
                    .rearrange("(k p) n -> p k n", p=128)),
                     reads=[buf("hbm_v")], writes=[buf(vstr.name)], dma="dsy")
                for half in range(2):
                    P.op("ve", lambda v, half=half: v.tensor_copy(
                        vext[:].rearrange("p (k n) -> p k n", k=32)
                        [:, :, half * 65:half * 65 + 64],
                        vstr[:].rearrange("p (k n) -> p k n", k=32)
                        [:, :, half * 64:(half + 1) * 64]),
                         reads=[buf(vstr.name)], writes=[buf(vext.name)])
                P.op("sy", lambda s, hp=hp, sup=sup: s.dma_start(
                    out=qs[:].rearrange("p (h n) -> p h n", h=2),
                    in_=g["q_hbm"][2 * hp:2 * hp + 2, :,
                                   sup * NSUPQ:(sup + 1) * NSUPQ]
                    .rearrange("h p n -> p h n")),
                     reads=[buf("hbm_q_hbm")], writes=[buf(qs.name)], dma="dsy")
                P.op("ve", lambda v: v.tensor_copy(q_hi[:], qs[:]),
                     reads=[buf(qs.name)], writes=[buf(q_hi.name)])
                P.op("ve", lambda v: v.tensor_tensor(
                    q_lo[:], qs[:], q_hi[:].bitcast(F32), op=OP.subtract),
                     reads=[buf(qs.name), buf(q_hi.name)], writes=[buf(q_lo.name)])

                for hh, ps_O in ((0, ps_OA), (1, ps_OB)):
                    kh, kl = k_hi, k_lo
                    P.op("sy", lambda s, hp=hp, hh=hh: s.dma_start(
                        out=kraw[:], in_=g["k_hbm"][2 * hp + hh:2 * hp + hh + 1,
                                                    :, :].squeeze(0)),
                         reads=[buf("hbm_k_hbm")], writes=[buf(kraw.name)],
                         dma="dsy")
                    P.op("ve", lambda v: v.tensor_copy(k_hi[:], kraw[:]),
                         reads=[buf(kraw.name)], writes=[buf(k_hi.name)])
                    P.op("ve", lambda v: v.tensor_tensor(
                        k_lo[:], kraw[:], k_hi[:].bitcast(F32), op=OP.subtract),
                         reads=[buf(kraw.name), buf(k_hi.name)],
                         writes=[buf(k_lo.name)])
                    for kt in range(32):
                        ps = ps_S[kt % 2]
                        psb = buf(ps.name)

                        def emit(te, kh=kh, kl=kl, kt=kt, hh=hh, ps=ps):
                            return _mm3(
                                te, ps[:],
                                kh[:, kt * 128:(kt + 1) * 128],
                                kl[:, kt * 128:(kt + 1) * 128],
                                q_hi[:, hh * NSUPQ:(hh + 1) * NSUPQ],
                                q_lo[:, hh * NSUPQ:(hh + 1) * NSUPQ],
                                True, True)
                        P.op("te", emit,
                             reads=[buf(kh.name), buf(kl.name),
                                    buf(q_hi.name), buf(q_lo.name)],
                             writes=[psb])
                        pt = ptile[kt % 2]
                        P.op("sc", lambda s, ps=ps, pt=pt: s.activation(
                            pt[:], ps[:], AF.Exp),
                             reads=[psb], writes=[buf(pt.name)])
                        P.op("te", lambda te, kt=kt, hh=hh, pt=pt, ps_O=ps_O:
                             te.matmul(
                                 ps_O[:],
                                 vext[:, kt * 130 + hh * 65:
                                      kt * 130 + (hh + 1) * 65],
                                 pt[:], start=(kt == 0), stop=(kt == 31),
                                 skip_group_check=True),
                             reads=[buf(pt.name), buf(vext.name)],
                             writes=[buf(ps_O.name)])
                    P.op("ve", lambda v, ps_O=ps_O: v.reciprocal(
                        dnr[:], ps_O[64:65, :]),
                         reads=[buf(ps_O.name)], writes=[buf(dnr.name)])
                    P.op("te", lambda te: te.matmul(
                        ps_bc[:], cx.onescol[0:1, 0:64], dnr[:],
                        start=True, stop=True),
                         reads=[buf(dnr.name), buf(cx.onescol.name)],
                         writes=[buf(ps_bc.name)])
                    P.op("ve", lambda v: v.tensor_copy(dbc[:], ps_bc[:]),
                         reads=[buf(ps_bc.name)], writes=[buf(dbc.name)])
                    hidx = 2 * hp + hh
                    P.op("ve", lambda v, hidx=hidx, ps_O=ps_O: v.tensor_tensor(
                        o_heads[:, hidx * NSUPQ:(hidx + 1) * NSUPQ],
                        ps_O[0:64, :], dbc[:], op=OP.mult),
                         reads=[buf(ps_O.name), buf(dbc.name)],
                         writes=[buf(o_heads.name)])

            P.op("sy", lambda s, sup=sup: s.dma_start(
                out=x2own[:].rearrange("p (c t) -> p c t", c=C),
                in_=g["x2_hbm"][:].rearrange("p (c t) -> p c t", c=C)
                [:, :, sup * NSUPQ:(sup + 1) * NSUPQ]),
                 reads=[buf("hbm_x2")], writes=[buf(x2own.name)], dma="dsy")
            for co in range(C):
                ps = ps_o[co % 2]
                psb = buf(ps.name)
                wot = wo[co % 2]
                P.op("sy", lambda s, co=co, wot=wot: s.dma_start(
                    out=wot[:].rearrange("p (h n) -> p h n", h=H),
                    in_=g["l1_wo"][:].rearrange("(h p) n -> p h n", p=64)
                    [:, :, co * 128:(co + 1) * 128]),
                     reads=[buf("hbm_l1_wo")], writes=[buf(wot.name)], dma="dsy")

                def emit(te, co=co, ps=ps, wot=wot):
                    i = None
                    for hh in range(H):
                        i = te.matmul(ps[:],
                                      wot[:, hh * 128:(hh + 1) * 128],
                                      o_heads[:, hh * NSUPQ:(hh + 1) * NSUPQ],
                                      start=(hh == 0), stop=(hh == H - 1))
                    return i
                P.op("te", emit,
                     reads=[buf(wot.name), buf(o_heads.name)], writes=[psb])
                P.op("ve", lambda v, co=co, ps=ps: v.scalar_tensor_tensor(
                    x3t[:, co * NSUPQ:(co + 1) * NSUPQ], ps[:],
                    cx.bpp1[:, 3, co:co + 1],
                    x2own[:, co * NSUPQ:(co + 1) * NSUPQ],
                    op0=OP.add, op1=OP.add),
                     reads=[psb, buf(x2own.name), buf(cx.bpp1.name)],
                     writes=[buf(x3t.name)])
            P.op("gp", lambda gp, sup=sup: gp.dma_start(
                out=g["x3_hbm"][:].rearrange("p (c t) -> p c t", c=C)
                [:, :, sup * NSUPQ:(sup + 1) * NSUPQ],
                in_=x3t[:].rearrange("p (c t) -> p c t", c=C)),
                 reads=[buf(x3t.name)], writes=[buf("hbm_x3")], dma="dgp")
            cx.dbg_sl("dbg_x3", np.s_[:, sup * NSUPQ * C:(sup + 1) * NSUPQ * C],
                      x3t[:], [buf(x3t.name)])


def _phase4(cx):
    nc, P, g, buf = cx.nc, cx.P, cx.g, cx.buf
    import contextlib
    NK2 = FF1 // 128

    with contextlib.ExitStack() as ph:
        wsl = [_sb(nc, ph, f"p4_wsl{i}", [128, FF1]) for i in range(2)]
        acc = _sb(nc, ph, "p4_acc", [128, 8])
        scl = _sb(nc, ph, "p4_scl", [128, 4])
        rscl = _sb(nc, ph, "p4_rscl", [128, 4])
        junk = _sb(nc, ph, "p4_junk", [128, FF1])
        nint = _sb(nc, ph, "p4_ni", [128, FF1], I32)
        nf = _sb(nc, ph, "p4_nf", [128, FF1])
        dd = _sb(nc, ph, "p4_dd", [128, FF1])
        cc = _sb(nc, ph, "p4_cc", [128, FF1])
        wi8 = _sb(nc, ph, "p4_wi8", [128, FF1], BF16)
        ps_s = _ps(nc, ph, "p4_ps_s", [128, 8])

        P.op("ve", lambda v: v.memset(rscl[:], 0.0), writes=[buf(rscl.name)])
        P.op("ve", lambda v: v.memset(scl[:], 0.0), writes=[buf(scl.name)])
        specs = [("f1_wg", C, FF1, 0), ("f1_wu", C, FF1, 1), ("f1_wd", NK2, D, 2)]
        for wname, nslab, ww, col in specs:
            for n in range(nslab):
                w = wsl[n % 2]
                bw = buf(w.name)
                P.op("sy", lambda s, wname=wname, n=n, w=w, ww=ww: s.dma_start(
                    out=w[:, 0:ww], in_=g[wname][n * 128:(n + 1) * 128, :]),
                     reads=[buf("hbm_" + wname)], writes=[bw], dma="dsy")
                P.op("sc", lambda s, w=w, ww=ww, col=col, n=n: s.activation(
                    junk[:, 0:ww], w[:, 0:ww], AF.Abs,
                    accum_out=acc[:, 4 + col:5 + col]),
                     reads=[bw], writes=[buf(junk.name), buf(acc.name)])
                if n == 0:
                    P.op("ve", lambda v, col=col: v.tensor_copy(
                        acc[:, col:col + 1], acc[:, 4 + col:5 + col]),
                         reads=[buf(acc.name)], writes=[buf(acc.name)])
                else:
                    P.op("ve", lambda v, col=col: v.tensor_tensor(
                        acc[:, col:col + 1], acc[:, col:col + 1],
                        acc[:, 4 + col:5 + col], op=OP.add),
                         reads=[buf(acc.name)], writes=[buf(acc.name)])
        for wname, nslab, ww, col in specs:
            P.op("te", lambda te, col=col: te.matmul(
                ps_s[:, col:col + 1], cx.ones[:], acc[:, col:col + 1],
                start=True, stop=True),
                 reads=[buf(acc.name), buf(cx.ones.name)], writes=[buf(ps_s.name)])
            nel = float(nslab * 128 * ww)
            P.op("ve", lambda v, col=col, nel=nel: v.tensor_scalar(
                rscl[:, col:col + 1], ps_s[:, col:col + 1], 1.0 / nel, 1e-5,
                op0=OP.mult, op1=OP.max),
                 reads=[buf(ps_s.name)], writes=[buf(rscl.name)])
            P.op("ve", lambda v, col=col: v.reciprocal(
                scl[:, col:col + 1], rscl[:, col:col + 1]),
                 reads=[buf(rscl.name)], writes=[buf(scl.name)])
        for wname, nslab, ww, col in specs:
            for n in range(nslab):
                w = wsl[n % 2]
                bw = buf(w.name)
                P.op("sy", lambda s, wname=wname, n=n, w=w, ww=ww: s.dma_start(
                    out=w[:, 0:ww], in_=g[wname][n * 128:(n + 1) * 128, :]),
                     reads=[buf("hbm_" + wname)], writes=[bw], dma="dsy")
                P.op("ve", lambda v, w=w, ww=ww, col=col: v.tensor_scalar(
                    junk[:, 0:ww], w[:, 0:ww], scl[:, col:col + 1], None,
                    op0=OP.mult),
                     reads=[bw, buf(scl.name)], writes=[buf(junk.name)])
                P.op("sc", lambda s, ww=ww: s.activation(
                    nint[:, 0:ww], junk[:, 0:ww], AF.Copy),
                     reads=[buf(junk.name)], writes=[buf(nint.name)])
                P.op("sc", lambda s, ww=ww: s.activation(
                    nf[:, 0:ww], nint[:, 0:ww], AF.Copy),
                     reads=[buf(nint.name)], writes=[buf(nf.name)])
                P.op("ve", lambda v, ww=ww: v.tensor_tensor(
                    dd[:, 0:ww], junk[:, 0:ww], nf[:, 0:ww], op=OP.subtract),
                     reads=[buf(junk.name), buf(nf.name)], writes=[buf(dd.name)])
                P.op("ve", lambda v, ww=ww: v.tensor_scalar(
                    cc[:, 0:ww], dd[:, 0:ww], 0.5, None, op0=OP.is_gt),
                     reads=[buf(dd.name)], writes=[buf(cc.name)])
                P.op("ve", lambda v, ww=ww: v.scalar_tensor_tensor(
                    cc[:, 0:ww], dd[:, 0:ww], -0.5, cc[:, 0:ww],
                    op0=OP.is_lt, op1=OP.subtract),
                     reads=[buf(dd.name), buf(cc.name)], writes=[buf(cc.name)])
                P.op("ve", lambda v, ww=ww: v.tensor_tensor(
                    junk[:, 0:ww], nf[:, 0:ww], cc[:, 0:ww], op=OP.subtract),
                     reads=[buf(nf.name), buf(cc.name)], writes=[buf(junk.name)])
                P.op("ve", lambda v, ww=ww: v.tensor_scalar(
                    wi8[:, 0:ww], junk[:, 0:ww], 1.0, -1.0,
                    op0=OP.min, op1=OP.max),
                     reads=[buf(junk.name)], writes=[buf(wi8.name)])
                if col < 2:
                    P.op("gp", lambda gp, col=col, n=n: gp.dma_start(
                        out=g["wint_hbm"][n * 128:(n + 1) * 128,
                                          col * FF1:(col + 1) * FF1],
                        in_=wi8[:, 0:FF1]),
                         reads=[buf(wi8.name)], writes=[buf("hbm_wint")],
                         dma="dgp")
                else:
                    P.op("gp", lambda gp, n=n: gp.dma_start(
                        out=g["wdint_hbm"][n * 128:(n + 1) * 128, :],
                        in_=wi8[:, 0:D]),
                         reads=[buf(wi8.name)], writes=[buf("hbm_wdint")],
                         dma="dgp")
        P.op("gp", lambda gp: gp.dma_start(
            out=g["mr_hbm"][0:1, 0:4], in_=rscl[0:1, 0:4]),
             reads=[buf(rscl.name)], writes=[buf("hbm_mr")], dma="dgp")
    P.fence()

    NSUP = 256
    NTP = 2
    with contextlib.ExitStack() as ph:
        x3sup = _sb(nc, ph, "p4_x3s", [128, C * NSUP])
        xqaT_hi = _sb(nc, ph, "p4_xth", [128, NTP * C * 128], F32R)
        xqaT_lo = _sb(nc, ph, "p4_xtl", [128, NTP * C * 128], F32R)
        hsup = _sb(nc, ph, "p4_hs", [128, NTP * FF1])
        wsl_i8 = [_sb(nc, ph, "p4_wsli0", [128, C * 512], BF16)] * 2
        wsl_r = [_sb(nc, ph, "p4_wslr0", [128, C * 512], F32R)] * 2
        wdbf = [_sb(nc, ph, f"p4_wdbf{i}", [128, NK2 * 128], BF16) for i in range(2)]
        wdstage = _sb(nc, ph, "p4_wdst", [128, NK2 * 128], F32R)
        hqT_hi = _sb(nc, ph, "p4_hqh", [128, NK2 * NSUP], F32R)
        hqT_lo = _sb(nc, ph, "p4_hql", [128, NK2 * NSUP], F32R)
        rscl = _sb(nc, ph, "p4m_rscl", [128, 4])
        amax = _sb(nc, ph, "p4_am", [128, 4])
        u_s = _sb(nc, ph, "p4_u", [128, 512])
        nf_s = _sb(nc, ph, "p4_nfs", [128, 512])
        ni_s = _sb(nc, ph, "p4_nis", [128, 512], I32)
        cc_s = _sb(nc, ph, "p4_ccs", [128, 512])
        xn4 = _sb(nc, ph, "p4_xn4", [128, D])
        xqa = _sb(nc, ph, "p4_xqa", [128, D])
        junk = xn4
        ytile = _sb(nc, ph, "p4_y", [128, NTP * D])
        ps_x3 = _ps(nc, ph, "p4_ps_x3", [128, D])
        ps_tp = _ps(nc, ph, "p4_ps_t", [128, 512])
        ps_gu = [_ps(nc, ph, f"p4_ps_gu{i}", [128, 512]) for i in range(2)]
        ps_dn = [_ps(nc, ph, f"p4_ps_dn{i}", [128, NSUP]) for i in range(2)]

        P.op("sy", lambda s: s.dma_start(
            out=rscl[0:1, :], in_=g["mr_hbm"][0:1, 0:4]),
             reads=[buf("hbm_mr")], writes=[buf(rscl.name)], dma="dsy")
        P.op("te", lambda te: te.matmul(ps_x3[:, 0:4], cx.onescol[0:1, :],
                                        rscl[0:1, :], start=True, stop=True),
             reads=[buf(rscl.name), buf(cx.onescol.name)], writes=[buf(ps_x3.name)])
        P.op("ve", lambda v: v.tensor_copy(rscl[:], ps_x3[:, 0:4]),
             reads=[buf(ps_x3.name)], writes=[buf(rscl.name)])

        def act_quant(src_ap, dst_ap, Wn, bsrc, bdst, scol):
            P.op("ve", lambda v: v.tensor_scalar(
                u_s[:, 0:Wn], src_ap, amax[:, scol:scol + 1], None, op0=OP.mult),
                 reads=[bsrc, buf(amax.name)], writes=[buf(u_s.name)])
            P.op("sc", lambda s: s.activation(ni_s[:, 0:Wn], u_s[:, 0:Wn],
                                              AF.Copy),
                 reads=[buf(u_s.name)], writes=[buf(ni_s.name)])
            P.op("sc", lambda s: s.activation(nf_s[:, 0:Wn], ni_s[:, 0:Wn],
                                              AF.Copy),
                 reads=[buf(ni_s.name)], writes=[buf(nf_s.name)])
            P.op("ve", lambda v: v.tensor_tensor(
                u_s[:, 0:Wn], u_s[:, 0:Wn], nf_s[:, 0:Wn], op=OP.subtract),
                 reads=[buf(u_s.name), buf(nf_s.name)], writes=[buf(u_s.name)])
            P.op("ve", lambda v: v.tensor_scalar(
                cc_s[:, 0:Wn], u_s[:, 0:Wn], 0.5, None, op0=OP.is_gt),
                 reads=[buf(u_s.name)], writes=[buf(cc_s.name)])
            P.op("ve", lambda v: v.scalar_tensor_tensor(
                cc_s[:, 0:Wn], u_s[:, 0:Wn], -0.5, cc_s[:, 0:Wn],
                op0=OP.is_lt, op1=OP.subtract),
                 reads=[buf(u_s.name), buf(cc_s.name)], writes=[buf(cc_s.name)])
            P.op("ve", lambda v: v.tensor_tensor(
                u_s[:, 0:Wn], nf_s[:, 0:Wn], cc_s[:, 0:Wn], op=OP.subtract),
                 reads=[buf(nf_s.name), buf(cc_s.name)], writes=[buf(u_s.name)])
            P.op("ve", lambda v: v.tensor_scalar(
                dst_ap, u_s[:, 0:Wn], amax[:, scol + 1:scol + 2], None,
                op0=OP.mult),
                 reads=[buf(u_s.name), buf(amax.name)], writes=[bdst])

        for sup in range(T2 // NSUP):
            P.op("sy", lambda s, sup=sup: s.dma_start(
                out=x3sup[:].rearrange("p (c t) -> p c t", c=C),
                in_=g["x3_hbm"][:].rearrange("p (c t) -> p c t", c=C)
                [:, :, sup * NSUP:(sup + 1) * NSUP]),
                 reads=[buf("hbm_x3")], writes=[buf(x3sup.name)], dma="dsy")
            for tp in range(NTP):
                for ci in range(C):
                    P.op("te", lambda te, tp=tp, ci=ci: te.transpose(
                        ps_x3[:, ci * 128:(ci + 1) * 128],
                        x3sup[:, ci * NSUP + tp * 128:ci * NSUP + (tp + 1) * 128],
                        cx.idn[:]),
                         reads=[buf(x3sup.name), buf(cx.idn.name)],
                         writes=[buf(ps_x3.name)])
                P.op("ve", lambda v: v.tensor_reduce(amax[:, 0:1], ps_x3[:],
                                                     axis=AX.X, op=OP.add),
                     reads=[buf(ps_x3.name)], writes=[buf(amax.name)])
                P.op("sc", lambda s: s.activation(junk[:], ps_x3[:], AF.Square,
                                                  accum_out=amax[:, 1:2]),
                     reads=[buf(ps_x3.name)],
                     writes=[buf(junk.name), buf(amax.name)])
                P.op("ve", lambda v: v.tensor_scalar(
                    amax[:, 0:1], amax[:, 0:1], 1.0 / D, None, op0=OP.mult),
                     reads=[buf(amax.name)], writes=[buf(amax.name)])
                P.op("ve", lambda v: v.tensor_tensor(
                    amax[:, 2:3], amax[:, 0:1], amax[:, 0:1], op=OP.mult),
                     reads=[buf(amax.name)], writes=[buf(amax.name)])
                P.op("ve", lambda v: v.scalar_tensor_tensor(
                    amax[:, 2:3], amax[:, 1:2], 1.0 / D, amax[:, 2:3],
                    op0=OP.mult, op1=OP.subtract),
                     reads=[buf(amax.name)], writes=[buf(amax.name)])
                P.op("sc", lambda s: s.activation(amax[:, 2:3], amax[:, 2:3],
                                                  AF.Sqrt, bias=cx.eps_ap),
                     reads=[buf(amax.name), buf(cx.cpool.name)],
                     writes=[buf(amax.name)])
                P.op("ve", lambda v: v.reciprocal(amax[:, 2:3], amax[:, 2:3]),
                     reads=[buf(amax.name)], writes=[buf(amax.name)])
                P.op("ve", lambda v: v.tensor_scalar(
                    xn4[:], ps_x3[:], amax[:, 0:1], amax[:, 2:3],
                    op0=OP.subtract, op1=OP.mult),
                     reads=[buf(ps_x3.name), buf(amax.name)], writes=[buf(xn4.name)])
                P.op("ve", lambda v: v.tensor_reduce(
                    amax[:, 0:1], xn4[:], axis=AX.X, op=OP.max,
                    apply_absolute_value=True),
                     reads=[buf(xn4.name)], writes=[buf(amax.name)])
                P.op("ve", lambda v: v.tensor_scalar(
                    amax[:, 1:2], amax[:, 0:1], 1e-5, None, op0=OP.max),
                     reads=[buf(amax.name)], writes=[buf(amax.name)])
                P.op("ve", lambda v: v.reciprocal(amax[:, 3:4], amax[:, 1:2]),
                     reads=[buf(amax.name)], writes=[buf(amax.name)])
                P.op("ve", lambda v: v.tensor_scalar(
                    amax[:, 0:1], amax[:, 3:4], 127.0, None, op0=OP.mult),
                     reads=[buf(amax.name)], writes=[buf(amax.name)])
                P.op("ve", lambda v: v.reciprocal(amax[:, 1:2], amax[:, 0:1]),
                     reads=[buf(amax.name)], writes=[buf(amax.name)])
                for half in range(2):
                    act_quant(xn4[:, half * 512:(half + 1) * 512],
                              xqa[:, half * 512:(half + 1) * 512], 512,
                              buf(xn4.name), buf(xqa.name), 0)
                for ci in range(C):
                    P.op("te", lambda te, ci=ci: te.transpose(
                        ps_tp[:, 0:128], xqa[:, ci * 128:(ci + 1) * 128],
                        cx.idn[:]),
                         reads=[buf(xqa.name), buf(cx.idn.name)],
                         writes=[buf(ps_tp.name)])
                    off = (tp * C + ci) * 128
                    P.op("ve", lambda v, off=off: v.tensor_copy(
                        xqaT_hi[:, off:off + 128], ps_tp[:, 0:128]),
                         reads=[buf(ps_tp.name)], writes=[buf(xqaT_hi.name)])
                    P.op("ve", lambda v, off=off: v.tensor_tensor(
                        xqaT_lo[:, off:off + 128], ps_tp[:, 0:128],
                        xqaT_hi[:, off:off + 128].bitcast(F32), op=OP.subtract),
                         reads=[buf(ps_tp.name), buf(xqaT_hi.name)],
                         writes=[buf(xqaT_lo.name)])

            for sl in range(FF1 // 512):
                wi = wsl_i8[sl % 2]
                wr = wsl_r[sl % 2]
                for which in range(2):
                    P.op("sy", lambda s, sl=sl, which=which, wi=wi: s.dma_start(
                        out=wi[:].rearrange("p (c n) -> p c n", c=C),
                        in_=g["wint_hbm"][:, which * FF1 + sl * 512:
                                          which * FF1 + (sl + 1) * 512]
                        .rearrange("(c p) n -> p c n", p=128)),
                         reads=[buf("hbm_wint")], writes=[buf(wi.name)], dma="dsy")
                    P.op("ve", lambda v, wi=wi, wr=wr: v.tensor_copy(wr[:], wi[:]),
                         reads=[buf(wi.name)], writes=[buf(wr.name)])
                    for tp in range(NTP):
                        ps = ps_gu[tp % 2]

                        def emit(te, tp=tp, wr=wr, ps=ps):
                            i = None
                            for ci in range(C):
                                off = (tp * C + ci) * 128
                                i = te.matmul(ps[:], xqaT_hi[:, off:off + 128],
                                              wr[:, ci * 512:(ci + 1) * 512],
                                              start=(ci == 0), stop=False)
                                i = te.matmul(ps[:], xqaT_lo[:, off:off + 128],
                                              wr[:, ci * 512:(ci + 1) * 512],
                                              start=False, stop=(ci == C - 1))
                            return i
                        P.op("te", emit,
                             reads=[buf(xqaT_hi.name), buf(xqaT_lo.name),
                                    buf(wr.name)],
                             writes=[buf(ps.name)])
                        dst = hsup[:, tp * FF1 + sl * 512: tp * FF1 + (sl + 1) * 512]
                        if which == 0:
                            P.op("sc", lambda s, ps=ps, dst=dst: s.activation(
                                dst, ps[:], AF.Silu, scale=rscl[:, 0:1]),
                                 reads=[buf(ps.name), buf(rscl.name)],
                                 writes=[buf(hsup.name)])
                        else:
                            P.op("ve", lambda v, ps=ps, dst=dst:
                                 v.scalar_tensor_tensor(
                                     dst, ps[:], rscl[:, 1:2], dst,
                                     op0=OP.mult, op1=OP.mult),
                                 reads=[buf(ps.name), buf(hsup.name),
                                        buf(rscl.name)],
                                 writes=[buf(hsup.name)])

            for tp in range(NTP):
                hrow = hsup[:, tp * FF1:(tp + 1) * FF1]
                P.op("ve", lambda v, hrow=hrow: v.tensor_reduce(
                    amax[:, 0:1], hrow, axis=AX.X, op=OP.max,
                    apply_absolute_value=True),
                     reads=[buf(hsup.name)], writes=[buf(amax.name)])
                P.op("ve", lambda v: v.tensor_scalar(
                    amax[:, 1:2], amax[:, 0:1], 1e-5, None, op0=OP.max),
                     reads=[buf(amax.name)], writes=[buf(amax.name)])
                P.op("ve", lambda v: v.reciprocal(amax[:, 3:4], amax[:, 1:2]),
                     reads=[buf(amax.name)], writes=[buf(amax.name)])
                P.op("ve", lambda v: v.tensor_scalar(
                    amax[:, 0:1], amax[:, 3:4], 127.0, None, op0=OP.mult),
                     reads=[buf(amax.name)], writes=[buf(amax.name)])
                P.op("ve", lambda v: v.reciprocal(amax[:, 1:2], amax[:, 0:1]),
                     reads=[buf(amax.name)], writes=[buf(amax.name)])
                for q8 in range(FF1 // 512):
                    act_quant(hrow[:, q8 * 512:(q8 + 1) * 512],
                              hrow[:, q8 * 512:(q8 + 1) * 512], 512,
                              buf(hsup.name), buf(hsup.name), 0)
                for k2 in range(NK2):
                    P.op("te", lambda te, k2=k2, hrow=hrow: te.transpose(
                        ps_tp[:, 0:128], hrow[:, k2 * 128:(k2 + 1) * 128],
                        cx.idn[:]),
                         reads=[buf(hsup.name), buf(cx.idn.name)],
                         writes=[buf(ps_tp.name)])
                    off = k2 * NSUP + tp * 128
                    P.op("ve", lambda v, off=off: v.tensor_copy(
                        hqT_hi[:, off:off + 128], ps_tp[:, 0:128]),
                         reads=[buf(ps_tp.name)], writes=[buf(hqT_hi.name)])
                    P.op("ve", lambda v, off=off: v.tensor_tensor(
                        hqT_lo[:, off:off + 128], ps_tp[:, 0:128],
                        hqT_hi[:, off:off + 128].bitcast(F32), op=OP.subtract),
                         reads=[buf(ps_tp.name), buf(hqT_hi.name)],
                         writes=[buf(hqT_lo.name)])
            for co in range(C):
                wdb = wdbf[co % 2]
                P.op("sy", lambda s, co=co, wdb=wdb: s.dma_start(
                    out=wdb[:].rearrange("p (k n) -> p k n", k=NK2),
                    in_=g["wdint_hbm"][:, co * 128:(co + 1) * 128]
                    .rearrange("(k p) n -> p k n", p=128)),
                     reads=[buf("hbm_wdint")], writes=[buf(wdb.name)], dma="dsy")
                P.op("ve", lambda v, wdb=wdb: v.tensor_copy(wdstage[:], wdb[:]),
                     reads=[buf(wdb.name)], writes=[buf(wdstage.name)])
                ps = ps_dn[co % 2]

                def emit(te, co=co, ps=ps):
                    i = None
                    for k2 in range(NK2):
                        st = wdstage[:, k2 * 128:(k2 + 1) * 128]
                        i = te.matmul(ps[:], st, hqT_hi[:, k2 * NSUP:(k2 + 1) * NSUP],
                                      start=(k2 == 0), stop=False)
                        i = te.matmul(ps[:], st, hqT_lo[:, k2 * NSUP:(k2 + 1) * NSUP],
                                      start=False, stop=(k2 == NK2 - 1))
                    return i
                P.op("te", emit,
                     reads=[buf(wdstage.name), buf(hqT_hi.name),
                            buf(hqT_lo.name)],
                     writes=[buf(ps.name)])
                P.op("ve", lambda v, co=co, ps=ps: v.scalar_tensor_tensor(
                    xn4[:, 0:NSUP], ps[:], rscl[:, 2:3],
                    x3sup[:, co * NSUP:(co + 1) * NSUP],
                    op0=OP.mult, op1=OP.add),
                     reads=[buf(ps.name), buf(x3sup.name), buf(rscl.name)],
                     writes=[buf(xn4.name)])
                for tp in range(NTP):
                    P.op("te", lambda te, tp=tp: te.transpose(
                        ps_tp[:, 128:256], xn4[:, tp * 128:(tp + 1) * 128],
                        cx.idn[:]),
                         reads=[buf(xn4.name), buf(cx.idn.name)],
                         writes=[buf(ps_tp.name)])
                    P.op("ve", lambda v, tp=tp, co=co: v.tensor_copy(
                        ytile[:, tp * D + co * 128:tp * D + (co + 1) * 128],
                        ps_tp[:, 128:256]),
                         reads=[buf(ps_tp.name)], writes=[buf(ytile.name)])
            P.op("gp", lambda gp, sup=sup: gp.dma_start(
                out=g["y_tm"][sup * NSUP:(sup + 1) * NSUP, :]
                .rearrange("(tp p) n -> p tp n", p=128),
                in_=ytile[:].rearrange("p (tp n) -> p tp n", tp=NTP)),
                 reads=[buf(ytile.name)], writes=[buf("hbm_y")], dma="dgp")



def kernel(**inputs):
    in_maps = prepare_inputs(inputs)
    stage = _DEBUG.get("stage")
    nc = build_program(stage)
    res = run_bass_kernel_spmd(nc, in_maps, core_ids=list(range(NCORES)))
    if stage is not None:
        return res
    out = np.zeros((4, T4, D), np.float32)
    for c in range(NCORES):
        b, h = c // 2, c % 2
        out[b, h * T2:(h + 1) * T2, :] = res.results[c]["y_tm"]
    return out
